# version 02
# baseline (speedup 1.0000x reference)
import contextlib
import numpy as np
import ml_dtypes
import concourse.bass as bass
import concourse.mybir as mybir
from concourse.bass import IndirectOffsetOnAxis
from concourse.bass_utils import run_bass_kernel_spmd

F32 = mybir.dt.float32
BF16 = mybir.dt.bfloat16
FP16 = mybir.dt.float16
I32 = mybir.dt.int32
U32 = mybir.dt.uint32
AF = mybir.ActivationFunctionType
ALU = mybir.AluOpType
AXX = mybir.AxisListType.X

NCORE = 8
D = 2048
NB = 8
TPC = NB * 128
NH = 16
HD = 128
S = 4096
SBLK = 32
NE = 32
EPC = 4
CAP = 2048
NEG = -1e30
RC = 1536.0
EPS_LN = 1e-5
EPS_RMS = 1e-6
ROWS = 1040
GT = NCORE * ROWS
JUNK_XR = 1024.0
UPTO = "all"

M_IDX = {"q": 0, "k": 1, "v": 2, "o": 3}


def blocks_of(core):
    o = core % 4
    return sorted(list(range(4 * o, 4 * o + 4)) + list(range(28 - 4 * o, 32 - 4 * o)))


BLK_OWNER = [0] * SBLK
BLK_IDX = [0] * SBLK
for _o in range(4):
    for _i, _b in enumerate(blocks_of(_o)):
        BLK_OWNER[_b] = _o
        BLK_IDX[_b] = _i
COLOF = [BLK_OWNER[g] * 1024 + BLK_IDX[g] * 128 for g in range(SBLK)]

ENGS = ("te", "ve", "sc", "gp", "sy")


class Op:
    __slots__ = ("eng", "fn", "deps", "marked", "chainval", "kind", "tag", "idx",
                 "_acc_key")

    def __init__(self, eng, fn, kind, tag):
        self.eng = eng
        self.fn = fn
        self.deps = set()
        self.marked = False
        self.chainval = None
        self.kind = kind
        self.tag = tag
        self.idx = None
        self._acc_key = None


class Prog:

    def __init__(self):
        self.ops = []
        self.lw = {}
        self.readers = {}
        self.spaces = {}

    def add(self, eng, fn, reads=(), writes=(), kind="c", tag=""):
        op = Op(eng, fn, kind, tag)
        op.idx = len(self.ops)
        for k in list(reads) + list(writes):
            self.spaces.setdefault(k[0], set()).add(k)
        for k in reads:
            w = self.lw.get(k)
            if w is not None:
                op.deps.add(w)
            self.readers.setdefault(k, []).append(op)
        for k in writes:
            w = self.lw.get(k)
            if w is not None:
                op.deps.add(w)
            for r in self.readers.get(k, ()):
                op.deps.add(r)
            self.lw[k] = op
            self.readers[k] = []
        op.deps.discard(op)
        self.ops.append(op)
        return op

    def add_acc_first(self, eng, fn, reads=(), writes=(), tag=""):
        op = self.add(eng, fn, reads=reads, writes=writes, kind="c", tag=tag)
        for k in writes:
            op._acc_key = k
        return op

    def add_acc_next(self, eng, fn, reads=(), keys=(), tag=""):
        op = Op(eng, fn, "c", tag)
        op.idx = len(self.ops)
        for k in reads:
            w = self.lw.get(k)
            if w is not None:
                op.deps.add(w)
            self.readers.setdefault(k, []).append(op)
        for k in keys:
            self.lw[k] = op
            self.readers[k] = []
            op._acc_key = k
        op.deps.discard(op)
        self.ops.append(op)
        return op


class MK:
    def __init__(self, upto="all", taps=()):
        self.nc = bass.Bass("TRN2", target_bir_lowering=False)
        self.stack = contextlib.ExitStack()
        self.upto = upto
        self.taps = set(taps)
        self.dr = {}
        self.t = {}
        self.prog = Prog()

    def sbuf(self, name, shape, dtype):
        t = self.stack.enter_context(self.nc.sbuf_tensor(name, shape, dtype))
        self.t[name] = t
        return t

    def psumt(self, name, shape, dtype):
        t = self.stack.enter_context(self.nc.psum_tensor(name, shape, dtype))
        self.t[name] = t
        return t

    def dram(self, name, shape, dtype, kind="Internal"):
        ap = self.nc.dram_tensor(name, shape, dtype, kind=kind).ap()
        self.dr[name] = ap
        return ap

    def claim(self, *arenas, exclude=()):
        keys = []
        for a in arenas:
            sp = self.prog.spaces.get(a, set())
            take = {k for k in sp if not (len(k) > 1 and k[1] in exclude)}
            keys.extend(sorted(take, key=repr))
            self.prog.spaces[a] = sp - take
        return keys

    def c(self, eng, fn, reads=(), writes=(), tag=""):
        return self.prog.add(eng, fn, reads=reads, writes=writes, kind="c", tag=tag)

    def mm0(self, fn, reads=(), writes=(), tag=""):
        return self.prog.add_acc_first("te", fn, reads=reads, writes=writes, tag=tag)

    def mmn(self, fn, reads=(), keys=(), tag=""):
        return self.prog.add_acc_next("te", fn, reads=reads, keys=keys, tag=tag)

    def dma(self, eng, fn, reads=(), writes=(), tag=""):
        return self.prog.add(eng, fn, reads=reads, writes=writes, kind="d", tag=tag)

    def coll(self, fn, reads=(), writes=(), tag=""):
        return self.prog.add("gp", fn, reads=reads, writes=writes, kind="cc", tag=tag)

    NDMASEM = 8
    NCCSEM = 4

    def finalize(self):
        for op in self.prog.ops:
            for d in op.deps:
                d.marked = True
        cc_counts = {}
        dma_counts = {}
        for op in self.prog.ops:
            if op.kind == "c":
                if op.marked:
                    cc_counts[op.eng] = cc_counts.get(op.eng, 0) + 1
                    op.chainval = ("c", op.eng, cc_counts[op.eng])
            elif op.kind == "d":
                i = dma_counts.get(op.eng, 0)
                dma_counts[op.eng] = i + 1
                slot = i % self.NDMASEM
                rnd = i // self.NDMASEM
                op.chainval = ("d", op.eng, slot, 16 * (rnd + 1))
            elif op.kind == "cc":
                i = dma_counts.get("cc", 0)
                dma_counts["cc"] = i + 1
                slot = i % self.NCCSEM
                rnd = i // self.NCCSEM
                op.chainval = ("cc", "gp", slot, rnd + 1)
        sems = {}
        for e, n in cc_counts.items():
            if n:
                sems[("c", e)] = self.stack.enter_context(self.nc.semaphore(f"s_c_{e}"))
        for q, n in dma_counts.items():
            if q == "cc":
                for s in range(min(n, self.NCCSEM)):
                    sems[("cc", "gp", s)] = self.stack.enter_context(
                        self.nc.semaphore(f"s_cc_{s}"))
            else:
                for s in range(min(n, self.NDMASEM)):
                    sems[("d", q, s)] = self.stack.enter_context(
                        self.nc.semaphore(f"s_d_{q}_{s}"))
        self.chain_sems = sems
        finals = []
        for op in self.prog.ops:
            if op.kind in ("d", "cc") and op.chainval is not None:
                k, e, slot, val = op.chainval
                finals.append(((k, e, slot), val))
        fin = {}
        for key, val in finals:
            fin[key] = max(fin.get(key, 0), val)
        for e, n in cc_counts.items():
            fin[("c", e)] = n
        self.final_chain = fin

        nc = self.nc
        mkself = self

        def wait_key(op):
            cv = op.chainval
            if cv is None:
                return None
            if cv[0] == "c":
                return (("c", cv[1]), cv[2])
            return ((cv[0], cv[1], cv[2]), cv[3])

        with nc.Block() as block:
            for eng_name in ENGS:
                ops = [o for o in self.prog.ops if o.eng == eng_name]
                if not ops and eng_name != "sy":
                    continue

                def section(engobj, eng_name=eng_name, ops=ops):
                    watermark = {}
                    for op in ops:
                        if op.kind in ("d", "cc"):
                            cv = op.chainval
                            skey = (cv[0], cv[1], cv[2])
                            prev = cv[3] - (16 if cv[0] == "d" else 1)
                            if prev > 0 and watermark.get(skey, 0) < prev:
                                engobj.wait_ge(mkself.chain_sems[skey], prev)
                                watermark[skey] = prev
                        for dop in sorted(op.deps, key=lambda o: o.idx):
                            wk = wait_key(dop)
                            if wk is None:
                                continue
                            skey, val = wk
                            if watermark.get(skey, 0) >= val:
                                continue
                            engobj.wait_ge(mkself.chain_sems[skey], val)
                            watermark[skey] = val
                        instr = op.fn(engobj)
                        wk = wait_key(op)
                        if wk is not None:
                            assert instr is not None, f"marked op {op.tag} returned None"
                            instr.then_inc(mkself.chain_sems[wk[0]],
                                           16 if wk[0][0] == "d" else 1)
                    if eng_name == "sy":
                        for skey, v in mkself.final_chain.items():
                            if v and watermark.get(skey, 0) < v:
                                engobj.wait_ge(mkself.chain_sems[skey], v)

                getattr(block, {"te": "tensor", "ve": "vector", "sc": "scalar",
                                "gp": "gpsimd", "sy": "sync"}[eng_name])(section)
        return nc

    def emit(self):
        self.declare_io()
        self.alloc_tiles()
        self.ph_consts()
        self.ph_wquant()
        self.ph_lnquant()
        if self.upto not in ("m1", "m2"):
            self.ph_proj_kv()
        if self.upto in ("m4", "m5", "all"):
            self.ph_attn()
            self.ph_wo()
        if self.upto in ("m5", "all"):
            self.ph_ffn_pre()
            self.ph_routing()
        if self.upto == "all":
            self.ph_shared_gu()
            self.ph_experts()
            self.ph_return()
        else:
            self.ph_out_partial()
        with self.stack:
            return self.finalize()

    def declare_io(self):
        nc = self.nc
        d = self.dram
        self.x_in = nc.dram_tensor("x_in", [TPC, D], F32, kind="ExternalInput").ap()
        self.wcol = {m: nc.dram_tensor(f"w_{m}", [D, 256], F32, kind="ExternalInput").ap()
                     for m in "qkvo"}
        self.gate_in = nc.dram_tensor("gate_w", [NE, D], F32, kind="ExternalInput").ap()
        self.normw_in = nc.dram_tensor("norm_w", [1, D], F32, kind="ExternalInput").ap()
        self.w13_in = nc.dram_tensor("w13", [EPC, D, D], F32, kind="ExternalInput").ap()
        self.w2_in = nc.dram_tensor("w2", [EPC, 1024, D], F32, kind="ExternalInput").ap()
        self.shg_in = nc.dram_tensor("sh_g", [D, 1024], F32, kind="ExternalInput").ap()
        self.shu_in = nc.dram_tensor("sh_u", [D, 1024], F32, kind="ExternalInput").ap()
        self.shd_in = nc.dram_tensor("sh_d", [1024, D], F32, kind="ExternalInput").ap()
        self.rowbase_in = nc.dram_tensor("rowbase", [128, 1], F32, kind="ExternalInput").ap()
        self.attidx_in = nc.dram_tensor("attidx", [128, 288], I32, kind="ExternalInput").ap()
        self.avidx_in = nc.dram_tensor("avidx", [128, 32], I32, kind="ExternalInput").ap()
        self.exidx_in = nc.dram_tensor("exidx", [128, 64], I32, kind="ExternalInput").ap()
        self.hsel_in = nc.dram_tensor("hsel", [16, 4], F32, kind="ExternalInput").ap()
        self.esel_in = nc.dram_tensor("esel", [128, 64], F32, kind="ExternalInput").ap()
        self.out = nc.dram_tensor("out", [TPC, D], F32, kind="ExternalOutput").ap()
        d("ar_in", [1, 4], F32)
        d("ar_out", [1, 4], F32)
        for m in "qkvo":
            d(f"wt_{m}_in", [256, D], BF16)
            d(f"wt_{m}", [D, D], BF16)
        d("kT_in", [2 * NH * 128, 1024], BF16)
        d("kT_tab", [4 * 2 * NH * 128, 1024], BF16)
        d("qT_in", [2 * NH * 128, 1024], BF16)
        d("qT_tab", [4 * 2 * NH * 128, 1024], BF16)
        d("v_in", [2 * NH * 1024, 128], BF16)
        d("v_tab", [4 * 2 * NH * 1024, 128], BF16)
        d("nrm_in", [1, 32], F32)
        d("nrm_tab", [4, 32], F32)
        d("av_in", [S, 1024], BF16)
        d("av_tab", [4 * S, 1024], BF16)
        d("xr_in", [ROWS, 2 * D], BF16)
        d("xrtab", [GT, 2 * D], BF16)
        d("rt_in", [ROWS, 8], F32)
        d("rttab", [GT, 8], F32)
        d("glw_in", [NE * CAP + 128, 2], F32)
        d("glw", [NE * CAP + 128, 2], F32)
        d("rs_in", [GT, D], F32)
        d("rs_out", [ROWS, D], F32)
        d("xspill", [TPC, D], F32)
        d("oaspill", [TPC, D], F32)
        d("xrT_sp", [128, 2 * NH * 1024], BF16)
        d("act_sp", [TPC, 1024], F32)
        d("shared_sp", [TPC, D], F32)
        self.dbgt = {}
        for name, shape in [
            ("dbg_xqT", [16 * 128, TPC]), ("dbg_sxq", [128, NB]),
            ("dbg_x1", [TPC, D]), ("dbg_xr", [TPC, D]),
            ("dbg_rt", [TPC, 8]), ("dbg_av", [TPC, D]),
        ]:
            if name in self.taps:
                self.dbgt[name] = nc.dram_tensor(name, shape, F32, kind="ExternalOutput").ap()

    def alloc_tiles(self):
        s = self.sbuf
        s("ones128", [128, 128], BF16)
        s("onesf", [128, 128], F32)
        s("ltri", [128, 128], BF16)
        s("ident", [128, 128], BF16)
        s("identf", [128, 128], F32)
        s("zerob", [128, 128], BF16)
        s("epsln", [128, 1], F32)
        s("epsrms", [128, 1], F32)
        s("arA", [128, 24576], F32)
        s("arB", [128, 16384], F32)
        s("scr0", [128, D], F32)
        s("scr1", [128, D], F32)
        s("scrh", [128, D], FP16)
        s("scrb", [128, D], BF16)
        s("mvt", [128, 16, 512], BF16)
        s("qst", [128, 1024], BF16)
        s("ti32", [128, 256], I32)
        s("stats", [128, 32], F32)
        s("sxq", [128, NB], F32)
        s("dqrow", [128, NB, 4], F32)
        s("thr", [128, 4], F32)
        s("swg", [128, 4], F32)
        s("nrmloc", [128, 64], F32)
        s("mh", [128, 4], F32)
        s("gwt", [128, 16], F32)
        s("wloc", [128, 8], F32)
        s("gidx", [128, 8], I32)
        s("attix", [128, 288], I32)
        s("avix", [128, 32], I32)
        s("exix", [128, 64], I32)
        s("rowb", [128, 1], F32)
        s("hselsb", [16, 4], F32)
        s("eselsb", [128, 64], F32)
        self.psumt("ps_a", [128, 512], F32)
        self.psumt("ps_b", [128, 512], F32)
        self.psumt("ps_c", [128, 512], F32)
        self.psumt("ps_d", [128, 512], F32)
        self.psumt("ps_t0", [128, 128], BF16)
        self.psumt("ps_t1", [128, 128], BF16)
        self.psumt("ps_f0", [128, 128], F32)
        self.psumt("ps_r", [128, 32], F32)

    def abf(self, name):
        return self.t[name].ap().bitcast(BF16)

    def af32(self, name):
        return self.t[name].ap()

    def vw_mvt32(self):
        return self.t["mvt"].ap().rearrange("p a b -> p (a b)").bitcast(F32)

    def vw_glwv(self):
        return self.af32("arA")[:, 20992:21504]

    def vw_rtloc(self):
        return self.t["nrmloc"].ap()[:, 56:64]

    def vw_idsu(self):
        return self.t["nrmloc"].ap().bitcast(U32)[:, 44:52]

    def vw_maskt(self):
        return self.t["mvt"].ap().rearrange("p a b -> p (a b)")[:, 2048:4096]

    def vw_wslab(self, mi):
        return self.af32("arA")[:, 0:16384].rearrange("p (m j c) -> p m j c", m=4, j=16)[:, mi]

    def vw_pq32(self):
        return self.af32("arA")[:, 0:16384].rearrange("p (b d) -> p b d", b=NB)

    def vw_xres(self):
        return self.af32("arB").rearrange("p (b d) -> p b d", b=NB)

    def vw_wtern(self):
        return self.abf("arB")[:, 0:4096].rearrange("p (j c) -> p j c", j=16)

    def vw_wtsl(self):
        return self.abf("arB")[:, 4096:8192].rearrange("p (e o) -> p e o", e=2)

    def vw_xqT(self):
        return self.abf("arA")[:, 32768:49152].rearrange("p (j t) -> p j t", j=16)

    def vw_kTb(self):
        return self.t["scr1"].ap().bitcast(BF16)[:, 0:2048]

    def vw_pair(self, slot, which):
        base = slot * 24576 + {"q": 0, "k": 8192, "v": 16384}[which]
        return self.abf("arA")[:, base:base + 8192].rearrange("p (hl t) -> p hl t", hl=2)

    def vw_avloc(self):
        return self.abf("arA")[:, 0:32768].rearrange("p (b r hl c) -> p b r hl c",
                                                     b=NB, r=4, hl=2)

    def vw_oqT(self):
        return self.abf("arA")[:, 0:16384].rearrange("p (j t) -> p j t", j=16)

    def vw_xrT(self):
        return self.abf("arA")[:, 0:32768].rearrange("p (hl j t) -> p hl j t", hl=2, j=16)

    def vw_xrTB(self):
        return self.abf("arB")[:, 0:32768].rearrange("p (hl j t) -> p hl j t", hl=2, j=16)

    def ph_consts(self):
        t = self.t
        self.c("ve", lambda v: v.memset(t["ones128"][:], 1.0), writes=[("ones128",)])
        self.c("ve", lambda v: v.memset(t["onesf"][:], 1.0), writes=[("onesf",)])
        self.c("ve", lambda v: v.memset(t["zerob"][:], 0.0), writes=[("zerob",)])
        self.c("ve", lambda v: v.memset(t["epsln"][:], EPS_LN), writes=[("epsln",)])
        self.c("ve", lambda v: v.memset(t["epsrms"][:], EPS_RMS), writes=[("epsrms",)])
        self.c("gp", lambda g: g.affine_select(t["ident"][:], t["ones128"][:],
                                               pattern=[[1, 128]], compare_op=ALU.is_equal,
                                               fill=0.0, base=0, channel_multiplier=-1),
               reads=[("ones128",)], writes=[("ident",)])
        self.c("gp", lambda g: g.affine_select(t["ltri"][:], t["ones128"][:],
                                               pattern=[[1, 128]], compare_op=ALU.is_gt,
                                               fill=0.0, base=0, channel_multiplier=-1),
               reads=[("ones128",)], writes=[("ltri",)])
        self.c("ve", lambda v: v.tensor_copy(out=t["identf"][:], in_=t["ident"][:]),
               reads=[("ident",)], writes=[("identf",)])
        self.dma("sy", lambda sy: sy.dma_start(out=t["attix"][:], in_=self.attidx_in[:]),
                 writes=[("attix",)])
        self.dma("sy", lambda sy: sy.dma_start(out=t["avix"][:], in_=self.avidx_in[:]),
                 writes=[("avix",)])
        self.dma("sy", lambda sy: sy.dma_start(out=t["exix"][:], in_=self.exidx_in[:]),
                 writes=[("exix",)])
        self.dma("sy", lambda sy: sy.dma_start(out=t["rowb"][:], in_=self.rowbase_in[:]),
                 writes=[("rowb",)])
        self.dma("sy", lambda sy: sy.dma_start(out=t["hselsb"][:], in_=self.hsel_in[:]),
                 writes=[("hselsb",)])
        self.dma("sy", lambda sy: sy.dma_start(out=t["eselsb"][:], in_=self.esel_in[:]),
                 writes=[("eselsb",)])

    def ph_wquant(self):
        t = self.t
        dr = self.dr
        for m in "qkvo":
            mi = M_IDX[m]
            self.dma("sy", (lambda sy, mi=mi, m=m: sy.dma_start(
                out=self.vw_wslab(mi),
                in_=self.wcol[m].rearrange("(a p) b -> p a b", p=128))),
                writes=[("arA", "wsl", mi)], tag=f"ld_w{m}")
        for mi in range(4):
            for h in range(2):
                self.c("sc", (lambda sc, mi=mi, h=h: sc.activation(
                    t["scr1"][:, 0:D],
                    self.vw_wslab(mi).rearrange("p a b -> p (a b)")[:, D * h:D * (h + 1)],
                    AF.Abs, accum_out=t["stats"][:, 2 * mi + h:2 * mi + h + 1])),
                    reads=[("arA", "wsl", mi)], writes=[("scr1",), ("stats", "abs", mi, h)],
                    tag=f"abs{mi}{h}")
        self.c("ve", lambda v: v.tensor_tensor(
            t["stats"][:, 8:12], t["stats"][:, 0:8:2], t["stats"][:, 1:8:2], ALU.add),
            reads=[("stats", "abs", mi_, h_) for mi_ in range(4) for h_ in range(2)],
            writes=[("stats", "abss")], tag="abssum")
        self.mm0(lambda te: te.matmul(self.t["ps_r"][:, 0:4], t["onesf"][:],
                                      t["stats"][:, 8:12], start=True, stop=True),
                 reads=[("onesf",), ("stats", "abss")], writes=[("ps_r",)], tag="abs_mm")
        self.c("ve", lambda v: v.tensor_copy(out=t["stats"][:, 12:16], in_=self.t["ps_r"][:, 0:4]),
               reads=[("ps_r",)], writes=[("stats", "abst")])
        self.dma("gp", lambda g: g.dma_start(out=dr["ar_in"][:], in_=t["stats"][0:1, 12:16]),
                 reads=[("stats", "abst")], writes=[("dram:ar_in",)])
        self.coll(lambda g: g.collective_compute(
            "AllReduce", ALU.add, replica_groups=[list(range(NCORE))],
            ins=[dr["ar_in"][:]], outs=[dr["ar_out"][:]]),
            reads=[("dram:ar_in",)], writes=[("dram:ar_out",)], tag="ar_abs")
        self.dma("gp", lambda g: g.dma_start(out=t["stats"][0:1, 16:20], in_=dr["ar_out"][:]),
                 reads=[("dram:ar_out",)], writes=[("stats", "arout")])
        self.mm0(lambda te: te.matmul(self.t["ps_r"][:, 4:8], t["onesf"][0:1, :],
                                      t["stats"][0:1, 16:20], start=True, stop=True),
                 reads=[("onesf",), ("stats", "arout")], writes=[("ps_r", "bc")], tag="bc_mm")
        self.c("ve", lambda v: v.tensor_scalar(t["swg"][:], self.t["ps_r"][:, 4:8],
                                               1.0 / (D * D), 1e-5, ALU.mult, ALU.max),
               reads=[("ps_r", "bc")], writes=[("swg",)])
        self.c("ve", lambda v: v.tensor_scalar(t["thr"][:], t["swg"][:], 0.5, None, ALU.mult),
               reads=[("swg",)], writes=[("thr",)])
        for m in "kvqo":
            mi = M_IDX[m]
            wsl = self.vw_wslab(mi).rearrange("p a b -> p (a b)")
            wtf = self.vw_wtern().rearrange("p j c -> p (j c)")
            for h in range(2):
                self.c("ve", (lambda v, wsl=wsl, h=h, mi=mi: v.tensor_scalar(
                    t["scr0"][:, 0:D], wsl[:, D * h:D * (h + 1)],
                    t["thr"][:, mi:mi + 1], None, ALU.is_ge)),
                    reads=[("arA", "wsl", mi), ("thr",)], writes=[("scr0",)])
                self.c("ve", (lambda v, wsl=wsl, h=h, mi=mi: v.tensor_scalar(
                    t["scr1"][:, 0:D], wsl[:, D * h:D * (h + 1)], -1.0,
                    t["thr"][:, mi:mi + 1], ALU.mult, ALU.is_ge)),
                    reads=[("arA", "wsl", mi), ("thr",)], writes=[("scr1",)])
                self.c("ve", (lambda v, wtf=wtf, h=h: v.tensor_tensor(
                    wtf[:, D * h:D * (h + 1)], t["scr0"][:, 0:D], t["scr1"][:, 0:D],
                    ALU.subtract)),
                    reads=[("scr0",), ("scr1",)], writes=[("arB", "wtern", h)])
            for j in range(16):
                for eb in range(2):
                    ps = "ps_t0" if ((j * 2 + eb) % 2 == 0) else "ps_t1"
                    self.mm0((lambda te, j=j, eb=eb, ps=ps: te.transpose(
                        self.t[ps][:], self.vw_wtern()[:, j, 128 * eb:128 * (eb + 1)],
                        t["ident"][:])),
                        reads=[("arB", "wtern", eb), ("ident",)], writes=[(ps,)], tag="wtT")
                    self.c("ve", (lambda v, j=j, eb=eb, ps=ps: v.tensor_copy(
                        out=self.vw_wtsl()[:, eb, 128 * j:128 * (j + 1)], in_=self.t[ps][:])),
                        reads=[(ps,)], writes=[("arB", "wtsl", eb, j)])
            self.dma("gp", (lambda g, m=m: g.dma_start(
                out=dr[f"wt_{m}_in"].rearrange("(eb p) o -> p eb o", p=128),
                in_=self.vw_wtsl())),
                reads=[("arB", "wtsl", eb_, j_) for eb_ in range(2) for j_ in range(16)],
                writes=[(f"dram:wt_{m}_in",)])
            self.coll((lambda g, m=m: g.collective_compute(
                "AllGather", ALU.bypass, replica_groups=[list(range(NCORE))],
                ins=[dr[f"wt_{m}_in"][:]], outs=[dr[f"wt_{m}"][:]])),
                reads=[(f"dram:wt_{m}_in",)], writes=[(f"dram:wt_{m}",)], tag=f"ag_w{m}")

    def ph_lnquant(self):
        t = self.t
        xr = self.vw_xres()
        for b in range(NB):
            extra = self.claim("arB") if b == 0 else []
            self.dma("sy", (lambda sy, b=b, xr=xr: sy.dma_start(
                out=xr[:, b, :], in_=self.x_in[128 * b:128 * (b + 1), :])),
                writes=[("arB", "x", b)] + extra, tag=f"ld_x{b}")
        for b in range(NB):
            self.c("sc", (lambda sc, b=b, xr=xr: sc.activation(
                t["scr1"][:, 0:D], xr[:, b, :], AF.Identity,
                accum_out=t["stats"][:, b:b + 1])),
                reads=[("arB", "x", b)], writes=[("scr1",), ("stats", "m", b)], tag=f"mean{b}")
        for b in range(NB):
            self.c("ve", (lambda v, b=b: v.tensor_scalar(
                t["stats"][:, b:b + 1], t["stats"][:, b:b + 1], 1.0 / D, None, ALU.mult)),
                reads=[("stats", "m", b)], writes=[("stats", "m", b)])
            self.c("ve", (lambda v, b=b, xr=xr: v.tensor_scalar(
                t["scr0"][:, :], xr[:, b, :], t["stats"][:, b:b + 1], None, ALU.subtract)),
                reads=[("arB", "x", b), ("stats", "m", b)], writes=[("scr0",)], tag=f"d{b}")
            self.c("sc", (lambda sc, b=b: sc.activation(
                t["scr1"][:, 0:D], t["scr0"][:, :], AF.Square,
                accum_out=t["stats"][:, 8 + b:9 + b])),
                reads=[("scr0",)], writes=[("scr1",), ("stats", "ss", b)], tag=f"ssq{b}")
            self.c("sc", (lambda sc, b=b: sc.activation(
                t["stats"][:, 16 + b:17 + b], t["stats"][:, 8 + b:9 + b], AF.Sqrt,
                bias=t["epsln"][:], scale=1.0 / D)),
                reads=[("stats", "ss", b), ("epsln",)], writes=[("stats", "std", b)])
            self.c("ve", (lambda v, b=b: v.reciprocal(
                t["stats"][:, 24 + b:25 + b], t["stats"][:, 16 + b:17 + b])),
                reads=[("stats", "std", b)], writes=[("stats", "rstd", b)])
            self.c("ve", (lambda v, b=b: v.reduce_max(
                t["sxq"][:, b:b + 1], t["scr0"][:, :], axis=AXX, apply_absolute_value=True)),
                reads=[("scr0",)], writes=[("sxq", b)])
            self.c("ve", (lambda v, b=b: v.tensor_tensor(
                t["sxq"][:, b:b + 1], t["sxq"][:, b:b + 1], t["stats"][:, 24 + b:25 + b], ALU.mult)),
                reads=[("sxq", b), ("stats", "rstd", b)], writes=[("sxq", b)])
            self.c("ve", (lambda v, b=b: v.tensor_scalar(
                t["sxq"][:, b:b + 1], t["sxq"][:, b:b + 1], 1e-5, None, ALU.max)),
                reads=[("sxq", b)], writes=[("sxq", b)])
            self.c("ve", (lambda v, b=b: v.tensor_scalar(
                t["stats"][:, 16 + b:17 + b], t["sxq"][:, b:b + 1], 1.0 / 127.0, None, ALU.mult)),
                reads=[("sxq", b)], writes=[("stats", "isx", b)])
            self.c("ve", (lambda v, b=b: v.reciprocal(
                t["sxq"][:, b:b + 1], t["sxq"][:, b:b + 1])),
                reads=[("sxq", b)], writes=[("sxq", b)])
            self.c("ve", (lambda v, b=b: v.tensor_scalar(
                t["sxq"][:, b:b + 1], t["sxq"][:, b:b + 1], 127.0, None, ALU.mult)),
                reads=[("sxq", b)], writes=[("sxq", b)], tag=f"sx{b}")
            self.c("ve", (lambda v, b=b: v.tensor_tensor(
                t["stats"][:, 24 + b:25 + b], t["stats"][:, 24 + b:25 + b],
                t["sxq"][:, b:b + 1], ALU.mult)),
                reads=[("stats", "rstd", b), ("sxq", b)], writes=[("stats", "qsc", b)])
            self.c("ve", (lambda v, b=b: v.tensor_scalar(
                t["scrh"][:, :], t["scr0"][:, :], t["stats"][:, 24 + b:25 + b], RC,
                ALU.mult, ALU.add)),
                reads=[("scr0",), ("stats", "qsc", b)], writes=[("scrh",)])
            self.c("ve", (lambda v, b=b: v.tensor_scalar(
                t["scrb"][:, :], t["scrh"][:, :], RC, None, ALU.subtract)),
                reads=[("scrh",)], writes=[("scrb",)], tag=f"xq{b}")
            for j in range(16):
                ps = "ps_t0" if j % 2 == 0 else "ps_t1"
                self.mm0((lambda te, j=j, ps=ps: te.transpose(
                    self.t[ps][:], t["scrb"][:, 128 * j:128 * (j + 1)], t["ident"][:])),
                    reads=[("scrb",), ("ident",)], writes=[(ps,)], tag="xqT_t")
                self.c("ve", (lambda v, j=j, b=b, ps=ps: v.tensor_copy(
                    out=self.vw_xqT()[:, j, 128 * b:128 * (b + 1)], in_=self.t[ps][:])),
                    reads=[(ps,)], writes=[("arA", "xqT", j, b)])
            for mi in range(4):
                self.c("ve", (lambda v, b=b, mi=mi: v.tensor_tensor(
                    t["dqrow"][:, b, mi:mi + 1], t["stats"][:, 16 + b:17 + b],
                    t["swg"][:, mi:mi + 1], ALU.mult)),
                    reads=[("stats", "isx", b), ("swg",)], writes=[("dqrow", b, mi)],
                    tag=f"dq{b}{mi}")
            self.dma("gp", (lambda g, b=b, xr=xr: g.dma_start(
                out=self.dr["xspill"][128 * b:128 * (b + 1), :], in_=xr[:, b, :])),
                reads=[("arB", "x", b)], writes=[("dram:xspill", b)], tag=f"spx{b}")
        if "dbg_xqT" in self.taps:
            for j in range(16):
                self.dma("sy", (lambda sy, j=j: sy.dma_start(
                    out=self.dbgt["dbg_xqT"][128 * j:128 * (j + 1), :], in_=self.vw_xqT()[:, j, :])),
                    reads=[("arA", "xqT", j, b_) for b_ in range(NB)], writes=[("dram:dbgxqT", j)])
        if "dbg_sxq" in self.taps:
            self.dma("sy", lambda sy: sy.dma_start(out=self.dbgt["dbg_sxq"][:], in_=t["sxq"][:]),
                     reads=[("sxq", b_) for b_ in range(NB)], writes=[("dram:dbgsxq",)])

    def proj_matmuls(self, m, first_claim=False):
        t = self.t
        dr = self.dr
        mi = M_IDX[m]
        pq = self.vw_pq32()
        for dt_ in range(4):
            extra = (self.claim("arA", exclude=("xqT",)) if (first_claim and dt_ == 0) else [])
            self.dma("sy", (lambda sy, m=m, dt_=dt_: sy.dma_start(
                out=t["mvt"][:, :, :],
                in_=dr[f"wt_{m}"].rearrange("(a p) o -> p a o", p=128)
                    [:, :, 512 * dt_:512 * (dt_ + 1)])),
                reads=[(f"dram:wt_{m}",)], writes=[("mvt",)] + extra, tag=f"mv_{m}{dt_}")
            for b in range(NB):
                ps = "ps_a" if b % 2 == 0 else "ps_b"
                for j in range(16):
                    mmf = (lambda te, j=j, b=b, ps=ps: te.matmul(
                        self.t[ps][:], self.vw_xqT()[:, j, 128 * b:128 * (b + 1)],
                        t["mvt"][:, j, :], start=(j == 0), stop=(j == 15)))
                    if j == 0:
                        self.mm0(mmf, reads=[("arA", "xqT", j, b), ("mvt",)],
                                 writes=[(ps,)], tag=f"pj_{m}{dt_}{b}")
                    else:
                        self.mmn(mmf, reads=[("arA", "xqT", j, b), ("mvt",)], keys=[(ps,)])
                self.c("ve", (lambda v, b=b, dt_=dt_, ps=ps, pq=pq, mi=mi: v.tensor_scalar(
                    pq[:, b, 512 * dt_:512 * (dt_ + 1)], self.t[ps][:],
                    t["dqrow"][:, b, mi:mi + 1], None, ALU.mult)),
                    reads=[(ps,), ("dqrow", b, mi)], writes=[("arA", "pq", b, dt_)],
                    tag=f"{m}ev{b}{dt_}")
        return pq

    def split_hilo(self, src_ap, rd, tag):
        t = self.t
        kvh = t["scrb"].ap()
        kvl = t["scrh"].ap().bitcast(BF16)
        self.c("sc", (lambda sc, src_ap=src_ap, kvh=kvh: sc.activation(
            kvh[:, :], src_ap, AF.Identity)),
            reads=rd, writes=[("scrb",)], tag=f"{tag}hi")
        self.c("ve", (lambda v, src_ap=src_ap, kvh=kvh, kvl=kvl: v.tensor_tensor(
            kvl[:, :], src_ap, kvh[:, :], ALU.subtract)),
            reads=rd + [("scrb",)], writes=[("scrh",)], tag=f"{tag}lo")
        return kvh, kvl

    def head_norms(self, st, b, dstoff, tag):
        t = self.t
        rd = [("arA", "pq", b, d_) for d_ in range(4)]
        for h in range(NH):
            self.c("sc", (lambda sc, b=b, h=h, st=st: sc.activation(
                t["scr0"][:, 0:HD], st[:, b, HD * h:HD * (h + 1)], AF.Square,
                accum_out=t["nrmloc"][:, 32 + h:33 + h])),
                reads=rd, writes=[("scr0",), ("nrm", "tmp", h)], tag=f"{tag}{b}{h}")
        key = ("nrm", "kmax") if dstoff == 0 else ("nrm", "qmax")
        self.c("ve", (lambda v, b=b, dstoff=dstoff: (
            v.tensor_copy(out=t["nrmloc"][:, dstoff:dstoff + 16], in_=t["nrmloc"][:, 32:48])
            if b == 0 else
            v.tensor_tensor(t["nrmloc"][:, dstoff:dstoff + 16],
                            t["nrmloc"][:, dstoff:dstoff + 16],
                            t["nrmloc"][:, 32:48], ALU.max))),
            reads=[("nrm", "tmp", h_) for h_ in range(NH)] + ([key] if b else []),
            writes=[key], tag=f"{tag}m{b}")

    def kqT_store(self, name, b, tag):
        t = self.t
        dr = self.dr
        tab = dr[f"{name}_in"].rearrange("(hl j p) t -> p hl j t", hl=2, j=16)
        for hl, srck in ((0, "scrb"), (1, "scrh")):
            srcv = t["scrb"].ap() if hl == 0 else t["scrh"].ap().bitcast(BF16)
            for j in range(16):
                ps = "ps_t0" if j % 2 == 0 else "ps_t1"
                self.mm0((lambda te, j=j, srcv=srcv, ps=ps: te.transpose(
                    self.t[ps][:], srcv[:, 128 * j:128 * (j + 1)], t["ident"][:])),
                    reads=[(srck,), ("ident",)], writes=[(ps,)], tag=f"{tag}T")
                self.c("ve", (lambda v, j=j, ps=ps: v.tensor_copy(
                    out=self.vw_kTb()[:, 128 * j:128 * (j + 1)], in_=self.t[ps][:])),
                    reads=[(ps,)], writes=[("scr1", "kT", j)])
            self.dma("gp", (lambda g, b=b, hl=hl, tab=tab: g.dma_start(
                out=tab[:, hl, :, 128 * b:128 * (b + 1)],
                in_=self.vw_kTb().rearrange("p (j c) -> p j c", j=16))),
                reads=[("scr1", "kT", j_) for j_ in range(16)],
                writes=[(f"dram:{name}_in", b, hl)], tag=f"{tag}st{hl}{b}")

    def ph_proj_kv(self):
        t = self.t
        dr = self.dr
        grp = [[0, 1, 2, 3], [4, 5, 6, 7]]
        kst = self.proj_matmuls("k", first_claim=True)
        for b in range(NB):
            rd = [("arA", "pq", b, d_) for d_ in range(4)]
            self.head_norms(kst, b, 0, "kn")
            self.split_hilo(kst[:, b, :], rd, f"k{b}")
            self.kqT_store("kT", b, f"k{b}")
        self.coll(lambda g: g.collective_compute(
            "AllGather", ALU.bypass, replica_groups=grp,
            ins=[dr["kT_in"][:]], outs=[dr["kT_tab"][:]]),
            reads=[("dram:kT_in", b_, hl_) for b_ in range(NB) for hl_ in range(2)],
            writes=[("dram:kT_tab",)], tag="ag_k")
        vst = self.proj_matmuls("v")
        vtab = dr["v_in"].rearrange("(hl h b k) d -> k hl h b d", hl=2, h=16, b=NB)
        for b in range(NB):
            rd = [("arA", "pq", b, d_) for d_ in range(4)]
            kvh, kvl = self.split_hilo(vst[:, b, :], rd, f"v{b}")
            for hl, srck, src in ((0, "scrb", kvh), (1, "scrh", kvl)):
                self.dma("gp", (lambda g, b=b, hl=hl, src=src, vtab=vtab: g.dma_start(
                    out=vtab[:, hl, :, b, :],
                    in_=src.rearrange("p (h d) -> p h d", h=16))),
                    reads=[(srck,)], writes=[("dram:v_in", b, hl)])
        self.coll(lambda g: g.collective_compute(
            "AllGather", ALU.bypass, replica_groups=grp,
            ins=[dr["v_in"][:]], outs=[dr["v_tab"][:]]),
            reads=[("dram:v_in", b_, hl_) for b_ in range(NB) for hl_ in range(2)],
            writes=[("dram:v_tab",)], tag="ag_v")
        for b in range(NB):
            self.c("ve", (lambda v, b=b: v.tensor_scalar(
                t["dqrow"][:, b, 0:1], t["dqrow"][:, b, 0:1],
                1.0 / float(np.sqrt(HD)), None, ALU.mult)),
                reads=[("dqrow", b, 0)], writes=[("dqrow", b, 0)])
        qst = self.proj_matmuls("q")
        for b in range(NB):
            rd = [("arA", "pq", b, d_) for d_ in range(4)]
            self.head_norms(qst, b, 16, "qn")
            self.split_hilo(qst[:, b, :], rd, f"q{b}")
            self.kqT_store("qT", b, f"q{b}")
        self.coll(lambda g: g.collective_compute(
            "AllGather", ALU.bypass, replica_groups=grp,
            ins=[dr["qT_in"][:]], outs=[dr["qT_tab"][:]]),
            reads=[("dram:qT_in", b_, hl_) for b_ in range(NB) for hl_ in range(2)],
            writes=[("dram:qT_tab",)], tag="ag_q")
        self.mm0(lambda te: te.transpose(self.t["ps_f0"][0:32, :],
                                         t["nrmloc"][:, 0:32], t["identf"][:]),
                 reads=[("nrm", "kmax"), ("nrm", "qmax"), ("identf",)],
                 writes=[("ps_f0",)], tag="nrm_T")
        self.c("ve", lambda v: v.reduce_max(t["nrmloc"][0:32, 60:61], self.t["ps_f0"][0:32, :],
                                            axis=AXX),
               reads=[("ps_f0",)], writes=[("nrm", "red")], tag="nrm_red")
        self.dma("gp", lambda g: g.dma_start(out=dr["nrm_in"][0:1, 0:32],
                                             in_=t["nrmloc"][0:32, 60:61]),
                 reads=[("nrm", "red")], writes=[("dram:nrm_in",)])
        self.coll(lambda g: g.collective_compute(
            "AllGather", ALU.bypass, replica_groups=grp,
            ins=[dr["nrm_in"][:]], outs=[dr["nrm_tab"][:]]),
            reads=[("dram:nrm_in",)], writes=[("dram:nrm_tab",)], tag="ag_nrm")

    def attn_mh(self):
        t = self.t
        dr = self.dr
        self.dma("sy", lambda sy: sy.dma_start(out=t["nrmloc"][0:4, 0:32],
                                               in_=dr["nrm_tab"][:, :]),
                 reads=[("dram:nrm_tab",)], writes=[("nrm", "tab")], tag="ld_nrm")
        self.mm0(lambda te: te.transpose(self.t["ps_f0"][0:32, 0:4],
                                         t["nrmloc"][0:4, 0:32], t["identf"][0:4, 0:4]),
                 reads=[("nrm", "tab"), ("identf",)], writes=[("ps_f0",)], tag="mh_T1")
        self.c("ve", lambda v: v.reduce_max(t["nrmloc"][0:32, 32:33],
                                            self.t["ps_f0"][0:32, 0:4], axis=AXX),
               reads=[("ps_f0",)], writes=[("nrm", "red2")], tag="mh_red")
        self.mm0(lambda te: te.transpose(self.t["ps_f0"][0:1, 0:32],
                                         t["nrmloc"][0:32, 32:33], t["identf"][0:32, 0:32]),
                 reads=[("nrm", "red2"), ("identf",)], writes=[("ps_f0",)], tag="mh_T2")
        self.c("ve", lambda v: v.tensor_copy(out=t["scr0"][0:1, 0:32],
                                             in_=self.t["ps_f0"][0:1, 0:32]),
               reads=[("ps_f0",)], writes=[("scr0",)], tag="mh_cp")
        self.c("ve", lambda v: v.tensor_tensor(
            t["nrmloc"][0:1, 34:50], t["scr0"][0:1, 0:16],
            t["scr0"][0:1, 16:32], ALU.mult),
            reads=[("scr0",)], writes=[("nrm", "m2")], tag="mh_m2")
        self.c("sc", lambda sc: sc.activation(t["nrmloc"][0:1, 34:50],
                                              t["nrmloc"][0:1, 34:50], AF.Sqrt),
               reads=[("nrm", "m2")], writes=[("nrm", "m2")], tag="mh_sqrt")
        self.c("ve", lambda v: v.tensor_scalar(t["nrmloc"][0:1, 34:50],
                                               t["nrmloc"][0:1, 34:50], -1.0, None, ALU.mult),
               reads=[("nrm", "m2")], writes=[("nrm", "m2")], tag="mh_neg")
        self.mm0(lambda te: te.transpose(self.t["ps_f0"][0:16, 16:17],
                                         t["nrmloc"][0:1, 34:50], t["identf"][0:1, 0:1]),
                 reads=[("nrm", "m2"), ("identf",)], writes=[("ps_f0",)], tag="mh_T3")
        self.c("ve", lambda v: v.tensor_copy(out=t["nrmloc"][0:16, 52:53],
                                             in_=self.t["ps_f0"][0:16, 16:17]),
               reads=[("ps_f0",)], writes=[("nrm", "mcol")], tag="mh_c1")
        self.mm0(lambda te: te.matmul(self.t["ps_r"][0:4, 8:9], t["hselsb"][:, :],
                                      t["nrmloc"][0:16, 52:53], start=True, stop=True),
                 reads=[("hselsb",), ("nrm", "mcol")], writes=[("ps_r", "ms")], tag="mh_sel")
        self.c("ve", lambda v: v.tensor_copy(out=t["nrmloc"][0:4, 60:61],
                                             in_=self.t["ps_r"][0:4, 8:9]),
               reads=[("ps_r", "ms")], writes=[("nrm", "msel")], tag="mh_c2")
        self.mm0(lambda te: te.transpose(self.t["ps_f0"][0:1, 20:24],
                                         t["nrmloc"][0:4, 60:61], t["identf"][0:4, 0:4]),
                 reads=[("nrm", "msel"), ("identf",)], writes=[("ps_f0",)], tag="mh_T4")
        self.c("ve", lambda v: v.tensor_copy(out=t["nrmloc"][0:1, 56:60],
                                             in_=self.t["ps_f0"][0:1, 20:24]),
               reads=[("ps_f0",)], writes=[("nrm", "mrow")], tag="mh_c3")
        self.mm0(lambda te: te.matmul(self.t["ps_r"][:, 12:16], t["onesf"][0:1, :],
                                      t["nrmloc"][0:1, 56:60], start=True, stop=True),
                 reads=[("onesf",), ("nrm", "mrow")], writes=[("ps_r", "mb")], tag="mh_bc")
        self.c("ve", lambda v: v.tensor_copy(out=t["mh"][:], in_=self.t["ps_r"][:, 12:16]),
               reads=[("ps_r", "mb")], writes=[("mh",)], tag="mh_fin")

    def attn_pair_loads(self, pr, slot, extra):
        t = self.t
        dr = self.dr
        wkeys = []
        for which, tabn in (("q", "qT_tab"), ("k", "kT_tab")):
            dst = self.vw_pair(slot, which)
            for r in range(4):
                for hl in range(2):
                    col = pr * 8 + r * 2 + hl
                    k = ("arA", "pair", slot, which, hl, r)
                    self.dma("gp", (lambda g, dst=dst, r=r, hl=hl, col=col, tabn=tabn:
                                    g.indirect_dma_start(
                                        out=dst[:, hl, 1024 * r:1024 * (r + 1)],
                                        out_offset=None, in_=dr[tabn][:, :],
                                        in_offset=IndirectOffsetOnAxis(
                                            t["attix"][:, col:col + 1], axis=0))),
                             reads=[(f"dram:{tabn}",), ("attix",)] + (extra if not wkeys else []),
                             writes=[k], tag=f"ld{which}{pr}")
                    wkeys.append(k)
        dst = self.vw_pair(slot, "v")
        for r in range(4):
            for hl in range(2):
                for bi in range(8):
                    col = 32 + pr * 64 + r * 16 + hl * 8 + bi
                    k = ("arA", "pair", slot, "v", hl, r, bi)
                    self.dma("gp", (lambda g, dst=dst, r=r, hl=hl, bi=bi, col=col:
                                    g.indirect_dma_start(
                                        out=dst[:, hl, (r * 8 + bi) * 128:(r * 8 + bi) * 128 + 128],
                                        out_offset=None, in_=dr["v_tab"][:, :],
                                        in_offset=IndirectOffsetOnAxis(
                                            t["attix"][:, col:col + 1], axis=0))),
                             reads=[("dram:v_tab",), ("attix",)],
                             writes=[k], tag=f"ldv{pr}")
                    wkeys.append(k)
        return wkeys

    def attn_pair_compute(self, pr, slot):
        t = self.t
        dr = self.dr
        kp = self.vw_pair(slot, "k")
        qp = self.vw_pair(slot, "q")
        vp = self.vw_pair(slot, "v")
        scrhb = t["scrh"].ap().bitcast(BF16)
        qkeys = [("arA", "pair", slot, "q", hl_, r_) for hl_ in range(2) for r_ in range(4)]
        for qc in range(8):
            for hl in range(2):
                for i in range(4):
                    g = 4 * qc + i
                    self.c("ve", (lambda v, hl=hl, i=i, g=g, qp=qp: v.tensor_copy(
                        out=t["qst"][:, hl * 512 + i * 128:hl * 512 + (i + 1) * 128],
                        in_=qp[:, hl, COLOF[g]:COLOF[g] + 128])),
                        reads=qkeys, writes=[("qst",)], tag=f"qa{pr}{qc}")
            nkb = 4 * qc + 4
            for kb in range(nkb):
                psS = "ps_a" if kb % 2 == 0 else "ps_b"
                co = COLOF[kb]
                kkey = [("arA", "pair", slot, "k", hl_, BLK_OWNER[kb]) for hl_ in range(2)]
                passes = [(0, 0), (0, 1), (1, 0)]
                for pi, (hk, hq) in enumerate(passes):
                    sp = (pi == 2 and kb < 4 * qc)
                    fn = (lambda te, hk=hk, hq=hq, co=co, psS=psS, pi=pi, kp=kp, sp=sp:
                          te.matmul(self.t[psS][:], kp[:, hk, co:co + 128],
                                    t["qst"][:, hq * 512:(hq + 1) * 512],
                                    start=(pi == 0), stop=sp))
                    if pi == 0:
                        self.mm0(fn, reads=kkey + [("qst",)], writes=[(psS,)],
                                 tag=f"S{pr}_{qc}_{kb}")
                    else:
                        self.mmn(fn, reads=kkey + [("qst",)], keys=[(psS,)])
                if kb >= 4 * qc:
                    i = kb - 4 * qc
                    self.mmn((lambda te, i=i, psS=psS: te.matmul(
                        self.t[psS][:], t["ident"][:], self.vw_maskt()[:, i * 512:(i + 1) * 512],
                        start=False, stop=True)),
                        reads=[("ident",)] + [("maskt", i, cb_) for cb_ in range(4)],
                        keys=[(psS,)], tag=f"Sm{pr}_{qc}_{kb}")
                pb = (kb % 2) * 512
                self.c("sc", (lambda sc, psS=psS, pb=pb: sc.activation(
                    t["scr0"][:, pb:pb + 512], self.t[psS][:], AF.Exp,
                    bias=t["mh"][:, pr:pr + 1])),
                    reads=[(psS,), ("mh",)], writes=[("scr0",)], tag=f"ex{pr}_{qc}_{kb}")
                self.c("sc", (lambda sc, pb=pb: sc.activation(
                    t["scrb"][:, pb:pb + 512], t["scr0"][:, pb:pb + 512], AF.Identity)),
                    reads=[("scr0",)], writes=[("scrb",)], tag=f"ph{pr}_{qc}_{kb}")
                self.c("ve", (lambda v, pb=pb, scrhb=scrhb: v.tensor_tensor(
                    scrhb[:, pb:pb + 512], t["scr0"][:, pb:pb + 512],
                    t["scrb"][:, pb:pb + 512], ALU.subtract)),
                    reads=[("scr0",), ("scrb",)], writes=[("scrh",)], tag=f"pl{pr}_{qc}_{kb}")
                vkey = [("arA", "pair", slot, "v", hl_, BLK_OWNER[kb], BLK_IDX[kb])
                        for hl_ in range(2)]
                avp = [(0, "scrb"), (0, "scrh"), (1, "scrb")]
                for pi, (hv, pm) in enumerate(avp):
                    pmv = t["scrb"].ap() if pm == "scrb" else scrhb
                    st = (kb == 0 and pi == 0)
                    sp = (kb == nkb - 1 and pi == 2)
                    fn = (lambda te, hv=hv, pmv=pmv, co=co, pb=pb, st=st, sp=sp, vp=vp:
                          te.matmul(self.t["ps_c"][:], vp[:, hv, co:co + 128],
                                    pmv[:, pb:pb + 512], start=st, stop=sp))
                    if st:
                        self.mm0(fn, reads=vkey + [(pm,)], writes=[("ps_c",)],
                                 tag=f"AV{pr}_{qc}")
                    else:
                        self.mmn(fn, reads=vkey + [(pm,)], keys=[("ps_c",)])
                for pi, pm in enumerate(("scrb", "scrh")):
                    pmv = t["scrb"].ap() if pm == "scrb" else scrhb
                    st = (kb == 0 and pi == 0)
                    sp = (kb == nkb - 1 and pi == 1)
                    fn = (lambda te, pmv=pmv, pb=pb, st=st, sp=sp: te.matmul(
                        self.t["ps_d"][0:1, :], t["ones128"][:, 0:1], pmv[:, pb:pb + 512],
                        start=st, stop=sp))
                    if st:
                        self.mm0(fn, reads=[("ones128",), (pm,)], writes=[("ps_d",)],
                                 tag=f"DN{pr}_{qc}")
                    else:
                        self.mmn(fn, reads=[("ones128",), (pm,)], keys=[("ps_d",)])
            mv32 = self.vw_mvt32()
            self.c("ve", lambda v, mv32=mv32: v.tensor_copy(out=mv32[:, 0:512],
                                                            in_=self.t["ps_c"][:]),
                   reads=[("ps_c",)], writes=[("mvt",)], tag=f"avv{pr}{qc}")
            self.c("ve", lambda v, mv32=mv32: v.tensor_copy(out=mv32[0:1, 512:1024],
                                                            in_=self.t["ps_d"][0:1, :]),
                   reads=[("ps_d",)], writes=[("mvt", "dn")], tag=f"dnv{pr}{qc}")
            for cb in range(4):
                self.mm0((lambda te, cb=cb, mv32=mv32: te.transpose(
                    self.t["ps_f0"][:, cb:cb + 1],
                    mv32[0:1, 512 + cb * 128:512 + (cb + 1) * 128], t["identf"][0:1, 0:1])),
                    reads=[("mvt", "dn"), ("identf",)], writes=[("ps_f0",)], tag=f"dT{pr}{qc}")
            self.c("ve", lambda v: v.tensor_copy(out=t["nrmloc"][:, 48:52],
                                                 in_=self.t["ps_f0"][:, 0:4]),
                   reads=[("ps_f0",)], writes=[("nrm", "dn4")], tag=f"dc{pr}{qc}")
            self.c("ve", lambda v: v.reciprocal(t["nrmloc"][:, 52:56], t["nrmloc"][:, 48:52]),
                   reads=[("nrm", "dn4")], writes=[("nrm", "rdn")], tag=f"dr{pr}{qc}")
            for cb in range(4):
                self.mm0((lambda te, cb=cb, mv32=mv32: te.transpose(
                    self.t["ps_f0"][:], mv32[:, cb * 128:(cb + 1) * 128], t["identf"][:])),
                    reads=[("mvt",), ("identf",)], writes=[("ps_f0",)], tag=f"aT{pr}{qc}")
                self.c("ve", (lambda v, cb=cb: v.tensor_scalar(
                    t["scr1"][:, cb * 128:(cb + 1) * 128], self.t["ps_f0"][:],
                    t["nrmloc"][:, 52 + cb:53 + cb], None, ALU.mult)),
                    reads=[("ps_f0",), ("nrm", "rdn")], writes=[("scr1",)], tag=f"ae{pr}{qc}")
            self.c("sc", lambda sc: sc.activation(t["scrb"][:, 1024:1536],
                                                  t["scr1"][:, 0:512], AF.Identity),
                   reads=[("scr1",)], writes=[("scrb",)], tag=f"ah{pr}{qc}")
            self.c("ve", (lambda v, scrhb=scrhb: v.tensor_tensor(
                scrhb[:, 1024:1536], t["scr1"][:, 0:512], t["scrb"][:, 1024:1536],
                ALU.subtract)),
                reads=[("scr1",), ("scrb",)], writes=[("scrh",)], tag=f"al{pr}{qc}")
            avt = dr["av_in"].rearrange("(q cb p) c -> p q cb c", q=8, cb=4)
            for hl, srck in ((0, "scrb"), (1, "scrh")):
                src = (t["scrb"].ap() if hl == 0 else scrhb)[:, 1024:1536]
                self.dma("gp", (lambda g, hl=hl, src=src, qc=qc, avt=avt: g.dma_start(
                    out=avt[:, qc, :, hl * 512 + pr * 128:hl * 512 + (pr + 1) * 128],
                    in_=src.rearrange("p (cb d) -> p cb d", cb=4))),
                    reads=[(srck,)], writes=[("dram:av_in", pr, qc, hl)],
                    tag=f"avst{pr}{qc}")

    def ph_attn(self):
        dr = self.dr
        t = self.t
        self.attn_mh()
        mk_ = self.vw_maskt()
        first_m = True
        for i in range(4):
            for cb in range(4):
                dst = mk_[:, i * 512 + cb * 128: i * 512 + (cb + 1) * 128]
                wk = [("maskt", i, cb)] + ([("mvt",)] if first_m else [])
                first_m = False
                if cb < i:
                    self.c("ve", (lambda v, dst=dst: v.memset(dst, NEG)), writes=wk)
                elif cb > i:
                    self.c("ve", (lambda v, dst=dst: v.memset(dst, 0.0)), writes=wk)
                else:
                    self.c("gp", (lambda g, dst=dst: g.affine_select(
                        dst, t["zerob"][:], pattern=[[1, 128]], compare_op=ALU.is_ge,
                        fill=NEG, base=0, channel_multiplier=-1)),
                        reads=[("zerob",)], writes=wk)
        mv32 = self.vw_mvt32()
        self.c("ve", lambda v, mv32=mv32: v.memset(mv32[:, 2048:4096], 0.0),
               reads=[("mvt",)], writes=[("mvt", "z")], tag="z_mk")
        if self.upto == "all":
            for r in range(65):
                self.dma("sy", (lambda sy, r=r, mv32=mv32: sy.dma_start(
                    out=dr["rs_in"][128 * r:128 * (r + 1), :], in_=mv32[:, 2048:4096])),
                    reads=[("mvt", "z")], writes=[("dram:rs_in", "z", r)], tag=f"z_rs{r}")
        extra = self.claim("arA")
        self.attn_pair_loads(0, 0, extra)
        for pr in range(4):
            if pr < 3:
                self.attn_pair_loads(pr + 1, (pr + 1) % 2, [])
            self.attn_pair_compute(pr, pr % 2)
        self.coll(lambda g: g.collective_compute(
            "AllGather", ALU.bypass, replica_groups=[[0, 1, 2, 3], [4, 5, 6, 7]],
            ins=[dr["av_in"][:]], outs=[dr["av_tab"][:]]),
            reads=[("dram:av_in", pr_, qc_, hl_) for pr_ in range(4) for qc_ in range(8)
                   for hl_ in range(2)],
            writes=[("dram:av_tab",)], tag="ag_av")

    def ph_wo(self):
        t = self.t
        dr = self.dr
        avloc = self.vw_avloc()
        extra = self.claim("arA")
        for r in range(4):
            for b in range(NB):
                col = r * 8 + b
                k = ("arA", "avloc", r, b)
                self.dma("gp", (lambda g, r=r, b=b, col=col, avloc=avloc:
                                g.indirect_dma_start(
                                    out=avloc[:, b, r, :, :].rearrange("p hl c -> p (hl c)"),
                                    out_offset=None, in_=dr["av_tab"][:, :],
                                    in_offset=IndirectOffsetOnAxis(
                                        t["avix"][:, col:col + 1], axis=0))),
                         reads=[("dram:av_tab",), ("avix",)] + (extra if (r == 0 and b == 0) else []),
                         writes=[k], tag=f"ldav{r}{b}")
        av = self.vw_xres()
        extrab = self.claim("arB")
        for b in range(NB):
            for r in range(4):
                self.c("ve", (lambda v, b=b, r=r, av=av, avloc=avloc: v.tensor_tensor(
                    av[:, b, r * 512:(r + 1) * 512], avloc[:, b, r, 0, :],
                    avloc[:, b, r, 1, :], ALU.add)),
                    reads=[("arA", "avloc", r, b)],
                    writes=[("arB", "av", b, r)] + (extrab if (b == 0 and r == 0) else []),
                    tag=f"av{b}{r}")
        if "dbg_av" in self.taps:
            for b in range(NB):
                self.dma("sy", (lambda sy, b=b, av=av: sy.dma_start(
                    out=self.dbgt["dbg_av"][128 * b:128 * (b + 1), :], in_=av[:, b, :])),
                    reads=[("arB", "av", b, r_) for r_ in range(4)], writes=[("dram:dbgav", b)])
        oqT = self.vw_oqT()
        extra2 = self.claim("arA")
        for b in range(NB):
            rdav = [("arB", "av", b, r_) for r_ in range(4)]
            self.c("ve", (lambda v, b=b, av=av: v.reduce_max(
                t["sxq"][:, b:b + 1], av[:, b, :], axis=AXX, apply_absolute_value=True)),
                reads=rdav, writes=[("sxq", b)], tag=f"oam{b}")
            self.c("ve", (lambda v, b=b: v.tensor_scalar(
                t["sxq"][:, b:b + 1], t["sxq"][:, b:b + 1], 1e-5, None, ALU.max)),
                reads=[("sxq", b)], writes=[("sxq", b)])
            self.c("ve", (lambda v, b=b: v.tensor_scalar(
                t["stats"][:, 16 + b:17 + b], t["sxq"][:, b:b + 1], 1.0 / 127.0, None, ALU.mult)),
                reads=[("sxq", b)], writes=[("stats", "isx", b)])
            self.c("ve", (lambda v, b=b: v.reciprocal(
                t["sxq"][:, b:b + 1], t["sxq"][:, b:b + 1])),
                reads=[("sxq", b)], writes=[("sxq", b)])
            self.c("ve", (lambda v, b=b: v.tensor_scalar(
                t["sxq"][:, b:b + 1], t["sxq"][:, b:b + 1], 127.0, None, ALU.mult)),
                reads=[("sxq", b)], writes=[("sxq", b)], tag=f"osx{b}")
            self.c("ve", (lambda v, b=b, av=av: v.tensor_scalar(
                t["scrh"][:, :], av[:, b, :], t["sxq"][:, b:b + 1], RC, ALU.mult, ALU.add)),
                reads=rdav + [("sxq", b)], writes=[("scrh",)])
            self.c("ve", (lambda v, b=b: v.tensor_scalar(
                t["scrb"][:, :], t["scrh"][:, :], RC, None, ALU.subtract)),
                reads=[("scrh",)], writes=[("scrb",)], tag=f"oq{b}")
            for j in range(16):
                ps = "ps_t0" if j % 2 == 0 else "ps_t1"
                self.mm0((lambda te, j=j, ps=ps: te.transpose(
                    self.t[ps][:], t["scrb"][:, 128 * j:128 * (j + 1)], t["ident"][:])),
                    reads=[("scrb",), ("ident",)], writes=[(ps,)], tag="oqT_t")
                self.c("ve", (lambda v, j=j, b=b, ps=ps, oqT=oqT: v.tensor_copy(
                    out=oqT[:, j, 128 * b:128 * (b + 1)], in_=self.t[ps][:])),
                    reads=[(ps,)], writes=[("arA", "oqT", j, b)] +
                    (extra2 if (b == 0 and j == 0) else []))
            self.c("ve", (lambda v, b=b: v.tensor_tensor(
                t["dqrow"][:, b, 3:4], t["stats"][:, 16 + b:17 + b],
                t["swg"][:, 3:4], ALU.mult)),
                reads=[("stats", "isx", b), ("swg",)], writes=[("dqrow", b, 3)],
                tag=f"odq{b}")
        x1 = self.vw_xres()
        extrab2 = self.claim("arB")
        for b in range(NB):
            self.dma("sy", (lambda sy, b=b, x1=x1: sy.dma_start(
                out=x1[:, b, :], in_=dr["xspill"][128 * b:128 * (b + 1), :])),
                reads=[("dram:xspill", b)],
                writes=[("arB", "x1", b)] + (extrab2 if b == 0 else []), tag=f"ldx{b}")
        for dt_ in range(4):
            self.dma("sy", (lambda sy, dt_=dt_: sy.dma_start(
                out=t["mvt"][:, :, :],
                in_=dr["wt_o"].rearrange("(a p) o -> p a o", p=128)
                    [:, :, 512 * dt_:512 * (dt_ + 1)])),
                reads=[("dram:wt_o",)], writes=[("mvt",), ("mvt", "dn"), ("mvt", "z")],
                tag=f"mv_o{dt_}")
            for b in range(NB):
                ps = "ps_a" if b % 2 == 0 else "ps_b"
                for j in range(16):
                    mmf = (lambda te, j=j, b=b, ps=ps, oqT=oqT: te.matmul(
                        self.t[ps][:], oqT[:, j, 128 * b:128 * (b + 1)],
                        t["mvt"][:, j, :], start=(j == 0), stop=(j == 15)))
                    if j == 0:
                        self.mm0(mmf, reads=[("arA", "oqT", j, b), ("mvt",)],
                                 writes=[(ps,)], tag=f"oj{dt_}{b}")
                    else:
                        self.mmn(mmf, reads=[("arA", "oqT", j, b), ("mvt",)], keys=[(ps,)])
                self.c("ve", (lambda v, b=b, ps=ps: v.tensor_scalar(
                    t["scr0"][:, 0:512], self.t[ps][:], t["dqrow"][:, b, 3:4], None, ALU.mult)),
                    reads=[(ps,), ("dqrow", b, 3)], writes=[("scr0",)], tag=f"oe{dt_}{b}")
                self.c("ve", (lambda v, b=b, dt_=dt_, x1=x1: v.tensor_tensor(
                    x1[:, b, 512 * dt_:512 * (dt_ + 1)], x1[:, b, 512 * dt_:512 * (dt_ + 1)],
                    t["scr0"][:, 0:512], ALU.add)),
                    reads=[("arB", "x1", b), ("scr0",)], writes=[("arB", "x1", b)],
                    tag=f"oa{dt_}{b}")
        if "dbg_x1" in self.taps:
            for b in range(NB):
                self.dma("sy", (lambda sy, b=b, x1=x1: sy.dma_start(
                    out=self.dbgt["dbg_x1"][128 * b:128 * (b + 1), :], in_=x1[:, b, :])),
                    reads=[("arB", "x1", b)], writes=[("dram:dbgx1", b)])

    def ph_ffn_pre(self):
        t = self.t
        dr = self.dr
        x1 = self.vw_xres()
        xrT = self.vw_xrT()
        mv32 = self.vw_mvt32()
        extra = self.claim("arA")
        self.dma("sy", lambda sy: sy.dma_start(out=t["scr0"][0:1, :], in_=self.normw_in[:]),
                 reads=[("scr0",)], writes=[("scr0",), ("tok_ffn",)] + extra, tag="ld_nw")
        nw = self.af32("arA")[:, 18432:20480]
        for cc in range(4):
            self.mm0((lambda te, cc=cc: te.matmul(
                self.t["ps_a"][:], t["onesf"][0:1, :], t["scr0"][0:1, 512 * cc:512 * (cc + 1)],
                start=True, stop=True)),
                reads=[("scr0",), ("onesf",), ("tok_ffn",)], writes=[("ps_a",)], tag=f"nwb{cc}")
            self.c("ve", (lambda v, cc=cc, nw=nw: v.tensor_copy(
                out=nw[:, 512 * cc:512 * (cc + 1)], in_=self.t["ps_a"][:])),
                reads=[("ps_a",)], writes=[("arA", "nw", cc)], tag=f"nwc{cc}")
        gw = self.af32("arA")[0:32, 16384:18432]
        gwT = self.af32("arA")[:, 20480:20992]
        self.dma("sy", (lambda sy, gw=gw: sy.dma_start(
            out=gw, in_=self.gate_in[:, :])),
            reads=[("tok_ffn",)], writes=[("arA", "gw")], tag="ld_gw")
        for j in range(16):
            self.mm0((lambda te, j=j, gw=gw: te.transpose(
                self.t["ps_f0"][:, 0:32], gw[:, 128 * j:128 * (j + 1)],
                t["identf"][0:32, 0:32])),
                reads=[("arA", "gw"), ("identf",)], writes=[("ps_f0",)], tag=f"gwT{j}")
            self.c("ve", (lambda v, j=j, gwT=gwT: v.tensor_copy(
                out=gwT[:, 32 * j:32 * (j + 1)], in_=self.t["ps_f0"][:, 0:32])),
                reads=[("ps_f0",)], writes=[("arA", "gwT", j)], tag=f"gwc{j}")
        self.c("ve", lambda v: v.memset(t["scrb"][:], 0.0),
               reads=[("tok_ffn",)], writes=[("scrb",)], tag="zjunk")
        for half in range(2):
            self.dma("gp", (lambda g, half=half: g.dma_start(
                out=dr["xr_in"][1024:1040, 2048 * half:2048 * (half + 1)],
                in_=t["scrb"][0:16, :])),
                reads=[("scrb",)], writes=[("dram:xr_in", "junk", half)], tag=f"zj{half}")
        self.c("ve", lambda v: v.memset(t["gwt"][:], 0.0),
               reads=[("tok_ffn",)], writes=[("gwt", "z")], tag="zrj0")
        self.dma("gp", lambda g: g.dma_start(out=dr["rt_in"][1024:1040, :],
                                             in_=t["gwt"][0:16, 0:8]),
                 reads=[("gwt", "z")], writes=[("dram:rt_in", "junk")], tag="zrj1")
        for b in range(NB):
            rdx = [("arB", "x1", b)]
            self.c("sc", (lambda sc, b=b, x1=x1: sc.activation(
                t["scr1"][:, 0:D], x1[:, b, :], AF.Identity,
                accum_out=t["stats"][:, b:b + 1])),
                reads=rdx, writes=[("scr1",), ("stats", "m", b)], tag=f"l2m{b}")
            self.c("ve", (lambda v, b=b: v.tensor_scalar(
                t["stats"][:, b:b + 1], t["stats"][:, b:b + 1], 1.0 / D, None, ALU.mult)),
                reads=[("stats", "m", b)], writes=[("stats", "m", b)])
            self.c("ve", (lambda v, b=b, x1=x1: v.tensor_scalar(
                t["scr0"][:, :], x1[:, b, :], t["stats"][:, b:b + 1], None, ALU.subtract)),
                reads=rdx + [("stats", "m", b)], writes=[("scr0",)], tag=f"l2d{b}")
            self.c("sc", (lambda sc, b=b: sc.activation(
                t["scr1"][:, 0:D], t["scr0"][:, :], AF.Square,
                accum_out=t["stats"][:, 8 + b:9 + b])),
                reads=[("scr0",)], writes=[("scr1",), ("stats", "ss", b)], tag=f"l2s{b}")
            self.c("sc", (lambda sc, b=b: sc.activation(
                t["stats"][:, 16 + b:17 + b], t["stats"][:, 8 + b:9 + b], AF.Sqrt,
                bias=t["epsln"][:], scale=1.0 / D)),
                reads=[("stats", "ss", b), ("epsln",)], writes=[("stats", "std", b)])
            self.c("ve", (lambda v, b=b: v.reciprocal(
                t["stats"][:, 24 + b:25 + b], t["stats"][:, 16 + b:17 + b])),
                reads=[("stats", "std", b)], writes=[("stats", "rstd", b)])
            self.c("ve", (lambda v, b=b: v.tensor_scalar(
                t["scr0"][:, :], t["scr0"][:, :], t["stats"][:, 24 + b:25 + b], None, ALU.mult)),
                reads=[("scr0",), ("stats", "rstd", b)], writes=[("scr0",)], tag=f"ln2{b}")
            self.c("ve", (lambda v, b=b, x1=x1: v.tensor_tensor(
                x1[:, b, :], x1[:, b, :], t["scr0"][:, :], ALU.add)),
                reads=rdx + [("scr0",)], writes=[("arB", "oacc", b)], tag=f"oac{b}")
            self.c("sc", (lambda sc, b=b: sc.activation(
                t["scr1"][:, 0:D], t["scr0"][:, :], AF.Square,
                accum_out=t["stats"][:, 8 + b:9 + b])),
                reads=[("scr0",)], writes=[("scr1",), ("stats", "ss", b)], tag=f"rms{b}")
            self.c("sc", (lambda sc, b=b: sc.activation(
                t["stats"][:, 16 + b:17 + b], t["stats"][:, 8 + b:9 + b], AF.Sqrt,
                bias=t["epsrms"][:], scale=1.0 / D)),
                reads=[("stats", "ss", b), ("epsrms",)], writes=[("stats", "std", b)])
            self.c("ve", (lambda v, b=b: v.reciprocal(
                t["stats"][:, 24 + b:25 + b], t["stats"][:, 16 + b:17 + b])),
                reads=[("stats", "std", b)], writes=[("stats", "rstd", b)])
            self.c("ve", (lambda v, b=b: v.tensor_scalar(
                t["scr1"][:, :], t["scr0"][:, :], t["stats"][:, 24 + b:25 + b], None, ALU.mult)),
                reads=[("scr0",), ("stats", "rstd", b)], writes=[("scr1",)], tag=f"xr0{b}")
            self.c("ve", (lambda v, b=b, nw=nw: v.tensor_tensor(
                t["scr1"][:, :], t["scr1"][:, :], nw[:, :], ALU.mult)),
                reads=[("scr1",)] + [("arA", "nw", cc_) for cc_ in range(4)],
                writes=[("scr1",)], tag=f"xr{b}")
            self.c("sc", lambda sc: sc.activation(t["scrb"][:], t["scr1"][:, :], AF.Identity),
                   reads=[("scr1",)], writes=[("scrb",)], tag=f"xh{b}")
            scrhb = t["scrh"].ap().bitcast(BF16)
            self.c("ve", (lambda v, scrhb=scrhb: v.tensor_tensor(
                scrhb[:, :], t["scr1"][:, :], t["scrb"][:], ALU.subtract)),
                reads=[("scr1",), ("scrb",)], writes=[("scrh",)], tag=f"xl{b}")
            for hl, srck in ((0, "scrb"), (1, "scrh")):
                src = t["scrb"].ap() if hl == 0 else scrhb
                self.dma("gp", (lambda g, b=b, hl=hl, src=src: g.dma_start(
                    out=dr["xr_in"][128 * b:128 * (b + 1), 2048 * hl:2048 * (hl + 1)],
                    in_=src[:, :])),
                    reads=[(srck,)], writes=[("dram:xr_in", b, hl)], tag=f"xst{b}{hl}")
                for j in range(16):
                    ps = "ps_t0" if j % 2 == 0 else "ps_t1"
                    self.mm0((lambda te, j=j, src=src, ps=ps: te.transpose(
                        self.t[ps][:], src[:, 128 * j:128 * (j + 1)], t["ident"][:])),
                        reads=[(srck,), ("ident",)], writes=[(ps,)], tag="xrT_t")
                    self.c("ve", (lambda v, j=j, b=b, hl=hl, ps=ps, xrT=xrT: v.tensor_copy(
                        out=xrT[:, hl, j, 128 * b:128 * (b + 1)], in_=self.t[ps][:])),
                        reads=[(ps,)], writes=[("arA", "xrT", hl, j, b)])
            self.c("ve", (lambda v, b=b, xrT=xrT, mv32=mv32: v.tensor_tensor(
                mv32[:, 0:2048].rearrange("p (j c) -> p j c", j=16),
                xrT[:, 0, :, 128 * b:128 * (b + 1)], xrT[:, 1, :, 128 * b:128 * (b + 1)],
                ALU.add)),
                reads=[("arA", "xrT", hl_, j_, b) for hl_ in range(2) for j_ in range(16)],
                writes=[("mvt",)], tag=f"x32{b}")
            for j in range(16):
                fn = (lambda te, j=j, gwT=gwT, mv32=mv32: te.matmul(
                    self.t["ps_r"][:, 0:32], mv32[:, 128 * j:128 * (j + 1)],
                    gwT[:, 32 * j:32 * (j + 1)], start=(j == 0), stop=(j == 15)))
                if j == 0:
                    self.mm0(fn, reads=[("mvt",), ("arA", "gwT", j)],
                             writes=[("ps_r", "lg")], tag=f"lg{b}")
                else:
                    self.mmn(fn, reads=[("mvt",), ("arA", "gwT", j)], keys=[("ps_r", "lg")])
            self.c("ve", lambda v: v.tensor_copy(out=t["nrmloc"][:, 0:32],
                                                 in_=self.t["ps_r"][:, 0:32]),
                   reads=[("ps_r", "lg")], writes=[("nrm", "lgv")], tag=f"lge{b}")
            self.c("ve", lambda v: v.reduce_max(t["nrmloc"][:, 40:41], t["nrmloc"][:, 0:32],
                                                axis=AXX),
                   reads=[("nrm", "lgv")], writes=[("nrm", "mx")], tag=f"lgm{b}")
            self.c("ve", lambda v: v.tensor_scalar(t["nrmloc"][:, 41:42], t["nrmloc"][:, 40:41],
                                                   -1.0, None, ALU.mult),
                   reads=[("nrm", "mx")], writes=[("nrm", "nmx")])
            self.c("sc", lambda sc: sc.activation(t["scr1"][:, 0:32], t["nrmloc"][:, 0:32],
                                                  AF.Exp, bias=t["nrmloc"][:, 41:42]),
                   reads=[("nrm", "lgv"), ("nrm", "nmx")], writes=[("scr1",)], tag=f"lge2{b}")
            self.c("ve", lambda v: v.max(t["nrmloc"][:, 32:40], t["scr1"][:, 0:32]),
                   reads=[("scr1",)], writes=[("nrm", "t8")], tag=f"t8{b}")
            self.c("ve", (lambda v: v.max_index(
                self.vw_idsu(), t["nrmloc"][:, 32:40], t["scr1"][:, 0:32])),
                reads=[("nrm", "t8"), ("scr1",)], writes=[("idsu",)], tag=f"mi{b}")
            self.c("ve", lambda v: v.tensor_copy(out=self.vw_rtloc()[:, 0:4],
                                                 in_=self.vw_idsu()[:, 0:4]),
                   reads=[("idsu",)], writes=[("rtloc", "ids")], tag=f"idc{b}")
            self.c("sc", lambda sc: sc.activation(t["scr1"][:, 32:36], t["nrmloc"][:, 32:36],
                                                  AF.Identity,
                                                  accum_out=t["nrmloc"][:, 42:43]),
                   reads=[("nrm", "t8")], writes=[("scr1",), ("nrm", "s4")], tag=f"s4{b}")
            self.c("ve", lambda v: v.reciprocal(t["nrmloc"][:, 43:44], t["nrmloc"][:, 42:43]),
                   reads=[("nrm", "s4")], writes=[("nrm", "rs4")])
            self.c("ve", lambda v: v.tensor_scalar(self.vw_rtloc()[:, 4:8], t["nrmloc"][:, 32:36],
                                                   t["nrmloc"][:, 43:44], None, ALU.mult),
                   reads=[("nrm", "t8"), ("nrm", "rs4")], writes=[("rtloc", "w")], tag=f"wt{b}")
            self.dma("gp", (lambda g, b=b: g.dma_start(
                out=dr["rt_in"][128 * b:128 * (b + 1), :], in_=self.vw_rtloc()[:, :])),
                reads=[("rtloc", "ids"), ("rtloc", "w")], writes=[("dram:rt_in", b)],
                tag=f"rst{b}")
        self.coll(lambda g: g.collective_compute(
            "AllGather", ALU.bypass, replica_groups=[list(range(NCORE))],
            ins=[dr["xr_in"][:]], outs=[dr["xrtab"][:]]),
            reads=[("dram:xr_in", b_, hl_) for b_ in range(NB) for hl_ in range(2)] +
                  [("dram:xr_in", "junk", hl_) for hl_ in range(2)],
            writes=[("dram:xrtab",)], tag="ag_xr")
        self.coll(lambda g: g.collective_compute(
            "AllGather", ALU.bypass, replica_groups=[list(range(NCORE))],
            ins=[dr["rt_in"][:]], outs=[dr["rttab"][:]]),
            reads=[("dram:rt_in", b_) for b_ in range(NB)] + [("dram:rt_in", "junk")],
            writes=[("dram:rttab",)], tag="ag_rt")
        for b in range(NB):
            self.dma("gp", (lambda g, b=b, x1=x1: g.dma_start(
                out=dr["oaspill"][128 * b:128 * (b + 1), :], in_=x1[:, b, :])),
                reads=[("arB", "oacc", b)], writes=[("dram:oaspill", b)], tag=f"spo{b}")
        self.dma("gp", lambda g: g.dma_start(
            out=dr["xrT_sp"].rearrange("p t -> p t"), in_=self.abf("arA")[:, 0:32768]),
            reads=[("arA", "xrT", hl_, j_, b_) for hl_ in range(2) for j_ in range(16)
                   for b_ in range(NB)],
            writes=[("dram:xrT_sp",)], tag="spxrT")
        if "dbg_xr" in self.taps:
            for b in range(NB):
                self.dma("gp", (lambda sy, b=b: sy.dma_start(
                    out=self.dbgt["dbg_xr"][128 * b:128 * (b + 1), :],
                    in_=dr["xr_in"][128 * b:128 * (b + 1), 0:2048])),
                    reads=[("dram:xr_in", b, 0)], writes=[("dram:dbgxr", b)])
        if "dbg_rt" in self.taps:
            for b in range(NB):
                self.dma("sy", (lambda sy, b=b: sy.dma_start(
                    out=self.dbgt["dbg_rt"][128 * b:128 * (b + 1), :],
                    in_=dr["rt_in"][128 * b:128 * (b + 1), :])),
                    reads=[("dram:rt_in", b)], writes=[("dram:dbgrt", b)])

    def ph_routing(self):
        t = self.t
        dr = self.dr
        A = self.af32("arA")
        extra = self.claim("arA")
        first = True
        for cc in range(NCORE):
            s_ = cc // 4
            o = cc % 4
            for rn in range(2):
                pbase = s_ * 64 + (8 * o if rn == 0 else 56 - 8 * o)
                src = dr["rttab"][ROWS * cc + 512 * rn:ROWS * cc + 512 * rn + 512, :]
                self.dma("sy", (lambda sy, pbase=pbase, src=src, A=A: sy.dma_start(
                    out=A[pbase:pbase + 8, 0:512],
                    in_=src.rearrange("(pp i) k -> pp (i k)", pp=8))),
                    reads=[("dram:rttab",)],
                    writes=[("arA", "rt", "raw", cc, rn)] + (extra + [("tok_rt",)] if first else []),
                    tag=f"rtl{cc}{rn}")
                first = False
        rawk = [("arA", "rt", "raw", cc_, rn_) for cc_ in range(NCORE) for rn_ in range(2)]
        rv = A[:, 0:512].rearrange("p (i k) -> p i k", i=64)
        self.c("ve", lambda v, rv=rv, A=A: v.tensor_copy(
            out=A[:, 512:768].rearrange("p (i k) -> p i k", i=64), in_=rv[:, :, 0:4]),
            reads=rawk, writes=[("arA", "rt", "ids")], tag="ids")
        self.c("ve", lambda v, rv=rv, A=A: v.tensor_copy(
            out=A[:, 768:1024].rearrange("p (i k) -> p i k", i=64), in_=rv[:, :, 4:8]),
            reads=rawk, writes=[("arA", "rt", "wts")], tag="wts")
        self.c("ve", lambda v, A=A: v.memset(A[:, 1024:1280], 0.0),
               reads=[("tok_rt",)], writes=[("arA", "rt", "z")], tag="rz")
        self.c("ve", lambda v, A=A: v.memset(A[:, 2048:2304], 0.0),
               reads=[("tok_rt",)], writes=[("arA", "rt", "tgt")], tag="tz")
        self.c("ve", lambda v, A=A: v.memset(A[:, 2304:2560], 0.0),
               reads=[("tok_rt",)], writes=[("arA", "rt", "ovr")], tag="oz")
        self.c("gp", lambda g, A=A: g.iota(A[:, 3328:3584], pattern=[[1, 64], [0, 4]],
                                           base=0, channel_multiplier=0,
                                           allow_small_or_imprecise_dtypes=True),
               reads=[("tok_rt",)], writes=[("arA", "rt", "iota")], tag="iota")
        self.c("gp", lambda g, A=A: g.iota(A[:, 3072:3328], pattern=[[0, 256]],
                                           base=NE * CAP, channel_multiplier=1,
                                           allow_small_or_imprecise_dtypes=True),
               reads=[("tok_rt",)], writes=[("arA", "rt", "junk")], tag="junkr")
        self.c("ve", lambda v, A=A: v.tensor_scalar(
            A[:, 2816:3072], A[:, 3328:3584], t["rowb"][:], None, ALU.add),
            reads=[("arA", "rt", "iota"), ("rowb",)], writes=[("arA", "rt", "gv")], tag="gvals")
        for e in range(NE):
            self.c("ve", (lambda v, e=e, A=A: v.tensor_scalar(
                A[:, 1536:1792], A[:, 512:768], float(e), None, ALU.is_equal)),
                reads=[("arA", "rt", "ids")], writes=[("arA", "rt", "mask")], tag=f"msk{e}")
            self.c("ve", lambda v, A=A: v.tensor_tensor_scan(
                A[:, 1280:1536], A[:, 1536:1792], A[:, 1024:1280], 0.0, ALU.add, ALU.add),
                reads=[("arA", "rt", "mask"), ("arA", "rt", "z")], writes=[("arA", "rt", "pref")], tag=f"scan{e}")
            self.c("sc", (lambda sc, A=A: sc.activation(
                t["scr0"][:, 0:256], A[:, 1536:1792], AF.Identity,
                accum_out=A[:, 3584:3585])),
                reads=[("arA", "rt", "mask")], writes=[("scr0",), ("arA", "rt", "s")], tag=f"rs{e}")
            self.c("ve", lambda v, A=A: v.tensor_copy(out=t["scrb"][:, 0:1],
                                                      in_=A[:, 3584:3585]),
                   reads=[("arA", "rt", "s")], writes=[("scrb",)], tag=f"sb16{e}")
            self.mm0((lambda te: te.matmul(self.t["ps_r"][:, 16:17], t["ltri"][:],
                                           t["scrb"][:, 0:1], start=True, stop=True)),
                     reads=[("ltri",), ("scrb",)], writes=[("ps_r", "sb")], tag=f"sbm{e}")
            self.c("ve", lambda v, A=A: v.tensor_copy(out=A[:, 3585:3586],
                                                      in_=self.t["ps_r"][:, 16:17]),
                   reads=[("ps_r", "sb")], writes=[("arA", "rt", "sbase")], tag=f"sbc{e}")
            self.c("ve", lambda v, A=A: v.tensor_tensor(
                A[:, 2560:2816], A[:, 1280:1536], A[:, 1536:1792], ALU.subtract),
                reads=[("arA", "rt", "pref"), ("arA", "rt", "mask")], writes=[("arA", "rt", "pos")], tag=f"p0{e}")
            self.c("ve", lambda v, A=A: v.tensor_scalar(
                A[:, 2560:2816], A[:, 2560:2816], A[:, 3585:3586], None, ALU.add),
                reads=[("arA", "rt", "pos"), ("arA", "rt", "sbase")], writes=[("arA", "rt", "pos")], tag=f"p1{e}")
            self.c("ve", lambda v, A=A: v.tensor_scalar(
                t["scr0"][:, 256:512], A[:, 2560:2816], float(CAP), None, ALU.is_ge),
                reads=[("arA", "rt", "pos")], writes=[("scr0",)], tag=f"ov{e}")
            self.c("ve", (lambda v, e=e, A=A: v.tensor_scalar(
                t["scr1"][:, 0:256], A[:, 2560:2816], float(CAP - 1), float(e * CAP),
                ALU.min, ALU.add)),
                reads=[("arA", "rt", "pos")], writes=[("scr1",)], tag=f"tg{e}")
            self.c("ve", lambda v, A=A: v.tensor_tensor(
                t["scr1"][:, 256:512], A[:, 1536:1792], t["scr1"][:, 0:256], ALU.mult),
                reads=[("arA", "rt", "mask"), ("scr1",)], writes=[("scr1",)], tag=f"tm{e}")
            self.c("ve", lambda v, A=A: v.tensor_tensor(
                A[:, 2048:2304], A[:, 2048:2304], t["scr1"][:, 256:512], ALU.add),
                reads=[("arA", "rt", "tgt"), ("scr1",)], writes=[("arA", "rt", "tgt")], tag=f"ta{e}")
            self.c("ve", lambda v, A=A: v.tensor_tensor(
                t["scr0"][:, 512:768], A[:, 1536:1792], t["scr0"][:, 256:512], ALU.mult),
                reads=[("arA", "rt", "mask"), ("scr0",)], writes=[("scr0",)], tag=f"om{e}")
            self.c("ve", lambda v, A=A: v.tensor_tensor(
                A[:, 2304:2560], A[:, 2304:2560], t["scr0"][:, 512:768], ALU.add),
                reads=[("arA", "rt", "ovr"), ("scr0",)], writes=[("arA", "rt", "ovr")], tag=f"oa{e}")
        self.c("ve", lambda v, A=A: v.tensor_tensor(
            t["scr0"][:, 0:256], A[:, 3072:3328], A[:, 2048:2304], ALU.subtract),
            reads=[("arA", "rt", "junk"), ("arA", "rt", "tgt")], writes=[("scr0",)], tag="j0")
        self.c("ve", lambda v, A=A: v.tensor_tensor(
            t["scr0"][:, 256:512], A[:, 2304:2560], t["scr0"][:, 0:256], ALU.mult),
            reads=[("arA", "rt", "ovr"), ("scr0",)], writes=[("scr0",)], tag="j1")
        self.c("ve", lambda v, A=A: v.tensor_tensor(
            A[:, 2048:2304], A[:, 2048:2304], t["scr0"][:, 256:512], ALU.add),
            reads=[("arA", "rt", "tgt"), ("scr0",)], writes=[("arA", "rt", "tgt")], tag="j2")
        payloads = [("gv", 2816), ("wts", 768), ("tgt", 2048)]
        dsts = {"gv": 3970, "wts": 4002, "tgt": 4034}
        for nm, off in payloads:
            for ch in range(2):
                self.mm0((lambda te, off=off, ch=ch, A=A: te.transpose(
                    self.t["ps_f0"][:], A[:, off + 128 * ch:off + 128 * (ch + 1)],
                    t["identf"][:])),
                    reads=[("arA", "rt", nm)], writes=[("ps_f0",)], tag=f"cx{nm}{ch}")
                self.c("ve", (lambda v, ch=ch, A=A: v.tensor_copy(
                    out=A[:, 3586 + 128 * ch:3586 + 128 * (ch + 1)], in_=self.t["ps_f0"][:])),
                    reads=[("ps_f0",)], writes=[("arA", "rt", "xT", ch)], tag=f"cxe{nm}{ch}")
            for ch in range(2):
                fn = (lambda te, ch=ch, A=A: te.matmul(
                    self.t["ps_f0"][0:32, :],
                    t["eselsb"][:, 32 * ch:32 * (ch + 1)],
                    A[:, 3586 + 128 * ch:3586 + 128 * (ch + 1)],
                    start=(ch == 0), stop=(ch == 1)))
                if ch == 0:
                    self.mm0(fn, reads=[("eselsb",), ("arA", "rt", "xT", 0),
                                        ("arA", "rt", "xT", 1)],
                             writes=[("ps_f0",)], tag=f"cs{nm}")
                else:
                    self.mmn(fn, reads=[("eselsb",), ("arA", "rt", "xT", 0),
                                        ("arA", "rt", "xT", 1)], keys=[("ps_f0",)])
            self.c("ve", (lambda v, A=A: v.tensor_copy(
                out=A[0:32, 3842:3970], in_=self.t["ps_f0"][0:32, :])),
                reads=[("ps_f0",)], writes=[("arA", "rt", "gT")], tag=f"cge{nm}")
            self.mm0((lambda te, A=A: te.transpose(
                self.t["ps_f0"][:, 0:32], A[0:32, 3842:3970], t["identf"][0:32, 0:32])),
                reads=[("arA", "rt", "gT")], writes=[("ps_f0",)], tag=f"cgt{nm}")
            self.c("ve", (lambda v, nm=nm, A=A, dsts=dsts: v.tensor_copy(
                out=A[:, dsts[nm]:dsts[nm] + 32], in_=self.t["ps_f0"][:, 0:32])),
                reads=[("ps_f0",)], writes=[("arA", "rt", "c", nm)], tag=f"cf{nm}")
        self.c("ve", lambda v, A=A: v.tensor_copy(out=t["ti32"][:, 0:32],
                                                  in_=A[:, 4034:4066]),
               reads=[("arA", "rt", "c", "tgt")], writes=[("ti32",)], tag="ti32")
        gl = self.vw_glwv().rearrange("p (f b) -> p f b", b=2)
        self.c("ve", lambda v, A=A, gl=gl: v.tensor_copy(
            out=gl[:, 0:32, 0:1], in_=A[:, 3970:4002].rearrange("p (f o) -> p f o", o=1)),
            reads=[("arA", "rt", "c", "gv")], writes=[("arA", "rt", "glwv")], tag="glv0")
        self.c("ve", lambda v, A=A, gl=gl: v.tensor_copy(
            out=gl[:, 0:32, 1:2], in_=A[:, 4002:4034].rearrange("p (f o) -> p f o", o=1)),
            reads=[("arA", "rt", "wts"), ("arA", "rt", "c", "wts")],
            writes=[("arA", "rt", "glwv")], tag="glv1")
        mv32 = self.vw_mvt32()
        self.c("ve", lambda v, mv32=mv32: v.memset(mv32[:, 0:1026], 0.0),
               reads=[("mvt",)], writes=[("mvt", "sf")], tag="sf0")
        self.dma("gp", lambda g, mv32=mv32: g.dma_start(
            out=dr["glw_in"].rearrange("(p a) b -> p (a b)", p=128), in_=mv32[:, 0:1026]),
            reads=[("mvt", "sf")], writes=[("dram:glw_in",)], tag="glwz")
        prev = ("dram:glw_in",)
        for f in range(32):
            k = ("dram:glw_in", "s", f)
            self.dma("gp", (lambda g, f=f, gl=gl: g.indirect_dma_start(
                out=dr["glw_in"][:, :],
                out_offset=IndirectOffsetOnAxis(t["ti32"][:, f:f + 1], axis=0),
                in_=gl[:, f, :], in_offset=None)),
                reads=[("ti32",), ("arA", "rt", "glwv"), prev],
                writes=[k], tag=f"gsc{f}")
            prev = k
        self.coll(lambda g: g.collective_compute(
            "AllReduce", ALU.add, replica_groups=[list(range(NCORE))],
            ins=[dr["glw_in"][:]], outs=[dr["glw"][:]]),
            reads=[prev], writes=[("dram:glw",)], tag="ar_glw")

    def ph_shared_gu(self):
        t = self.t
        dr = self.dr
        mv32 = self.vw_mvt32()
        xrT = self.vw_xrTB()
        extrab = self.claim("arB")
        self.dma("sy", lambda sy: sy.dma_start(
            out=self.abf("arB")[:, 0:32768], in_=dr["xrT_sp"][:, :]),
            reads=[("dram:xrT_sp",)], writes=[("arB", "xrT"), ("tok_sh",)] + extrab,
            tag="ldxrT")
        slab = self.abf("arA")[:, 0:32768].rearrange("p (m hl j c) -> p m hl j c",
                                                     m=2, hl=2, j=16)
        extra = self.claim("arA", exclude=("rt",))
        for ic in range(2):
            for mi_, wsrc in ((0, self.shg_in), (1, self.shu_in)):
                for jh in range(2):
                    self.dma("sy", (lambda sy, ic=ic, wsrc=wsrc, jh=jh, mv32=mv32:
                                    sy.dma_start(
                                        out=mv32[:, 0:4096].rearrange(
                                            "p (j f) -> p j f", j=8),
                                        in_=wsrc[1024 * jh:1024 * (jh + 1),
                                                 512 * ic:512 * (ic + 1)]
                                        .rearrange("(j p) f -> p j f", p=128))),
                             reads=[("tok_sh",)],
                             writes=[("mvt",), ("mvt", "sf"), ("mvt", "z"), ("mvt", "dn")] +
                                    (extra if (ic == 0 and mi_ == 0 and jh == 0) else []),
                             tag=f"shw{ic}{mi_}{jh}")
                    hi = slab[:, mi_, 0, 8 * jh:8 * jh + 8, :].rearrange("p j c -> p (j c)")
                    lo = slab[:, mi_, 1, 8 * jh:8 * jh + 8, :].rearrange("p j c -> p (j c)")
                    self.c("ve", (lambda v, hi=hi, mv32=mv32: v.tensor_copy(
                        out=hi, in_=mv32[:, 0:4096])),
                        reads=[("mvt",)], writes=[("arA", "sl", mi_, 0, jh)], tag="shh")
                    self.c("ve", (lambda v, hi=hi, lo=lo, mv32=mv32: v.tensor_tensor(
                        lo, mv32[:, 0:4096], hi, ALU.subtract)),
                        reads=[("mvt",), ("arA", "sl", mi_, 0, jh)],
                        writes=[("arA", "sl", mi_, 1, jh)], tag="shl")
            slkeys = [("arA", "sl", m_, hl_, jh_) for m_ in range(2) for hl_ in range(2)
                      for jh_ in range(2)]
            xkeys = [("arB", "xrT")]
            for b in range(NB):
                for mi_, psn in ((0, "ps_a"), (1, "ps_b")):
                    first = True
                    for j in range(16):
                        for hx, hw in ((0, 0), (0, 1), (1, 0)):
                            fn = (lambda te, b=b, j=j, hx=hx, hw=hw, mi_=mi_, psn=psn,
                                  xrT=xrT, slab=slab, first=first:
                                  te.matmul(self.t[psn][:],
                                            xrT[:, hx, j, 128 * b:128 * (b + 1)],
                                            slab[:, mi_, hw, j, :],
                                            start=first, stop=(j == 15 and hx == 1)))
                            if first:
                                self.mm0(fn, reads=xkeys + slkeys, writes=[(psn,)],
                                         tag=f"sg{ic}{b}{mi_}")
                                first = False
                            else:
                                self.mmn(fn, reads=xkeys + slkeys, keys=[(psn,)])
                self.c("sc", lambda sc: sc.activation(t["scr0"][:, 0:512],
                                                      self.t["ps_a"][:], AF.Silu),
                       reads=[("ps_a",)], writes=[("scr0",)], tag=f"sil{ic}{b}")
                self.c("ve", lambda v: v.tensor_tensor(
                    t["scr1"][:, 0:512], t["scr0"][:, 0:512], self.t["ps_b"][:], ALU.mult),
                    reads=[("scr0",), ("ps_b",)], writes=[("scr1",)], tag=f"au{ic}{b}")
                self.dma("sy", (lambda sy, ic=ic, b=b: sy.dma_start(
                    out=dr["act_sp"][128 * b:128 * (b + 1), 512 * ic:512 * (ic + 1)],
                    in_=t["scr1"][:, 0:512])),
                    reads=[("scr1",)], writes=[("dram:act_sp", b, ic)], tag=f"acst{ic}{b}")

    def ph_experts(self):
        t = self.t
        dr = self.dr
        self._last_rs = None
        mv32 = self.vw_mvt32()
        bufv = self.abf("arB")[:, 0:32768].rearrange("p (sb c) -> p sb c", sb=8)
        bufT = self.abf("arA")[:, 0:32768].rearrange("p (hl hc s) -> p hl hc s", hl=2, hc=16)
        sl13 = self.abf("arA")[:, 32768:49152].rearrange("p (hl j c) -> p hl j c", hl=2, j=16)
        actT = self.abf("arA")[:, 0:16384].rearrange("p (hl ic s) -> p hl ic s", hl=2, ic=8)
        sl2 = self.abf("arA")[:, 16384:24576].rearrange("p (hl i c) -> p hl i c", hl=2, i=8)
        guv = self.af32("arB").rearrange("p (sb c) -> p sb c", sb=8)
        scrhb = t["scrh"].ap().bitcast(BF16)
        for el in range(EPC):
            for sup in range(2):
                eid = (el, sup)
                for sb in range(8):
                    col = el * 16 + sup * 8 + sb
                    self.dma("gp", (lambda g, col=col, sb=sb: g.indirect_dma_start(
                        out=t["gwt"][:, 2 * sb:2 * sb + 2], out_offset=None,
                        in_=dr["glw"][:, :],
                        in_offset=IndirectOffsetOnAxis(t["exix"][:, col:col + 1], axis=0))),
                        reads=[("dram:glw",), ("exix",)],
                        writes=[("gwt", sb)], tag=f"gl{el}{sup}{sb}")
                gw2 = t["gwt"].ap().rearrange("p (s b) -> p s b", s=8)
                self.c("ve", lambda v, gw2=gw2: v.tensor_copy(
                    out=t["gidx"].ap().rearrange("p (s o) -> p s o", o=1), in_=gw2[:, :, 0:1]),
                    reads=[("gwt", sb_) for sb_ in range(8)], writes=[("gidx",)],
                    tag=f"gi{el}{sup}")
                self.c("ve", lambda v, gw2=gw2: v.tensor_copy(
                    out=t["wloc"].ap().rearrange("p (s o) -> p s o", o=1), in_=gw2[:, :, 1:2]),
                    reads=[("gwt", sb_) for sb_ in range(8)], writes=[("wloc",)],
                    tag=f"gww{el}{sup}")
                exb = self.claim("arB") if True else []
                for sb in range(8):
                    self.dma("gp", (lambda g, sb=sb, bufv=bufv: g.indirect_dma_start(
                        out=bufv[:, sb, :], out_offset=None, in_=dr["xrtab"][:, :],
                        in_offset=IndirectOffsetOnAxis(t["gidx"][:, sb:sb + 1], axis=0))),
                        reads=[("dram:xrtab",), ("gidx",)],
                        writes=[("arB", "buf", eid, sb)] + (exb if sb == 0 else []),
                        tag=f"gx{el}{sup}{sb}")
                exa = self.claim("arA")
                for sb in range(8):
                    for hl in range(2):
                        for hc in range(16):
                            ps = "ps_t0" if (hc % 2 == 0) else "ps_t1"
                            src = bufv[:, sb, hl * 2048 + hc * 128:hl * 2048 + (hc + 1) * 128]
                            self.mm0((lambda te, src=src, ps=ps: te.transpose(
                                self.t[ps][:], src, t["ident"][:])),
                                reads=[("arB", "buf", eid, sb), ("ident",)],
                                writes=[(ps,)], tag="bT")
                            self.c("ve", (lambda v, sb=sb, hl=hl, hc=hc, ps=ps, bufT=bufT:
                                          v.tensor_copy(
                                              out=bufT[:, hl, hc, 128 * sb:128 * (sb + 1)],
                                              in_=self.t[ps][:])),
                                reads=[(ps,)],
                                writes=[("arA", "bufT", eid, hl, hc, sb)] +
                                       (exa if (sb == 0 and hl == 0 and hc == 0) else []),
                                tag="bTe")
                btk = [("arA", "bufT", eid, hl_, hc_, sb_) for hl_ in range(2)
                       for hc_ in range(16) for sb_ in range(8)]
                exg = self.claim("arB", exclude=("buf",))
                for fcol in range(4):
                    for jh in range(2):
                        self.dma("sy", (lambda sy, el=el, fcol=fcol, jh=jh, mv32=mv32:
                                        sy.dma_start(
                                            out=mv32[:, 0:4096].rearrange(
                                                "p (j f) -> p j f", j=8),
                                            in_=self.w13_in[el, 1024 * jh:1024 * (jh + 1),
                                                            512 * fcol:512 * (fcol + 1)]
                                            .rearrange("(j p) f -> p j f", p=128))),
                                 reads=[("gidx",)],
                                 writes=[("mvt",), ("mvt", "sf"), ("mvt", "z"), ("mvt", "dn")],
                                 tag=f"w13{el}{sup}{fcol}{jh}")
                        hi = sl13[:, 0, 8 * jh:8 * jh + 8, :].rearrange("p j c -> p (j c)")
                        lo = sl13[:, 1, 8 * jh:8 * jh + 8, :].rearrange("p j c -> p (j c)")
                        self.c("ve", (lambda v, hi=hi, mv32=mv32: v.tensor_copy(
                            out=hi, in_=mv32[:, 0:4096])),
                            reads=[("mvt",)], writes=[("arA", "s13", 0, jh)], tag="w13h")
                        self.c("ve", (lambda v, hi=hi, lo=lo, mv32=mv32: v.tensor_tensor(
                            lo, mv32[:, 0:4096], hi, ALU.subtract)),
                            reads=[("mvt",), ("arA", "s13", 0, jh)],
                            writes=[("arA", "s13", 1, jh)], tag="w13l")
                    slk = [("arA", "s13", hl_, jh_) for hl_ in range(2) for jh_ in range(2)]
                    for sb in range(8):
                        psn = "ps_a" if sb % 2 == 0 else "ps_b"
                        first = True
                        for hc in range(16):
                            for hb, hw in ((0, 0), (0, 1), (1, 0)):
                                fn = (lambda te, sb=sb, hc=hc, hb=hb, hw=hw, psn=psn,
                                      bufT=bufT, sl13=sl13, first=first:
                                      te.matmul(self.t[psn][:],
                                                bufT[:, hb, hc, 128 * sb:128 * (sb + 1)],
                                                sl13[:, hw, hc, :],
                                                start=first, stop=(hc == 15 and hb == 1)))
                                if first:
                                    self.mm0(fn, reads=btk + slk, writes=[(psn,)],
                                             tag=f"g13{el}{sup}{fcol}{sb}")
                                    first = False
                                else:
                                    self.mmn(fn, reads=btk + slk, keys=[(psn,)])
                        self.c("ve", (lambda v, sb=sb, fcol=fcol, psn=psn, guv=guv:
                                      v.tensor_copy(
                                          out=guv[:, sb, 512 * fcol:512 * (fcol + 1)],
                                          in_=self.t[psn][:])),
                            reads=[(psn,)],
                            writes=[("arB", "gu", eid, sb, fcol)] +
                                   (exg if (fcol == 0 and sb == 0) else []),
                            tag=f"gue{el}{sup}{fcol}{sb}")
                for sb in range(8):
                    for half in range(2):
                        self.c("sc", (lambda sc, sb=sb, half=half, guv=guv: sc.activation(
                            t["scr0"][:, 512 * half:512 * (half + 1)],
                            guv[:, sb, 512 * half:512 * (half + 1)], AF.Silu)),
                            reads=[("arB", "gu", eid, sb, half)], writes=[("scr0",)],
                            tag=f"si{el}{sup}{sb}{half}")
                        self.c("ve", (lambda v, sb=sb, half=half, guv=guv: v.tensor_tensor(
                            guv[:, sb, 512 * half:512 * (half + 1)],
                            t["scr0"][:, 512 * half:512 * (half + 1)],
                            guv[:, sb, 1024 + 512 * half:1024 + 512 * (half + 1)], ALU.mult)),
                            reads=[("scr0",), ("arB", "gu", eid, sb, half),
                                   ("arB", "gu", eid, sb, 2 + half)],
                            writes=[("arB", "act", eid, sb, half)], tag=f"am{el}{sup}{sb}{half}")
                exa2 = self.claim("arA", exclude=("s13",))
                for sb in range(8):
                    ak = [("arB", "act", eid, sb, 0), ("arB", "act", eid, sb, 1)]
                    self.c("sc", (lambda sc, sb=sb, guv=guv: sc.activation(
                        t["scrb"][:, 0:1024], guv[:, sb, 0:1024], AF.Identity)),
                        reads=ak, writes=[("scrb",)], tag=f"ash{el}{sup}{sb}")
                    self.c("ve", (lambda v, sb=sb, guv=guv, scrhb=scrhb: v.tensor_tensor(
                        scrhb[:, 0:1024], guv[:, sb, 0:1024], t["scrb"][:, 0:1024],
                        ALU.subtract)),
                        reads=ak + [("scrb",)], writes=[("scrh",)], tag=f"asl{el}{sup}{sb}")
                    for hl, srck in ((0, "scrb"), (1, "scrh")):
                        src = t["scrb"].ap() if hl == 0 else scrhb
                        for icn in range(8):
                            ps = "ps_t0" if icn % 2 == 0 else "ps_t1"
                            self.mm0((lambda te, src=src, icn=icn, ps=ps: te.transpose(
                                self.t[ps][:], src[:, 128 * icn:128 * (icn + 1)],
                                t["ident"][:])),
                                reads=[(srck,), ("ident",)], writes=[(ps,)], tag="aT")
                            self.c("ve", (lambda v, sb=sb, hl=hl, icn=icn, ps=ps, actT=actT:
                                          v.tensor_copy(
                                              out=actT[:, hl, icn, 128 * sb:128 * (sb + 1)],
                                              in_=self.t[ps][:])),
                                reads=[(ps,)],
                                writes=[("arA", "actT", eid, hl, icn, sb)] +
                                       (exa2 if (sb == 0 and hl == 0 and icn == 0) else []),
                                tag="aTe")
                atk = [("arA", "actT", eid, hl_, ic_, sb_) for hl_ in range(2)
                       for ic_ in range(8) for sb_ in range(8)]
                exe = self.claim("arB", exclude=("buf",))
                eov = self.af32("arB").rearrange("p (sb c) -> p sb c", sb=8)
                for hcol in range(4):
                    self.dma("sy", (lambda sy, el=el, hcol=hcol, mv32=mv32: sy.dma_start(
                        out=mv32[:, 0:4096].rearrange("p (i h) -> p i h", i=8),
                        in_=self.w2_in[el, :, 512 * hcol:512 * (hcol + 1)]
                        .rearrange("(i p) h -> p i h", p=128))),
                        reads=[("gidx",)],
                        writes=[("mvt",), ("mvt", "sf"), ("mvt", "z"), ("mvt", "dn")],
                        tag=f"w2{el}{sup}{hcol}")
                    hi2 = sl2[:, 0, :, :].rearrange("p i c -> p (i c)")
                    lo2 = sl2[:, 1, :, :].rearrange("p i c -> p (i c)")
                    self.c("ve", (lambda v, hi2=hi2, mv32=mv32: v.tensor_copy(
                        out=hi2, in_=mv32[:, 0:4096])),
                        reads=[("mvt",)], writes=[("arA", "s2", 0)], tag="w2h")
                    self.c("ve", (lambda v, hi2=hi2, lo2=lo2, mv32=mv32: v.tensor_tensor(
                        lo2, mv32[:, 0:4096], hi2, ALU.subtract)),
                        reads=[("mvt",), ("arA", "s2", 0)], writes=[("arA", "s2", 1)],
                        tag="w2l")
                    slk2 = [("arA", "s2", 0), ("arA", "s2", 1)]
                    for sb in range(8):
                        psn = "ps_a" if sb % 2 == 0 else "ps_b"
                        first = True
                        for icn in range(8):
                            for ha, hw in ((0, 0), (0, 1), (1, 0)):
                                fn = (lambda te, sb=sb, icn=icn, ha=ha, hw=hw, psn=psn,
                                      actT=actT, sl2=sl2, first=first:
                                      te.matmul(self.t[psn][:],
                                                actT[:, ha, icn, 128 * sb:128 * (sb + 1)],
                                                sl2[:, hw, icn, :],
                                                start=first, stop=(icn == 7 and ha == 1)))
                                if first:
                                    self.mm0(fn, reads=atk + slk2, writes=[(psn,)],
                                             tag=f"e2{el}{sup}{hcol}{sb}")
                                    first = False
                                else:
                                    self.mmn(fn, reads=atk + slk2, keys=[(psn,)])
                        self.c("ve", (lambda v, sb=sb, hcol=hcol, psn=psn, eov=eov:
                                      v.tensor_scalar(
                                          eov[:, sb, 512 * hcol:512 * (hcol + 1)],
                                          self.t[psn][:], t["wloc"][:, sb:sb + 1], None,
                                          ALU.mult)),
                            reads=[(psn,), ("wloc",)],
                            writes=[("arB", "eo", eid, sb, hcol)] +
                                   (exe if (hcol == 0 and sb == 0) else []),
                            tag=f"eoe{el}{sup}{hcol}{sb}")
                for sb in range(8):
                    k = ("dram:rs_in", eid, sb)
                    self.dma("gp", (lambda g, sb=sb, eov=eov: g.indirect_dma_start(
                        out=dr["rs_in"][:, :],
                        out_offset=IndirectOffsetOnAxis(t["gidx"][:, sb:sb + 1], axis=0),
                        in_=eov[:, sb, :], in_offset=None, compute_op=ALU.add)),
                        reads=[("gidx",)] + [("arB", "eo", eid, sb, hc_) for hc_ in range(4)] +
                              [("dram:rs_in", "z", r_) for r_ in range(65)] +
                              ([self._last_rs] if self._last_rs else []),
                        writes=[k], tag=f"rsc{el}{sup}{sb}")
                    self._last_rs = k

    def ph_return(self):
        t = self.t
        dr = self.dr
        mv32 = self.vw_mvt32()
        scrhb = t["scrh"].ap().bitcast(BF16)
        self.coll(lambda g: g.collective_compute(
            "ReduceScatter", ALU.add, replica_groups=[list(range(NCORE))],
            ins=[dr["rs_in"][:]], outs=[dr["rs_out"][:]]),
            reads=[("dram:rs_in", (el_, sup_), sb_) for el_ in range(EPC)
                   for sup_ in range(2) for sb_ in range(8)] +
                  [("dram:rs_in", "z", r_) for r_ in range(65)],
            writes=[("dram:rs_out",)], tag="rs")
        actT = self.abf("arB")[:, 0:16384].rearrange("p (hl ic s) -> p hl ic s", hl=2, ic=8)
        sld = self.abf("arB")[:, 16384:24576].rearrange("p (hl i c) -> p hl i c", hl=2, i=8)
        extrab = self.claim("arB")
        for b in range(NB):
            self.dma("sy", (lambda sy, b=b, mv32=mv32: sy.dma_start(
                out=mv32[:, 0:1024], in_=dr["act_sp"][128 * b:128 * (b + 1), :])),
                reads=[("dram:act_sp", b, 0), ("dram:act_sp", b, 1)],
                writes=[("mvt",), ("mvt", "sf"), ("mvt", "z"), ("mvt", "dn")] +
                       (extrab if b == 0 else []),
                tag=f"lda{b}")
            self.c("sc", (lambda sc, mv32=mv32: sc.activation(
                t["scrb"][:, 0:1024], mv32[:, 0:1024], AF.Identity)),
                reads=[("mvt",)], writes=[("scrb",)], tag=f"dsh{b}")
            self.c("ve", (lambda v, mv32=mv32, scrhb=scrhb: v.tensor_tensor(
                scrhb[:, 0:1024], mv32[:, 0:1024], t["scrb"][:, 0:1024], ALU.subtract)),
                reads=[("mvt",), ("scrb",)], writes=[("scrh",)], tag=f"dsl{b}")
            for hl, srck in ((0, "scrb"), (1, "scrh")):
                src = t["scrb"].ap() if hl == 0 else scrhb
                for icn in range(8):
                    ps = "ps_t0" if icn % 2 == 0 else "ps_t1"
                    self.mm0((lambda te, src=src, icn=icn, ps=ps: te.transpose(
                        self.t[ps][:], src[:, 128 * icn:128 * (icn + 1)], t["ident"][:])),
                        reads=[(srck,), ("ident",)], writes=[(ps,)], tag="dT")
                    self.c("ve", (lambda v, b=b, hl=hl, icn=icn, ps=ps, actT=actT:
                                  v.tensor_copy(
                                      out=actT[:, hl, icn, 128 * b:128 * (b + 1)],
                                      in_=self.t[ps][:])),
                        reads=[(ps,)], writes=[("arB", "dactT", hl, icn, b)], tag="dTe")
        atk = [("arB", "dactT", hl_, ic_, b_) for hl_ in range(2) for ic_ in range(8)
               for b_ in range(NB)]
        for hcol in range(4):
            self.dma("sy", (lambda sy, hcol=hcol, mv32=mv32: sy.dma_start(
                out=mv32[:, 0:4096].rearrange("p (i h) -> p i h", i=8),
                in_=self.shd_in[:, 512 * hcol:512 * (hcol + 1)]
                .rearrange("(i p) h -> p i h", p=128))),
                reads=atk[:1],
                writes=[("mvt",), ("mvt", "sf"), ("mvt", "z"), ("mvt", "dn")],
                tag=f"wd{hcol}")
            hi2 = sld[:, 0, :, :].rearrange("p i c -> p (i c)")
            lo2 = sld[:, 1, :, :].rearrange("p i c -> p (i c)")
            self.c("ve", (lambda v, hi2=hi2, mv32=mv32: v.tensor_copy(
                out=hi2, in_=mv32[:, 0:4096])),
                reads=[("mvt",)], writes=[("arB", "sd", 0)], tag="wdh")
            self.c("ve", (lambda v, hi2=hi2, lo2=lo2, mv32=mv32: v.tensor_tensor(
                lo2, mv32[:, 0:4096], hi2, ALU.subtract)),
                reads=[("mvt",), ("arB", "sd", 0)], writes=[("arB", "sd", 1)], tag="wdl")
            for b in range(NB):
                psn = "ps_a" if b % 2 == 0 else "ps_b"
                first = True
                for icn in range(8):
                    for ha, hw in ((0, 0), (0, 1), (1, 0)):
                        fn = (lambda te, b=b, icn=icn, ha=ha, hw=hw, psn=psn,
                              actT=actT, sld=sld, first=first:
                              te.matmul(self.t[psn][:],
                                        actT[:, ha, icn, 128 * b:128 * (b + 1)],
                                        sld[:, hw, icn, :],
                                        start=first, stop=(icn == 7 and ha == 1)))
                        if first:
                            self.mm0(fn, reads=atk + [("arB", "sd", 0), ("arB", "sd", 1)],
                                     writes=[(psn,)], tag=f"dn{hcol}{b}")
                            first = False
                        else:
                            self.mmn(fn, reads=atk + [("arB", "sd", 0), ("arB", "sd", 1)],
                                     keys=[(psn,)])
                self.c("ve", (lambda v, psn=psn: v.tensor_copy(
                    out=t["scr0"][:, 0:512], in_=self.t[psn][:])),
                    reads=[(psn,)], writes=[("scr0",)], tag=f"dne{hcol}{b}")
                self.dma("sy", (lambda sy, b=b, hcol=hcol: sy.dma_start(
                    out=dr["shared_sp"][128 * b:128 * (b + 1), 512 * hcol:512 * (hcol + 1)],
                    in_=t["scr0"][:, 0:512])),
                    reads=[("scr0",)], writes=[("dram:shared_sp", b, hcol)],
                    tag=f"dns{hcol}{b}")
        for b in range(NB):
            self.dma("sy", (lambda sy, b=b, mv32=mv32: sy.dma_start(
                out=mv32[:, 0:2048], in_=dr["oaspill"][128 * b:128 * (b + 1), :])),
                reads=[("dram:oaspill", b)],
                writes=[("mvt",), ("mvt", "sf"), ("mvt", "z"), ("mvt", "dn")],
                tag=f"f0{b}")
            self.dma("sy", (lambda sy, b=b: sy.dma_start(
                out=t["scr0"][:, :], in_=dr["shared_sp"][128 * b:128 * (b + 1), :])),
                reads=[("dram:shared_sp", b, hc_) for hc_ in range(4)],
                writes=[("scr0",)], tag=f"f1{b}")
            self.dma("sy", (lambda sy, b=b: sy.dma_start(
                out=t["scr1"][:, :], in_=dr["rs_out"][128 * b:128 * (b + 1), :])),
                reads=[("dram:rs_out",)], writes=[("scr1",)], tag=f"f2{b}")
            self.c("ve", (lambda v, mv32=mv32: v.tensor_tensor(
                t["scr0"][:, :], t["scr0"][:, :], mv32[:, 0:2048], ALU.add)),
                reads=[("scr0",), ("mvt",)], writes=[("scr0",)], tag=f"fa{b}")
            self.c("ve", lambda v: v.tensor_tensor(
                t["scr0"][:, :], t["scr0"][:, :], t["scr1"][:, :], ALU.add),
                reads=[("scr0",), ("scr1",)], writes=[("scr0",)], tag=f"fb{b}")
            self.dma("sy", (lambda sy, b=b: sy.dma_start(
                out=self.out[128 * b:128 * (b + 1), :], in_=t["scr0"][:, :])),
                reads=[("scr0",)], writes=[("dram:out", b)], tag=f"fout{b}")

    def ph_out_partial(self):
        dr = self.dr
        self.dma("sy", lambda sy: sy.dma_start(out=self.out[:, :], in_=dr["xspill"][:, :]),
                 reads=[("dram:xspill", b_) for b_ in range(NB)],
                 writes=[("dram:out",)], tag="st_out")


def make_layout_inputs(core):
    rank = core % 4
    blks = blocks_of(core)
    attidx = np.zeros((128, 288), np.int32)
    for pr in range(4):
        h = 4 * rank + pr
        for r in range(4):
            for hl in range(2):
                attidx[:, pr * 8 + r * 2 + hl] = (r * 4096 + hl * 2048 + h * 128 +
                                                  np.arange(128))
                for bi in range(8):
                    attidx[:, 32 + pr * 64 + r * 16 + hl * 8 + bi] = (
                        r * 32768 + hl * 16384 + h * 1024 + bi * 128 + np.arange(128))
    avidx = np.zeros((128, 32), np.int32)
    for r in range(4):
        for b in range(NB):
            avidx[:, r * 8 + b] = r * 4096 + blks[b] * 128 + np.arange(128)
    exidx = np.zeros((128, 64), np.int32)
    for el in range(EPC):
        for sup in range(2):
            for sb in range(8):
                exidx[:, el * 16 + sup * 8 + sb] = ((4 * core + el) * CAP + sup * 1024 +
                                                    sb * 128 + np.arange(128))
    esel = np.zeros((128, 64), np.float32)
    for i in range(32):
        f = 32 * core + i
        esel[f % 128, (f // 128) * 32 + i] = 1.0
    hsel = np.zeros((16, 4), np.float32)
    for pr in range(4):
        hsel[4 * rank + pr, pr] = 1.0
    rowbase = np.zeros((128, 1), np.float32)
    for p in range(128):
        s_ = p // 64
        bglob = (p % 64) // 2
        half = p % 2
        oc = 4 * s_ + BLK_OWNER[bglob]
        l = BLK_IDX[bglob] * 128 + half * 64
        rowbase[p, 0] = oc * ROWS + l
    return {"attidx": attidx, "avidx": avidx, "exidx": exidx, "hsel": hsel,
            "esel": esel, "rowbase": rowbase}


def shard_inputs(inputs):
    x = np.asarray(inputs["L0_attn_x"])
    ws = {m: np.asarray(inputs[f"L0_attn_w_{m}"]) for m in "qkvo"}
    gw = np.asarray(inputs["L0_ffn_gate_w"])
    nw = np.asarray(inputs["L0_ffn_norm_w"]).reshape(1, D)
    w13 = np.asarray(inputs["L0_ffn_w13"])
    w2 = np.asarray(inputs["L0_ffn_w2"])
    shg = np.asarray(inputs["L0_ffn_shared_gate_w"])
    shu = np.asarray(inputs["L0_ffn_shared_up_w"])
    shd = np.asarray(inputs["L0_ffn_shared_down_w"])
    maps = []
    for c in range(NCORE):
        seq = c // 4
        rows = np.concatenate([x[seq, 128 * b:128 * (b + 1), :] for b in blocks_of(c)], 0)
        maps.append({
            "x_in": np.ascontiguousarray(rows),
            **{f"w_{m}": np.ascontiguousarray(ws[m][:, 256 * c:256 * (c + 1)]) for m in "qkvo"},
            "gate_w": gw, "norm_w": nw,
            "w13": np.ascontiguousarray(w13[EPC * c:EPC * (c + 1)]),
            "w2": np.ascontiguousarray(w2[EPC * c:EPC * (c + 1)]),
            "sh_g": shg, "sh_u": shu, "sh_d": shd,
            **make_layout_inputs(c),
        })
    return maps


def gather_output(results):
    out = np.zeros((2, S, D), np.float32)
    for c in range(NCORE):
        seq = c // 4
        r = np.asarray(results[c]["out"])
        for i, b in enumerate(blocks_of(c)):
            out[seq, 128 * b:128 * (b + 1), :] = r[128 * i:128 * (i + 1), :]
    return out


def kernel(**inputs):
    import os
    os.environ.setdefault("NEURON_SCRATCHPAD_PAGE_SIZE", "512")
    mk = MK(upto=UPTO)
    nc = mk.emit()
    maps = shard_inputs(inputs)
    res = run_bass_kernel_spmd(nc, maps, core_ids=list(range(NCORE)))
    return gather_output(res.results)


# version 04
# speedup vs baseline: 1.0743x; 1.0743x over previous
import contextlib
import numpy as np
import ml_dtypes
import concourse.bass as bass
import concourse.mybir as mybir
from concourse.bass import IndirectOffsetOnAxis
from concourse.bass_utils import run_bass_kernel_spmd

F32 = mybir.dt.float32
BF16 = mybir.dt.bfloat16
FP16 = mybir.dt.float16
I32 = mybir.dt.int32
U32 = mybir.dt.uint32
AF = mybir.ActivationFunctionType
ALU = mybir.AluOpType
AXX = mybir.AxisListType.X

NCORE = 8
D = 2048
NB = 8
TPC = NB * 128
NH = 16
HD = 128
S = 4096
SBLK = 32
NE = 32
EPC = 4
CAP = 2048
NEG = -1e30
RC = 1536.0
EPS_LN = 1e-5
EPS_RMS = 1e-6
ROWS = 1040
GT = NCORE * ROWS
JUNK_XR = 1024.0
UPTO = "all"

M_IDX = {"q": 0, "k": 1, "v": 2, "o": 3}


def blocks_of(core):
    o = core % 4
    return sorted(list(range(4 * o, 4 * o + 4)) + list(range(28 - 4 * o, 32 - 4 * o)))


BLK_OWNER = [0] * SBLK
BLK_IDX = [0] * SBLK
for _o in range(4):
    for _i, _b in enumerate(blocks_of(_o)):
        BLK_OWNER[_b] = _o
        BLK_IDX[_b] = _i
COLOF = [BLK_OWNER[g] * 1024 + BLK_IDX[g] * 128 for g in range(SBLK)]

ENGS = ("te", "ve", "sc", "gp", "sy")


class Op:
    __slots__ = ("eng", "fn", "deps", "marked", "chainval", "kind", "tag", "idx",
                 "_acc_key")

    def __init__(self, eng, fn, kind, tag):
        self.eng = eng
        self.fn = fn
        self.deps = set()
        self.marked = False
        self.chainval = None
        self.kind = kind
        self.tag = tag
        self.idx = None
        self._acc_key = None


class Prog:

    def __init__(self):
        self.ops = []
        self.lw = {}
        self.readers = {}
        self.spaces = {}

    def add(self, eng, fn, reads=(), writes=(), kind="c", tag=""):
        op = Op(eng, fn, kind, tag)
        op.idx = len(self.ops)
        for k in list(reads) + list(writes):
            self.spaces.setdefault(k[0], set()).add(k)
        for k in reads:
            w = self.lw.get(k)
            if w is not None:
                op.deps.add(w)
            self.readers.setdefault(k, []).append(op)
        for k in writes:
            w = self.lw.get(k)
            if w is not None:
                op.deps.add(w)
            for r in self.readers.get(k, ()):
                op.deps.add(r)
            self.lw[k] = op
            self.readers[k] = []
        op.deps.discard(op)
        self.ops.append(op)
        return op

    def add_acc_first(self, eng, fn, reads=(), writes=(), tag=""):
        op = self.add(eng, fn, reads=reads, writes=writes, kind="c", tag=tag)
        for k in writes:
            op._acc_key = k
        return op

    def add_acc_next(self, eng, fn, reads=(), keys=(), tag=""):
        op = Op(eng, fn, "c", tag)
        op.idx = len(self.ops)
        for k in reads:
            w = self.lw.get(k)
            if w is not None:
                op.deps.add(w)
            self.readers.setdefault(k, []).append(op)
        for k in keys:
            self.lw[k] = op
            self.readers[k] = []
            op._acc_key = k
        op.deps.discard(op)
        self.ops.append(op)
        return op


class MK:
    def __init__(self, upto="all", taps=()):
        self.nc = bass.Bass("TRN2", target_bir_lowering=False)
        self.stack = contextlib.ExitStack()
        self.upto = upto
        self.taps = set(taps)
        self.dr = {}
        self.t = {}
        self.prog = Prog()

    def sbuf(self, name, shape, dtype):
        t = self.stack.enter_context(self.nc.sbuf_tensor(name, shape, dtype))
        self.t[name] = t
        return t

    def psumt(self, name, shape, dtype):
        t = self.stack.enter_context(self.nc.psum_tensor(name, shape, dtype))
        self.t[name] = t
        return t

    def dram(self, name, shape, dtype, kind="Internal"):
        ap = self.nc.dram_tensor(name, shape, dtype, kind=kind).ap()
        self.dr[name] = ap
        return ap

    def claim(self, *arenas, exclude=()):
        keys = []
        for a in arenas:
            sp = self.prog.spaces.get(a, set())
            take = {k for k in sp if not (len(k) > 1 and k[1] in exclude)}
            keys.extend(sorted(take, key=repr))
            self.prog.spaces[a] = sp - take
        return keys

    def c(self, eng, fn, reads=(), writes=(), tag=""):
        return self.prog.add(eng, fn, reads=reads, writes=writes, kind="c", tag=tag)

    def mm0(self, fn, reads=(), writes=(), tag=""):
        return self.prog.add_acc_first("te", fn, reads=reads, writes=writes, tag=tag)

    def mmn(self, fn, reads=(), keys=(), tag=""):
        return self.prog.add_acc_next("te", fn, reads=reads, keys=keys, tag=tag)

    def dma(self, eng, fn, reads=(), writes=(), tag=""):
        return self.prog.add(eng, fn, reads=reads, writes=writes, kind="d", tag=tag)

    def coll(self, fn, reads=(), writes=(), tag=""):
        return self.prog.add("gp", fn, reads=reads, writes=writes, kind="cc", tag=tag)

    NDMASEM = 8
    NCCSEM = 4

    def finalize(self):
        for op in self.prog.ops:
            for d in op.deps:
                d.marked = True
        cc_counts = {}
        dma_counts = {}
        for op in self.prog.ops:
            if op.kind == "c":
                if op.marked:
                    cc_counts[op.eng] = cc_counts.get(op.eng, 0) + 1
                    op.chainval = ("c", op.eng, cc_counts[op.eng])
            elif op.kind == "d":
                i = dma_counts.get(op.eng, 0)
                dma_counts[op.eng] = i + 1
                slot = i % self.NDMASEM
                rnd = i // self.NDMASEM
                op.chainval = ("d", op.eng, slot, 16 * (rnd + 1))
            elif op.kind == "cc":
                i = dma_counts.get("cc", 0)
                dma_counts["cc"] = i + 1
                slot = i % self.NCCSEM
                rnd = i // self.NCCSEM
                op.chainval = ("cc", "gp", slot, rnd + 1)
        sems = {}
        for e, n in cc_counts.items():
            if n:
                sems[("c", e)] = self.stack.enter_context(self.nc.semaphore(f"s_c_{e}"))
        for q, n in dma_counts.items():
            if q == "cc":
                for s in range(min(n, self.NCCSEM)):
                    sems[("cc", "gp", s)] = self.stack.enter_context(
                        self.nc.semaphore(f"s_cc_{s}"))
            else:
                for s in range(min(n, self.NDMASEM)):
                    sems[("d", q, s)] = self.stack.enter_context(
                        self.nc.semaphore(f"s_d_{q}_{s}"))
        self.chain_sems = sems
        finals = []
        for op in self.prog.ops:
            if op.kind in ("d", "cc") and op.chainval is not None:
                k, e, slot, val = op.chainval
                finals.append(((k, e, slot), val))
        fin = {}
        for key, val in finals:
            fin[key] = max(fin.get(key, 0), val)
        for e, n in cc_counts.items():
            fin[("c", e)] = n
        self.final_chain = fin

        nc = self.nc
        mkself = self

        def wait_key(op):
            cv = op.chainval
            if cv is None:
                return None
            if cv[0] == "c":
                return (("c", cv[1]), cv[2])
            return ((cv[0], cv[1], cv[2]), cv[3])

        with nc.Block() as block:
            for eng_name in ENGS:
                ops = [o for o in self.prog.ops if o.eng == eng_name]
                if not ops and eng_name != "sy":
                    continue

                def section(engobj, eng_name=eng_name, ops=ops):
                    watermark = {}
                    for op in ops:
                        if op.kind in ("d", "cc"):
                            cv = op.chainval
                            skey = (cv[0], cv[1], cv[2])
                            prev = cv[3] - (16 if cv[0] == "d" else 1)
                            if prev > 0 and watermark.get(skey, 0) < prev:
                                engobj.wait_ge(mkself.chain_sems[skey], prev)
                                watermark[skey] = prev
                        for dop in sorted(op.deps, key=lambda o: o.idx):
                            wk = wait_key(dop)
                            if wk is None:
                                continue
                            skey, val = wk
                            if watermark.get(skey, 0) >= val:
                                continue
                            engobj.wait_ge(mkself.chain_sems[skey], val)
                            watermark[skey] = val
                        instr = op.fn(engobj)
                        wk = wait_key(op)
                        if wk is not None:
                            assert instr is not None, f"marked op {op.tag} returned None"
                            instr.then_inc(mkself.chain_sems[wk[0]],
                                           16 if wk[0][0] == "d" else 1)
                    if eng_name == "sy":
                        for skey, v in mkself.final_chain.items():
                            if v and watermark.get(skey, 0) < v:
                                engobj.wait_ge(mkself.chain_sems[skey], v)

                getattr(block, {"te": "tensor", "ve": "vector", "sc": "scalar",
                                "gp": "gpsimd", "sy": "sync"}[eng_name])(section)
        return nc

    def emit(self):
        self.declare_io()
        self.alloc_tiles()
        self.ph_consts()
        self.ph_wquant()
        self.ph_lnquant()
        if self.upto not in ("m1", "m2"):
            self.ph_proj_kv()
        if self.upto in ("m4", "m5", "all"):
            self.ph_attn()
            self.ph_wo()
        if self.upto in ("m5", "all"):
            self.ph_ffn_pre()
            self.ph_routing()
        if self.upto == "all":
            self.ph_shared_gu()
            self.ph_experts()
            self.ph_return()
        else:
            self.ph_out_partial()
        with self.stack:
            return self.finalize()

    def declare_io(self):
        nc = self.nc
        d = self.dram
        self.x_in = nc.dram_tensor("x_in", [TPC, D], F32, kind="ExternalInput").ap()
        self.wcol = {m: nc.dram_tensor(f"w_{m}", [D, 256], F32, kind="ExternalInput").ap()
                     for m in "qkvo"}
        self.gate_in = nc.dram_tensor("gate_w", [NE, D], F32, kind="ExternalInput").ap()
        self.normw_in = nc.dram_tensor("norm_w", [1, D], F32, kind="ExternalInput").ap()
        self.w13_in = nc.dram_tensor("w13", [EPC, D, D], F32, kind="ExternalInput").ap()
        self.w2_in = nc.dram_tensor("w2", [EPC, 1024, D], F32, kind="ExternalInput").ap()
        self.shg_in = nc.dram_tensor("sh_g", [D, 1024], F32, kind="ExternalInput").ap()
        self.shu_in = nc.dram_tensor("sh_u", [D, 1024], F32, kind="ExternalInput").ap()
        self.shd_in = nc.dram_tensor("sh_d", [1024, D], F32, kind="ExternalInput").ap()
        self.rowbase_in = nc.dram_tensor("rowbase", [128, 1], F32, kind="ExternalInput").ap()
        self.attidx_in = nc.dram_tensor("attidx", [128, 288], I32, kind="ExternalInput").ap()
        self.avidx_in = nc.dram_tensor("avidx", [128, 32], I32, kind="ExternalInput").ap()
        self.exidx_in = nc.dram_tensor("exidx", [128, 64], I32, kind="ExternalInput").ap()
        self.hsel_in = nc.dram_tensor("hsel", [16, 4], F32, kind="ExternalInput").ap()
        self.esel_in = nc.dram_tensor("esel", [128, 64], F32, kind="ExternalInput").ap()
        self.out = nc.dram_tensor("out", [TPC, D], F32, kind="ExternalOutput").ap()
        d("ar_in", [1, 4], F32)
        d("ar_out", [1, 4], F32)
        for m in "qkvo":
            d(f"wt_{m}_in", [256, D], BF16)
            d(f"wt_{m}", [D, D], BF16)
        d("kT_in", [2 * NH * 128, 1024], BF16)
        d("kT_tab", [4 * 2 * NH * 128, 1024], BF16)
        d("qT_in", [2 * NH * 128, 1024], BF16)
        d("qT_tab", [4 * 2 * NH * 128, 1024], BF16)
        d("v_in", [2 * NH * 1024, 128], BF16)
        d("v_tab", [4 * 2 * NH * 1024, 128], BF16)
        d("nrm_in", [1, 32], F32)
        d("nrm_tab", [4, 32], F32)
        d("av_in", [S, 1024], BF16)
        d("av_tab", [4 * S, 1024], BF16)
        d("xr_in", [ROWS, 2 * D], BF16)
        d("xrtab", [GT, 2 * D], BF16)
        d("rt_in", [ROWS, 8], F32)
        d("rttab", [GT, 8], F32)
        d("glw_in", [NE * CAP + 128, 2], F32)
        d("glw", [NE * CAP + 128, 2], F32)
        d("rs_in", [GT, D], F32)
        d("rs_out", [ROWS, D], F32)
        d("xspill", [TPC, D], F32)
        d("oaspill", [TPC, D], F32)
        d("xrT_sp", [128, 2 * NH * 1024], BF16)
        d("act_sp", [TPC, 1024], F32)
        d("shared_sp", [TPC, D], F32)
        self.dbgt = {}
        for name, shape in [
            ("dbg_xqT", [16 * 128, TPC]), ("dbg_sxq", [128, NB]),
            ("dbg_x1", [TPC, D]), ("dbg_xr", [TPC, D]),
            ("dbg_rt", [TPC, 8]), ("dbg_av", [TPC, D]),
        ]:
            if name in self.taps:
                self.dbgt[name] = nc.dram_tensor(name, shape, F32, kind="ExternalOutput").ap()

    def alloc_tiles(self):
        s = self.sbuf
        s("ones128", [128, 128], BF16)
        s("onesf", [128, 128], F32)
        s("ltri", [128, 128], BF16)
        s("ident", [128, 128], BF16)
        s("identf", [128, 128], F32)
        s("zerob", [128, 128], BF16)
        s("epsln", [128, 1], F32)
        s("epsrms", [128, 1], F32)
        s("arA", [128, 24576], F32)
        s("arB", [128, 16384], F32)
        s("scr0", [128, D], F32)
        s("scr1", [128, D], F32)
        s("scrh", [128, D], FP16)
        s("scrb", [128, D], BF16)
        s("mvt", [128, 16, 512], BF16)
        s("qst", [128, 1024], BF16)
        s("ti32", [128, 256], I32)
        s("stats", [128, 32], F32)
        s("sxq", [128, NB], F32)
        s("dqrow", [128, NB, 4], F32)
        s("thr", [128, 4], F32)
        s("swg", [128, 4], F32)
        s("nrmloc", [128, 64], F32)
        s("mh", [128, 4], F32)
        s("gwt", [128, 16], F32)
        s("wloc", [128, 8], F32)
        s("gidx", [128, 8], I32)
        s("attix", [128, 288], I32)
        s("avix", [128, 32], I32)
        s("exix", [128, 64], I32)
        s("rowb", [128, 1], F32)
        s("hselsb", [16, 4], F32)
        s("eselsb", [128, 64], F32)
        self.psumt("ps_a", [128, 512], F32)
        self.psumt("ps_b", [128, 512], F32)
        self.psumt("ps_c", [128, 512], F32)
        self.psumt("ps_d", [128, 512], F32)
        self.psumt("ps_t0", [128, 128], BF16)
        self.psumt("ps_t1", [128, 128], BF16)
        self.psumt("ps_f0", [128, 128], F32)
        self.psumt("ps_r", [128, 32], F32)

    def abf(self, name):
        return self.t[name].ap().bitcast(BF16)

    def af32(self, name):
        return self.t[name].ap()

    def vw_mvt32(self):
        return self.t["mvt"].ap().rearrange("p a b -> p (a b)").bitcast(F32)

    def vw_glwv(self):
        return self.af32("arA")[:, 20992:21504]

    def vw_rtloc(self):
        return self.t["nrmloc"].ap()[:, 56:64]

    def vw_idsu(self):
        return self.t["nrmloc"].ap().bitcast(U32)[:, 44:52]

    def vw_maskt(self):
        return self.t["mvt"].ap().rearrange("p a b -> p (a b)")[:, 2048:4096]

    def vw_wslab(self, mi):
        return self.af32("arA")[:, 0:16384].rearrange("p (m j c) -> p m j c", m=4, j=16)[:, mi]

    def vw_pq32(self):
        return self.af32("arA")[:, 0:16384].rearrange("p (b d) -> p b d", b=NB)

    def vw_xres(self):
        return self.af32("arB").rearrange("p (b d) -> p b d", b=NB)

    def vw_wtern(self):
        return self.abf("arB")[:, 0:4096].rearrange("p (j c) -> p j c", j=16)

    def vw_wtsl(self):
        return self.abf("arB")[:, 4096:8192].rearrange("p (e o) -> p e o", e=2)

    def vw_xqT(self):
        return self.abf("arA")[:, 32768:49152].rearrange("p (j t) -> p j t", j=16)

    def vw_kTb(self):
        return self.t["scr1"].ap().bitcast(BF16)[:, 0:2048]

    def vw_pair(self, slot, which):
        base = slot * 24576 + {"q": 0, "k": 8192, "v": 16384}[which]
        return self.abf("arA")[:, base:base + 8192].rearrange("p (hl t) -> p hl t", hl=2)

    def vw_avloc(self):
        return self.abf("arA")[:, 0:32768].rearrange("p (b r hl c) -> p b r hl c",
                                                     b=NB, r=4, hl=2)

    def vw_oqT(self):
        return self.abf("arA")[:, 0:16384].rearrange("p (j t) -> p j t", j=16)

    def vw_xrT(self):
        return self.abf("arA")[:, 0:32768].rearrange("p (hl j t) -> p hl j t", hl=2, j=16)

    def vw_xrTB(self):
        return self.abf("arB")[:, 0:32768].rearrange("p (hl j t) -> p hl j t", hl=2, j=16)

    def ph_consts(self):
        t = self.t
        self.c("ve", lambda v: v.memset(t["ones128"][:], 1.0), writes=[("ones128",)])
        self.c("ve", lambda v: v.memset(t["onesf"][:], 1.0), writes=[("onesf",)])
        self.c("ve", lambda v: v.memset(t["zerob"][:], 0.0), writes=[("zerob",)])
        self.c("ve", lambda v: v.memset(t["epsln"][:], EPS_LN), writes=[("epsln",)])
        self.c("ve", lambda v: v.memset(t["epsrms"][:], EPS_RMS), writes=[("epsrms",)])
        self.c("gp", lambda g: g.affine_select(t["ident"][:], t["ones128"][:],
                                               pattern=[[1, 128]], compare_op=ALU.is_equal,
                                               fill=0.0, base=0, channel_multiplier=-1),
               reads=[("ones128",)], writes=[("ident",)])
        self.c("gp", lambda g: g.affine_select(t["ltri"][:], t["ones128"][:],
                                               pattern=[[1, 128]], compare_op=ALU.is_gt,
                                               fill=0.0, base=0, channel_multiplier=-1),
               reads=[("ones128",)], writes=[("ltri",)])
        self.c("ve", lambda v: v.tensor_copy(out=t["identf"][:], in_=t["ident"][:]),
               reads=[("ident",)], writes=[("identf",)])
        self.dma("sy", lambda sy: sy.dma_start(out=t["attix"][:], in_=self.attidx_in[:]),
                 writes=[("attix",)])
        self.dma("sy", lambda sy: sy.dma_start(out=t["avix"][:], in_=self.avidx_in[:]),
                 writes=[("avix",)])
        self.dma("sy", lambda sy: sy.dma_start(out=t["exix"][:], in_=self.exidx_in[:]),
                 writes=[("exix",)])
        self.dma("sy", lambda sy: sy.dma_start(out=t["rowb"][:], in_=self.rowbase_in[:]),
                 writes=[("rowb",)])
        self.dma("sy", lambda sy: sy.dma_start(out=t["hselsb"][:], in_=self.hsel_in[:]),
                 writes=[("hselsb",)])
        self.dma("sy", lambda sy: sy.dma_start(out=t["eselsb"][:], in_=self.esel_in[:]),
                 writes=[("eselsb",)])

    def ph_wquant(self):
        t = self.t
        dr = self.dr
        for m in "qkvo":
            mi = M_IDX[m]
            self.dma("sy", (lambda sy, mi=mi, m=m: sy.dma_start(
                out=self.vw_wslab(mi),
                in_=self.wcol[m].rearrange("(a p) b -> p a b", p=128))),
                writes=[("arA", "wsl", mi)], tag=f"ld_w{m}")
        for mi in range(4):
            for h in range(2):
                self.c("sc", (lambda sc, mi=mi, h=h: sc.activation(
                    t["scr1"][:, 0:D],
                    self.vw_wslab(mi).rearrange("p a b -> p (a b)")[:, D * h:D * (h + 1)],
                    AF.Abs, accum_out=t["stats"][:, 2 * mi + h:2 * mi + h + 1])),
                    reads=[("arA", "wsl", mi)], writes=[("scr1",), ("stats", "abs", mi, h)],
                    tag=f"abs{mi}{h}")
        self.c("ve", lambda v: v.tensor_tensor(
            t["stats"][:, 8:12], t["stats"][:, 0:8:2], t["stats"][:, 1:8:2], ALU.add),
            reads=[("stats", "abs", mi_, h_) for mi_ in range(4) for h_ in range(2)],
            writes=[("stats", "abss")], tag="abssum")
        self.mm0(lambda te: te.matmul(self.t["ps_r"][:, 0:4], t["onesf"][:],
                                      t["stats"][:, 8:12], start=True, stop=True),
                 reads=[("onesf",), ("stats", "abss")], writes=[("ps_r",)], tag="abs_mm")
        self.c("ve", lambda v: v.tensor_copy(out=t["stats"][:, 12:16], in_=self.t["ps_r"][:, 0:4]),
               reads=[("ps_r",)], writes=[("stats", "abst")])
        self.dma("gp", lambda g: g.dma_start(out=dr["ar_in"][:], in_=t["stats"][0:1, 12:16]),
                 reads=[("stats", "abst")], writes=[("dram:ar_in",)])
        self.coll(lambda g: g.collective_compute(
            "AllReduce", ALU.add, replica_groups=[list(range(NCORE))],
            ins=[dr["ar_in"][:]], outs=[dr["ar_out"][:]]),
            reads=[("dram:ar_in",)], writes=[("dram:ar_out",)], tag="ar_abs")
        self.dma("gp", lambda g: g.dma_start(out=t["stats"][0:1, 16:20], in_=dr["ar_out"][:]),
                 reads=[("dram:ar_out",)], writes=[("stats", "arout")])
        self.mm0(lambda te: te.matmul(self.t["ps_r"][:, 4:8], t["onesf"][0:1, :],
                                      t["stats"][0:1, 16:20], start=True, stop=True),
                 reads=[("onesf",), ("stats", "arout")], writes=[("ps_r", "bc")], tag="bc_mm")
        self.c("ve", lambda v: v.tensor_scalar(t["swg"][:], self.t["ps_r"][:, 4:8],
                                               1.0 / (D * D), 1e-5, ALU.mult, ALU.max),
               reads=[("ps_r", "bc")], writes=[("swg",)])
        self.c("ve", lambda v: v.tensor_scalar(t["thr"][:], t["swg"][:], 0.5, None, ALU.mult),
               reads=[("swg",)], writes=[("thr",)])
        for m in "kvqo":
            mi = M_IDX[m]
            wsl = self.vw_wslab(mi).rearrange("p a b -> p (a b)")
            wtf = self.vw_wtern().rearrange("p j c -> p (j c)")
            for h in range(2):
                self.c("ve", (lambda v, wsl=wsl, h=h, mi=mi: v.tensor_scalar(
                    t["scr0"][:, 0:D], wsl[:, D * h:D * (h + 1)],
                    t["thr"][:, mi:mi + 1], None, ALU.is_ge)),
                    reads=[("arA", "wsl", mi), ("thr",)], writes=[("scr0",)])
                self.c("ve", (lambda v, wsl=wsl, h=h, mi=mi: v.tensor_scalar(
                    t["scr1"][:, 0:D], wsl[:, D * h:D * (h + 1)], -1.0,
                    t["thr"][:, mi:mi + 1], ALU.mult, ALU.is_ge)),
                    reads=[("arA", "wsl", mi), ("thr",)], writes=[("scr1",)])
                self.c("ve", (lambda v, wtf=wtf, h=h: v.tensor_tensor(
                    wtf[:, D * h:D * (h + 1)], t["scr0"][:, 0:D], t["scr1"][:, 0:D],
                    ALU.subtract)),
                    reads=[("scr0",), ("scr1",)], writes=[("arB", "wtern", h)])
            for j in range(16):
                for eb in range(2):
                    ps = "ps_t0" if ((j * 2 + eb) % 2 == 0) else "ps_t1"
                    self.mm0((lambda te, j=j, eb=eb, ps=ps: te.transpose(
                        self.t[ps][:], self.vw_wtern()[:, j, 128 * eb:128 * (eb + 1)],
                        t["ident"][:])),
                        reads=[("arB", "wtern", eb), ("ident",)], writes=[(ps,)], tag="wtT")
                    self.c("ve", (lambda v, j=j, eb=eb, ps=ps: v.tensor_copy(
                        out=self.vw_wtsl()[:, eb, 128 * j:128 * (j + 1)], in_=self.t[ps][:])),
                        reads=[(ps,)], writes=[("arB", "wtsl", eb, j)])
            self.dma("gp", (lambda g, m=m: g.dma_start(
                out=dr[f"wt_{m}_in"].rearrange("(eb p) o -> p eb o", p=128),
                in_=self.vw_wtsl())),
                reads=[("arB", "wtsl", eb_, j_) for eb_ in range(2) for j_ in range(16)],
                writes=[(f"dram:wt_{m}_in",)])
            self.coll((lambda g, m=m: g.collective_compute(
                "AllGather", ALU.bypass, replica_groups=[list(range(NCORE))],
                ins=[dr[f"wt_{m}_in"][:]], outs=[dr[f"wt_{m}"][:]])),
                reads=[(f"dram:wt_{m}_in",)], writes=[(f"dram:wt_{m}",)], tag=f"ag_w{m}")

    def ph_lnquant(self):
        t = self.t
        xr = self.vw_xres()
        for b in range(NB):
            extra = self.claim("arB") if b == 0 else []
            self.dma("sy", (lambda sy, b=b, xr=xr: sy.dma_start(
                out=xr[:, b, :], in_=self.x_in[128 * b:128 * (b + 1), :])),
                writes=[("arB", "x", b)] + extra, tag=f"ld_x{b}")
        for b in range(NB):
            self.c("sc", (lambda sc, b=b, xr=xr: sc.activation(
                t["scr1"][:, 0:D], xr[:, b, :], AF.Identity,
                accum_out=t["stats"][:, b:b + 1])),
                reads=[("arB", "x", b)], writes=[("scr1",), ("stats", "m", b)], tag=f"mean{b}")
        for b in range(NB):
            self.c("ve", (lambda v, b=b: v.tensor_scalar(
                t["stats"][:, b:b + 1], t["stats"][:, b:b + 1], 1.0 / D, None, ALU.mult)),
                reads=[("stats", "m", b)], writes=[("stats", "m", b)])
            self.c("ve", (lambda v, b=b, xr=xr: v.tensor_scalar(
                t["scr0"][:, :], xr[:, b, :], t["stats"][:, b:b + 1], None, ALU.subtract)),
                reads=[("arB", "x", b), ("stats", "m", b)], writes=[("scr0",)], tag=f"d{b}")
            self.c("sc", (lambda sc, b=b: sc.activation(
                t["scr1"][:, 0:D], t["scr0"][:, :], AF.Square,
                accum_out=t["stats"][:, 8 + b:9 + b])),
                reads=[("scr0",)], writes=[("scr1",), ("stats", "ss", b)], tag=f"ssq{b}")
            self.c("sc", (lambda sc, b=b: sc.activation(
                t["stats"][:, 16 + b:17 + b], t["stats"][:, 8 + b:9 + b], AF.Sqrt,
                bias=t["epsln"][:], scale=1.0 / D)),
                reads=[("stats", "ss", b), ("epsln",)], writes=[("stats", "std", b)])
            self.c("ve", (lambda v, b=b: v.reciprocal(
                t["stats"][:, 24 + b:25 + b], t["stats"][:, 16 + b:17 + b])),
                reads=[("stats", "std", b)], writes=[("stats", "rstd", b)])
            self.c("ve", (lambda v, b=b: v.reduce_max(
                t["sxq"][:, b:b + 1], t["scr0"][:, :], axis=AXX, apply_absolute_value=True)),
                reads=[("scr0",)], writes=[("sxq", b)])
            self.c("ve", (lambda v, b=b: v.tensor_tensor(
                t["sxq"][:, b:b + 1], t["sxq"][:, b:b + 1], t["stats"][:, 24 + b:25 + b], ALU.mult)),
                reads=[("sxq", b), ("stats", "rstd", b)], writes=[("sxq", b)])
            self.c("ve", (lambda v, b=b: v.tensor_scalar(
                t["sxq"][:, b:b + 1], t["sxq"][:, b:b + 1], 1e-5, None, ALU.max)),
                reads=[("sxq", b)], writes=[("sxq", b)])
            self.c("ve", (lambda v, b=b: v.tensor_scalar(
                t["stats"][:, 16 + b:17 + b], t["sxq"][:, b:b + 1], 1.0 / 127.0, None, ALU.mult)),
                reads=[("sxq", b)], writes=[("stats", "isx", b)])
            self.c("ve", (lambda v, b=b: v.reciprocal(
                t["sxq"][:, b:b + 1], t["sxq"][:, b:b + 1])),
                reads=[("sxq", b)], writes=[("sxq", b)])
            self.c("ve", (lambda v, b=b: v.tensor_scalar(
                t["sxq"][:, b:b + 1], t["sxq"][:, b:b + 1], 127.0, None, ALU.mult)),
                reads=[("sxq", b)], writes=[("sxq", b)], tag=f"sx{b}")
            self.c("ve", (lambda v, b=b: v.tensor_tensor(
                t["stats"][:, 24 + b:25 + b], t["stats"][:, 24 + b:25 + b],
                t["sxq"][:, b:b + 1], ALU.mult)),
                reads=[("stats", "rstd", b), ("sxq", b)], writes=[("stats", "qsc", b)])
            self.c("ve", (lambda v, b=b: v.tensor_scalar(
                t["scrh"][:, :], t["scr0"][:, :], t["stats"][:, 24 + b:25 + b], RC,
                ALU.mult, ALU.add)),
                reads=[("scr0",), ("stats", "qsc", b)], writes=[("scrh",)])
            self.c("ve", (lambda v, b=b: v.tensor_scalar(
                t["scrb"][:, :], t["scrh"][:, :], RC, None, ALU.subtract)),
                reads=[("scrh",)], writes=[("scrb",)], tag=f"xq{b}")
            for j in range(16):
                ps = "ps_t0" if j % 2 == 0 else "ps_t1"
                self.mm0((lambda te, j=j, ps=ps: te.transpose(
                    self.t[ps][:], t["scrb"][:, 128 * j:128 * (j + 1)], t["ident"][:])),
                    reads=[("scrb",), ("ident",)], writes=[(ps,)], tag="xqT_t")
                self.c("ve", (lambda v, j=j, b=b, ps=ps: v.tensor_copy(
                    out=self.vw_xqT()[:, j, 128 * b:128 * (b + 1)], in_=self.t[ps][:])),
                    reads=[(ps,)], writes=[("arA", "xqT", j, b)])
            for mi in range(4):
                self.c("ve", (lambda v, b=b, mi=mi: v.tensor_tensor(
                    t["dqrow"][:, b, mi:mi + 1], t["stats"][:, 16 + b:17 + b],
                    t["swg"][:, mi:mi + 1], ALU.mult)),
                    reads=[("stats", "isx", b), ("swg",)], writes=[("dqrow", b, mi)],
                    tag=f"dq{b}{mi}")
            self.dma("gp", (lambda g, b=b, xr=xr: g.dma_start(
                out=self.dr["xspill"][128 * b:128 * (b + 1), :], in_=xr[:, b, :])),
                reads=[("arB", "x", b)], writes=[("dram:xspill", b)], tag=f"spx{b}")
        if "dbg_xqT" in self.taps:
            for j in range(16):
                self.dma("sy", (lambda sy, j=j: sy.dma_start(
                    out=self.dbgt["dbg_xqT"][128 * j:128 * (j + 1), :], in_=self.vw_xqT()[:, j, :])),
                    reads=[("arA", "xqT", j, b_) for b_ in range(NB)], writes=[("dram:dbgxqT", j)])
        if "dbg_sxq" in self.taps:
            self.dma("sy", lambda sy: sy.dma_start(out=self.dbgt["dbg_sxq"][:], in_=t["sxq"][:]),
                     reads=[("sxq", b_) for b_ in range(NB)], writes=[("dram:dbgsxq",)])

    def proj_matmuls(self, m, first_claim=False):
        t = self.t
        dr = self.dr
        mi = M_IDX[m]
        pq = self.vw_pq32()
        for dt_ in range(4):
            extra = (self.claim("arA", exclude=("xqT",)) if (first_claim and dt_ == 0) else [])
            self.dma("sy", (lambda sy, m=m, dt_=dt_: sy.dma_start(
                out=t["mvt"][:, :, :],
                in_=dr[f"wt_{m}"].rearrange("(a p) o -> p a o", p=128)
                    [:, :, 512 * dt_:512 * (dt_ + 1)])),
                reads=[(f"dram:wt_{m}",)], writes=[("mvt",)] + extra, tag=f"mv_{m}{dt_}")
            for b in range(NB):
                ps = "ps_a" if b % 2 == 0 else "ps_b"
                for j in range(16):
                    mmf = (lambda te, j=j, b=b, ps=ps: te.matmul(
                        self.t[ps][:], self.vw_xqT()[:, j, 128 * b:128 * (b + 1)],
                        t["mvt"][:, j, :], start=(j == 0), stop=(j == 15)))
                    if j == 0:
                        self.mm0(mmf, reads=[("arA", "xqT", j, b), ("mvt",)],
                                 writes=[(ps,)], tag=f"pj_{m}{dt_}{b}")
                    else:
                        self.mmn(mmf, reads=[("arA", "xqT", j, b), ("mvt",)], keys=[(ps,)])
                self.c("ve", (lambda v, b=b, dt_=dt_, ps=ps, pq=pq, mi=mi: v.tensor_scalar(
                    pq[:, b, 512 * dt_:512 * (dt_ + 1)], self.t[ps][:],
                    t["dqrow"][:, b, mi:mi + 1], None, ALU.mult)),
                    reads=[(ps,), ("dqrow", b, mi)], writes=[("arA", "pq", b, dt_)],
                    tag=f"{m}ev{b}{dt_}")
        return pq

    def split_hilo(self, src_ap, rd, tag):
        t = self.t
        kvh = t["scrb"].ap()
        kvl = t["scrh"].ap().bitcast(BF16)
        self.c("sc", (lambda sc, src_ap=src_ap, kvh=kvh: sc.activation(
            kvh[:, :], src_ap, AF.Identity)),
            reads=rd, writes=[("scrb",)], tag=f"{tag}hi")
        self.c("ve", (lambda v, src_ap=src_ap, kvh=kvh, kvl=kvl: v.tensor_tensor(
            kvl[:, :], src_ap, kvh[:, :], ALU.subtract)),
            reads=rd + [("scrb",)], writes=[("scrh",)], tag=f"{tag}lo")
        return kvh, kvl

    def head_norms(self, st, b, dstoff, tag):
        t = self.t
        rd = [("arA", "pq", b, d_) for d_ in range(4)]
        for h in range(NH):
            self.c("sc", (lambda sc, b=b, h=h, st=st: sc.activation(
                t["scr0"][:, 0:HD], st[:, b, HD * h:HD * (h + 1)], AF.Square,
                accum_out=t["nrmloc"][:, 32 + h:33 + h])),
                reads=rd, writes=[("scr0",), ("nrm", "tmp", h)], tag=f"{tag}{b}{h}")
        key = ("nrm", "kmax") if dstoff == 0 else ("nrm", "qmax")
        self.c("ve", (lambda v, b=b, dstoff=dstoff: (
            v.tensor_copy(out=t["nrmloc"][:, dstoff:dstoff + 16], in_=t["nrmloc"][:, 32:48])
            if b == 0 else
            v.tensor_tensor(t["nrmloc"][:, dstoff:dstoff + 16],
                            t["nrmloc"][:, dstoff:dstoff + 16],
                            t["nrmloc"][:, 32:48], ALU.max))),
            reads=[("nrm", "tmp", h_) for h_ in range(NH)] + ([key] if b else []),
            writes=[key], tag=f"{tag}m{b}")

    def kqT_store(self, name, b, tag):
        t = self.t
        dr = self.dr
        tab = dr[f"{name}_in"].rearrange("(hl j p) t -> p hl j t", hl=2, j=16)
        for hl, srck in ((0, "scrb"), (1, "scrh")):
            srcv = t["scrb"].ap() if hl == 0 else t["scrh"].ap().bitcast(BF16)
            for j in range(16):
                ps = "ps_t0" if j % 2 == 0 else "ps_t1"
                self.mm0((lambda te, j=j, srcv=srcv, ps=ps: te.transpose(
                    self.t[ps][:], srcv[:, 128 * j:128 * (j + 1)], t["ident"][:])),
                    reads=[(srck,), ("ident",)], writes=[(ps,)], tag=f"{tag}T")
                self.c("ve", (lambda v, j=j, ps=ps: v.tensor_copy(
                    out=self.vw_kTb()[:, 128 * j:128 * (j + 1)], in_=self.t[ps][:])),
                    reads=[(ps,)], writes=[("scr1", "kT", j)])
            self.dma("gp", (lambda g, b=b, hl=hl, tab=tab: g.dma_start(
                out=tab[:, hl, :, 128 * b:128 * (b + 1)],
                in_=self.vw_kTb().rearrange("p (j c) -> p j c", j=16))),
                reads=[("scr1", "kT", j_) for j_ in range(16)],
                writes=[(f"dram:{name}_in", b, hl)], tag=f"{tag}st{hl}{b}")

    def ph_proj_kv(self):
        t = self.t
        dr = self.dr
        grp = [[0, 1, 2, 3], [4, 5, 6, 7]]
        kst = self.proj_matmuls("k", first_claim=True)
        for b in range(NB):
            rd = [("arA", "pq", b, d_) for d_ in range(4)]
            self.head_norms(kst, b, 0, "kn")
            self.split_hilo(kst[:, b, :], rd, f"k{b}")
            self.kqT_store("kT", b, f"k{b}")
        self.coll(lambda g: g.collective_compute(
            "AllGather", ALU.bypass, replica_groups=grp,
            ins=[dr["kT_in"][:]], outs=[dr["kT_tab"][:]]),
            reads=[("dram:kT_in", b_, hl_) for b_ in range(NB) for hl_ in range(2)],
            writes=[("dram:kT_tab",)], tag="ag_k")
        vst = self.proj_matmuls("v")
        vtab = dr["v_in"].rearrange("(hl h b k) d -> k hl h b d", hl=2, h=16, b=NB)
        for b in range(NB):
            rd = [("arA", "pq", b, d_) for d_ in range(4)]
            kvh, kvl = self.split_hilo(vst[:, b, :], rd, f"v{b}")
            for hl, srck, src in ((0, "scrb", kvh), (1, "scrh", kvl)):
                self.dma("gp", (lambda g, b=b, hl=hl, src=src, vtab=vtab: g.dma_start(
                    out=vtab[:, hl, :, b, :],
                    in_=src.rearrange("p (h d) -> p h d", h=16))),
                    reads=[(srck,)], writes=[("dram:v_in", b, hl)])
        self.coll(lambda g: g.collective_compute(
            "AllGather", ALU.bypass, replica_groups=grp,
            ins=[dr["v_in"][:]], outs=[dr["v_tab"][:]]),
            reads=[("dram:v_in", b_, hl_) for b_ in range(NB) for hl_ in range(2)],
            writes=[("dram:v_tab",)], tag="ag_v")
        for b in range(NB):
            self.c("ve", (lambda v, b=b: v.tensor_scalar(
                t["dqrow"][:, b, 0:1], t["dqrow"][:, b, 0:1],
                1.0 / float(np.sqrt(HD)), None, ALU.mult)),
                reads=[("dqrow", b, 0)], writes=[("dqrow", b, 0)])
        qst = self.proj_matmuls("q")
        for b in range(NB):
            rd = [("arA", "pq", b, d_) for d_ in range(4)]
            self.head_norms(qst, b, 16, "qn")
            self.split_hilo(qst[:, b, :], rd, f"q{b}")
            self.kqT_store("qT", b, f"q{b}")
        self.coll(lambda g: g.collective_compute(
            "AllGather", ALU.bypass, replica_groups=grp,
            ins=[dr["qT_in"][:]], outs=[dr["qT_tab"][:]]),
            reads=[("dram:qT_in", b_, hl_) for b_ in range(NB) for hl_ in range(2)],
            writes=[("dram:qT_tab",)], tag="ag_q")
        self.mm0(lambda te: te.transpose(self.t["ps_f0"][0:32, :],
                                         t["nrmloc"][:, 0:32], t["identf"][:]),
                 reads=[("nrm", "kmax"), ("nrm", "qmax"), ("identf",)],
                 writes=[("ps_f0",)], tag="nrm_T")
        self.c("ve", lambda v: v.reduce_max(t["nrmloc"][0:32, 60:61], self.t["ps_f0"][0:32, :],
                                            axis=AXX),
               reads=[("ps_f0",)], writes=[("nrm", "red")], tag="nrm_red")
        self.dma("gp", lambda g: g.dma_start(out=dr["nrm_in"][0:1, 0:32],
                                             in_=t["nrmloc"][0:32, 60:61]),
                 reads=[("nrm", "red")], writes=[("dram:nrm_in",)])
        self.coll(lambda g: g.collective_compute(
            "AllGather", ALU.bypass, replica_groups=grp,
            ins=[dr["nrm_in"][:]], outs=[dr["nrm_tab"][:]]),
            reads=[("dram:nrm_in",)], writes=[("dram:nrm_tab",)], tag="ag_nrm")

    def attn_mh(self):
        t = self.t
        dr = self.dr
        self.dma("sy", lambda sy: sy.dma_start(out=t["nrmloc"][0:4, 0:32],
                                               in_=dr["nrm_tab"][:, :]),
                 reads=[("dram:nrm_tab",)], writes=[("nrm", "tab")], tag="ld_nrm")
        self.mm0(lambda te: te.transpose(self.t["ps_f0"][0:32, 0:4],
                                         t["nrmloc"][0:4, 0:32], t["identf"][0:4, 0:4]),
                 reads=[("nrm", "tab"), ("identf",)], writes=[("ps_f0",)], tag="mh_T1")
        self.c("ve", lambda v: v.reduce_max(t["nrmloc"][0:32, 32:33],
                                            self.t["ps_f0"][0:32, 0:4], axis=AXX),
               reads=[("ps_f0",)], writes=[("nrm", "red2")], tag="mh_red")
        self.mm0(lambda te: te.transpose(self.t["ps_f0"][0:1, 0:32],
                                         t["nrmloc"][0:32, 32:33], t["identf"][0:32, 0:32]),
                 reads=[("nrm", "red2"), ("identf",)], writes=[("ps_f0",)], tag="mh_T2")
        self.c("ve", lambda v: v.tensor_copy(out=t["scr0"][0:1, 0:32],
                                             in_=self.t["ps_f0"][0:1, 0:32]),
               reads=[("ps_f0",)], writes=[("scr0",)], tag="mh_cp")
        self.c("ve", lambda v: v.tensor_tensor(
            t["nrmloc"][0:1, 34:50], t["scr0"][0:1, 0:16],
            t["scr0"][0:1, 16:32], ALU.mult),
            reads=[("scr0",)], writes=[("nrm", "m2")], tag="mh_m2")
        self.c("sc", lambda sc: sc.activation(t["nrmloc"][0:1, 34:50],
                                              t["nrmloc"][0:1, 34:50], AF.Sqrt),
               reads=[("nrm", "m2")], writes=[("nrm", "m2")], tag="mh_sqrt")
        self.c("ve", lambda v: v.tensor_scalar(t["nrmloc"][0:1, 34:50],
                                               t["nrmloc"][0:1, 34:50], -1.0, None, ALU.mult),
               reads=[("nrm", "m2")], writes=[("nrm", "m2")], tag="mh_neg")
        self.mm0(lambda te: te.transpose(self.t["ps_f0"][0:16, 16:17],
                                         t["nrmloc"][0:1, 34:50], t["identf"][0:1, 0:1]),
                 reads=[("nrm", "m2"), ("identf",)], writes=[("ps_f0",)], tag="mh_T3")
        self.c("ve", lambda v: v.tensor_copy(out=t["nrmloc"][0:16, 52:53],
                                             in_=self.t["ps_f0"][0:16, 16:17]),
               reads=[("ps_f0",)], writes=[("nrm", "mcol")], tag="mh_c1")
        self.mm0(lambda te: te.matmul(self.t["ps_r"][0:4, 8:9], t["hselsb"][:, :],
                                      t["nrmloc"][0:16, 52:53], start=True, stop=True),
                 reads=[("hselsb",), ("nrm", "mcol")], writes=[("ps_r", "ms")], tag="mh_sel")
        self.c("ve", lambda v: v.tensor_copy(out=t["nrmloc"][0:4, 60:61],
                                             in_=self.t["ps_r"][0:4, 8:9]),
               reads=[("ps_r", "ms")], writes=[("nrm", "msel")], tag="mh_c2")
        self.mm0(lambda te: te.transpose(self.t["ps_f0"][0:1, 20:24],
                                         t["nrmloc"][0:4, 60:61], t["identf"][0:4, 0:4]),
                 reads=[("nrm", "msel"), ("identf",)], writes=[("ps_f0",)], tag="mh_T4")
        self.c("ve", lambda v: v.tensor_copy(out=t["nrmloc"][0:1, 56:60],
                                             in_=self.t["ps_f0"][0:1, 20:24]),
               reads=[("ps_f0",)], writes=[("nrm", "mrow")], tag="mh_c3")
        self.mm0(lambda te: te.matmul(self.t["ps_r"][:, 12:16], t["onesf"][0:1, :],
                                      t["nrmloc"][0:1, 56:60], start=True, stop=True),
                 reads=[("onesf",), ("nrm", "mrow")], writes=[("ps_r", "mb")], tag="mh_bc")
        self.c("ve", lambda v: v.tensor_copy(out=t["mh"][:], in_=self.t["ps_r"][:, 12:16]),
               reads=[("ps_r", "mb")], writes=[("mh",)], tag="mh_fin")

    def attn_pair_loads(self, pr, slot, extra):
        t = self.t
        dr = self.dr
        wkeys = []
        for which, tabn in (("q", "qT_tab"), ("k", "kT_tab")):
            dst = self.vw_pair(slot, which)
            for r in range(4):
                for hl in range(2):
                    col = pr * 8 + r * 2 + hl
                    k = ("arA", "pair", slot, which, hl, r)
                    self.dma("gp", (lambda g, dst=dst, r=r, hl=hl, col=col, tabn=tabn:
                                    g.indirect_dma_start(
                                        out=dst[:, hl, 1024 * r:1024 * (r + 1)],
                                        out_offset=None, in_=dr[tabn][:, :],
                                        in_offset=IndirectOffsetOnAxis(
                                            t["attix"][:, col:col + 1], axis=0))),
                             reads=[(f"dram:{tabn}",), ("attix",)] + (extra if not wkeys else []),
                             writes=[k], tag=f"ld{which}{pr}")
                    wkeys.append(k)
        dst = self.vw_pair(slot, "v")
        for r in range(4):
            for hl in range(2):
                for bi in range(8):
                    col = 32 + pr * 64 + r * 16 + hl * 8 + bi
                    k = ("arA", "pair", slot, "v", hl, r, bi)
                    self.dma("gp", (lambda g, dst=dst, r=r, hl=hl, bi=bi, col=col:
                                    g.indirect_dma_start(
                                        out=dst[:, hl, (r * 8 + bi) * 128:(r * 8 + bi) * 128 + 128],
                                        out_offset=None, in_=dr["v_tab"][:, :],
                                        in_offset=IndirectOffsetOnAxis(
                                            t["attix"][:, col:col + 1], axis=0))),
                             reads=[("dram:v_tab",), ("attix",)],
                             writes=[k], tag=f"ldv{pr}")
                    wkeys.append(k)
        return wkeys

    def attn_pair_compute(self, pr, slot):
        t = self.t
        dr = self.dr
        kp = self.vw_pair(slot, "k")
        qp = self.vw_pair(slot, "q")
        vp = self.vw_pair(slot, "v")
        scrhb = t["scrh"].ap().bitcast(BF16)
        qkeys = [("arA", "pair", slot, "q", hl_, r_) for hl_ in range(2) for r_ in range(4)]

        def emit_S(qc, kb):
            psS = "ps_a" if kb % 2 == 0 else "ps_b"
            co = COLOF[kb]
            kkey = [("arA", "pair", slot, "k", hl_, BLK_OWNER[kb]) for hl_ in range(2)]
            for pi, (hk, hq) in enumerate([(0, 0), (0, 1), (1, 0)]):
                sp = (pi == 2 and kb < 4 * qc)
                fn = (lambda te, hk=hk, hq=hq, co=co, psS=psS, pi=pi, kp=kp, sp=sp:
                      te.matmul(self.t[psS][:], kp[:, hk, co:co + 128],
                                t["qst"][:, hq * 512:(hq + 1) * 512],
                                start=(pi == 0), stop=sp))
                if pi == 0:
                    self.mm0(fn, reads=kkey + [("qst",)], writes=[(psS,)],
                             tag=f"S{pr}_{qc}_{kb}")
                else:
                    self.mmn(fn, reads=kkey + [("qst",)], keys=[(psS,)])
            if kb >= 4 * qc:
                i = kb - 4 * qc
                self.mmn((lambda te, i=i, psS=psS: te.matmul(
                    self.t[psS][:], t["ident"][:], self.vw_maskt()[:, i * 512:(i + 1) * 512],
                    start=False, stop=True)),
                    reads=[("ident",)] + [("maskt", i, cb_) for cb_ in range(4)],
                    keys=[(psS,)], tag=f"Sm{pr}_{qc}_{kb}")

        def emit_PAV(qc, kb, nkb):
            psS = "ps_a" if kb % 2 == 0 else "ps_b"
            co = COLOF[kb]
            pb = (kb % 2) * 512
            self.c("sc", (lambda sc, psS=psS, pb=pb: sc.activation(
                t["scr0"][:, pb:pb + 512], self.t[psS][:], AF.Exp,
                bias=t["mh"][:, pr:pr + 1])),
                reads=[(psS,), ("mh",)], writes=[("scr0",)], tag=f"ex{pr}_{qc}_{kb}")
            self.c("sc", (lambda sc, pb=pb: sc.activation(
                t["scrb"][:, pb:pb + 512], t["scr0"][:, pb:pb + 512], AF.Identity)),
                reads=[("scr0",)], writes=[("scrb",)], tag=f"ph{pr}_{qc}_{kb}")
            self.c("ve", (lambda v, pb=pb, scrhb=scrhb: v.tensor_tensor(
                scrhb[:, pb:pb + 512], t["scr0"][:, pb:pb + 512],
                t["scrb"][:, pb:pb + 512], ALU.subtract)),
                reads=[("scr0",), ("scrb",)], writes=[("scrh",)], tag=f"pl{pr}_{qc}_{kb}")
            vkey = [("arA", "pair", slot, "v", hl_, BLK_OWNER[kb], BLK_IDX[kb])
                    for hl_ in range(2)]
            for pi, (hv, pm) in enumerate([(0, "scrb"), (0, "scrh"), (1, "scrb")]):
                pmv = t["scrb"].ap() if pm == "scrb" else scrhb
                st = (kb == 0 and pi == 0)
                sp = (kb == nkb - 1 and pi == 2)
                fn = (lambda te, hv=hv, pmv=pmv, co=co, pb=pb, st=st, sp=sp, vp=vp:
                      te.matmul(self.t["ps_c"][:], vp[:, hv, co:co + 128],
                                pmv[:, pb:pb + 512], start=st, stop=sp))
                if st:
                    self.mm0(fn, reads=vkey + [(pm,)], writes=[("ps_c",)],
                             tag=f"AV{pr}_{qc}")
                else:
                    self.mmn(fn, reads=vkey + [(pm,)], keys=[("ps_c",)])
            for pi, pm in enumerate(("scrb", "scrh")):
                pmv = t["scrb"].ap() if pm == "scrb" else scrhb
                st = (kb == 0 and pi == 0)
                sp = (kb == nkb - 1 and pi == 1)
                fn = (lambda te, pmv=pmv, pb=pb, st=st, sp=sp: te.matmul(
                    self.t["ps_d"][0:1, :], t["ones128"][:, 0:1], pmv[:, pb:pb + 512],
                    start=st, stop=sp))
                if st:
                    self.mm0(fn, reads=[("ones128",), (pm,)], writes=[("ps_d",)],
                             tag=f"DN{pr}_{qc}")
                else:
                    self.mmn(fn, reads=[("ones128",), (pm,)], keys=[("ps_d",)])

        for qc in range(8):
            for hl in range(2):
                for i in range(4):
                    g = 4 * qc + i
                    self.c("ve", (lambda v, hl=hl, i=i, g=g, qp=qp: v.tensor_copy(
                        out=t["qst"][:, hl * 512 + i * 128:hl * 512 + (i + 1) * 128],
                        in_=qp[:, hl, COLOF[g]:COLOF[g] + 128])),
                        reads=qkeys, writes=[("qst",)], tag=f"qa{pr}{qc}")
            nkb = 4 * qc + 4
            emit_S(qc, 0)
            for kb in range(nkb):
                if kb + 1 < nkb:
                    emit_S(qc, kb + 1)
                emit_PAV(qc, kb, nkb)
            mv32 = self.vw_mvt32()
            self.c("ve", lambda v, mv32=mv32: v.tensor_copy(out=mv32[:, 0:512],
                                                            in_=self.t["ps_c"][:]),
                   reads=[("ps_c",)], writes=[("mvt",)], tag=f"avv{pr}{qc}")
            self.c("ve", lambda v, mv32=mv32: v.tensor_copy(out=mv32[0:1, 512:1024],
                                                            in_=self.t["ps_d"][0:1, :]),
                   reads=[("ps_d",)], writes=[("mvt", "dn")], tag=f"dnv{pr}{qc}")
            for cb in range(4):
                self.mm0((lambda te, cb=cb, mv32=mv32: te.transpose(
                    self.t["ps_f0"][:, cb:cb + 1],
                    mv32[0:1, 512 + cb * 128:512 + (cb + 1) * 128], t["identf"][0:1, 0:1])),
                    reads=[("mvt", "dn"), ("identf",)], writes=[("ps_f0",)], tag=f"dT{pr}{qc}")
            self.c("ve", lambda v: v.tensor_copy(out=t["nrmloc"][:, 48:52],
                                                 in_=self.t["ps_f0"][:, 0:4]),
                   reads=[("ps_f0",)], writes=[("nrm", "dn4")], tag=f"dc{pr}{qc}")
            self.c("ve", lambda v: v.reciprocal(t["nrmloc"][:, 52:56], t["nrmloc"][:, 48:52]),
                   reads=[("nrm", "dn4")], writes=[("nrm", "rdn")], tag=f"dr{pr}{qc}")
            for cb in range(4):
                self.mm0((lambda te, cb=cb, mv32=mv32: te.transpose(
                    self.t["ps_f0"][:], mv32[:, cb * 128:(cb + 1) * 128], t["identf"][:])),
                    reads=[("mvt",), ("identf",)], writes=[("ps_f0",)], tag=f"aT{pr}{qc}")
                self.c("ve", (lambda v, cb=cb: v.tensor_scalar(
                    t["scr1"][:, cb * 128:(cb + 1) * 128], self.t["ps_f0"][:],
                    t["nrmloc"][:, 52 + cb:53 + cb], None, ALU.mult)),
                    reads=[("ps_f0",), ("nrm", "rdn")], writes=[("scr1",)], tag=f"ae{pr}{qc}")
            self.c("sc", lambda sc: sc.activation(t["scrb"][:, 1024:1536],
                                                  t["scr1"][:, 0:512], AF.Identity),
                   reads=[("scr1",)], writes=[("scrb",)], tag=f"ah{pr}{qc}")
            self.c("ve", (lambda v, scrhb=scrhb: v.tensor_tensor(
                scrhb[:, 1024:1536], t["scr1"][:, 0:512], t["scrb"][:, 1024:1536],
                ALU.subtract)),
                reads=[("scr1",), ("scrb",)], writes=[("scrh",)], tag=f"al{pr}{qc}")
            avt = dr["av_in"].rearrange("(q cb p) c -> p q cb c", q=8, cb=4)
            for hl, srck in ((0, "scrb"), (1, "scrh")):
                src = (t["scrb"].ap() if hl == 0 else scrhb)[:, 1024:1536]
                self.dma("gp", (lambda g, hl=hl, src=src, qc=qc, avt=avt: g.dma_start(
                    out=avt[:, qc, :, hl * 512 + pr * 128:hl * 512 + (pr + 1) * 128],
                    in_=src.rearrange("p (cb d) -> p cb d", cb=4))),
                    reads=[(srck,)], writes=[("dram:av_in", pr, qc, hl)],
                    tag=f"avst{pr}{qc}")

    def ph_attn(self):
        dr = self.dr
        t = self.t
        self.attn_mh()
        mk_ = self.vw_maskt()
        first_m = True
        for i in range(4):
            for cb in range(4):
                dst = mk_[:, i * 512 + cb * 128: i * 512 + (cb + 1) * 128]
                wk = [("maskt", i, cb)] + ([("mvt",)] if first_m else [])
                first_m = False
                if cb < i:
                    self.c("ve", (lambda v, dst=dst: v.memset(dst, NEG)), writes=wk)
                elif cb > i:
                    self.c("ve", (lambda v, dst=dst: v.memset(dst, 0.0)), writes=wk)
                else:
                    self.c("gp", (lambda g, dst=dst: g.affine_select(
                        dst, t["zerob"][:], pattern=[[1, 128]], compare_op=ALU.is_ge,
                        fill=NEG, base=0, channel_multiplier=-1)),
                        reads=[("zerob",)], writes=wk)
        mv32 = self.vw_mvt32()
        self.c("ve", lambda v, mv32=mv32: v.memset(mv32[:, 2048:4096], 0.0),
               reads=[("mvt",)], writes=[("mvt", "z")], tag="z_mk")
        if self.upto == "all":
            for r in range(65):
                self.dma("sy", (lambda sy, r=r, mv32=mv32: sy.dma_start(
                    out=dr["rs_in"][128 * r:128 * (r + 1), :], in_=mv32[:, 2048:4096])),
                    reads=[("mvt", "z")], writes=[("dram:rs_in", "z", r)], tag=f"z_rs{r}")
        extra = self.claim("arA")
        self.attn_pair_loads(0, 0, extra)
        for pr in range(4):
            if pr < 3:
                self.attn_pair_loads(pr + 1, (pr + 1) % 2, [])
            self.attn_pair_compute(pr, pr % 2)
        self.coll(lambda g: g.collective_compute(
            "AllGather", ALU.bypass, replica_groups=[[0, 1, 2, 3], [4, 5, 6, 7]],
            ins=[dr["av_in"][:]], outs=[dr["av_tab"][:]]),
            reads=[("dram:av_in", pr_, qc_, hl_) for pr_ in range(4) for qc_ in range(8)
                   for hl_ in range(2)],
            writes=[("dram:av_tab",)], tag="ag_av")

    def ph_wo(self):
        t = self.t
        dr = self.dr
        avloc = self.vw_avloc()
        extra = self.claim("arA")
        for r in range(4):
            for b in range(NB):
                col = r * 8 + b
                k = ("arA", "avloc", r, b)
                self.dma("gp", (lambda g, r=r, b=b, col=col, avloc=avloc:
                                g.indirect_dma_start(
                                    out=avloc[:, b, r, :, :].rearrange("p hl c -> p (hl c)"),
                                    out_offset=None, in_=dr["av_tab"][:, :],
                                    in_offset=IndirectOffsetOnAxis(
                                        t["avix"][:, col:col + 1], axis=0))),
                         reads=[("dram:av_tab",), ("avix",)] + (extra if (r == 0 and b == 0) else []),
                         writes=[k], tag=f"ldav{r}{b}")
        av = self.vw_xres()
        extrab = self.claim("arB")
        for b in range(NB):
            for r in range(4):
                self.c("ve", (lambda v, b=b, r=r, av=av, avloc=avloc: v.tensor_tensor(
                    av[:, b, r * 512:(r + 1) * 512], avloc[:, b, r, 0, :],
                    avloc[:, b, r, 1, :], ALU.add)),
                    reads=[("arA", "avloc", r, b)],
                    writes=[("arB", "av", b, r)] + (extrab if (b == 0 and r == 0) else []),
                    tag=f"av{b}{r}")
        if "dbg_av" in self.taps:
            for b in range(NB):
                self.dma("sy", (lambda sy, b=b, av=av: sy.dma_start(
                    out=self.dbgt["dbg_av"][128 * b:128 * (b + 1), :], in_=av[:, b, :])),
                    reads=[("arB", "av", b, r_) for r_ in range(4)], writes=[("dram:dbgav", b)])
        oqT = self.vw_oqT()
        extra2 = self.claim("arA")
        for b in range(NB):
            rdav = [("arB", "av", b, r_) for r_ in range(4)]
            self.c("ve", (lambda v, b=b, av=av: v.reduce_max(
                t["sxq"][:, b:b + 1], av[:, b, :], axis=AXX, apply_absolute_value=True)),
                reads=rdav, writes=[("sxq", b)], tag=f"oam{b}")
            self.c("ve", (lambda v, b=b: v.tensor_scalar(
                t["sxq"][:, b:b + 1], t["sxq"][:, b:b + 1], 1e-5, None, ALU.max)),
                reads=[("sxq", b)], writes=[("sxq", b)])
            self.c("ve", (lambda v, b=b: v.tensor_scalar(
                t["stats"][:, 16 + b:17 + b], t["sxq"][:, b:b + 1], 1.0 / 127.0, None, ALU.mult)),
                reads=[("sxq", b)], writes=[("stats", "isx", b)])
            self.c("ve", (lambda v, b=b: v.reciprocal(
                t["sxq"][:, b:b + 1], t["sxq"][:, b:b + 1])),
                reads=[("sxq", b)], writes=[("sxq", b)])
            self.c("ve", (lambda v, b=b: v.tensor_scalar(
                t["sxq"][:, b:b + 1], t["sxq"][:, b:b + 1], 127.0, None, ALU.mult)),
                reads=[("sxq", b)], writes=[("sxq", b)], tag=f"osx{b}")
            self.c("ve", (lambda v, b=b, av=av: v.tensor_scalar(
                t["scrh"][:, :], av[:, b, :], t["sxq"][:, b:b + 1], RC, ALU.mult, ALU.add)),
                reads=rdav + [("sxq", b)], writes=[("scrh",)])
            self.c("ve", (lambda v, b=b: v.tensor_scalar(
                t["scrb"][:, :], t["scrh"][:, :], RC, None, ALU.subtract)),
                reads=[("scrh",)], writes=[("scrb",)], tag=f"oq{b}")
            for j in range(16):
                ps = "ps_t0" if j % 2 == 0 else "ps_t1"
                self.mm0((lambda te, j=j, ps=ps: te.transpose(
                    self.t[ps][:], t["scrb"][:, 128 * j:128 * (j + 1)], t["ident"][:])),
                    reads=[("scrb",), ("ident",)], writes=[(ps,)], tag="oqT_t")
                self.c("ve", (lambda v, j=j, b=b, ps=ps, oqT=oqT: v.tensor_copy(
                    out=oqT[:, j, 128 * b:128 * (b + 1)], in_=self.t[ps][:])),
                    reads=[(ps,)], writes=[("arA", "oqT", j, b)] +
                    (extra2 if (b == 0 and j == 0) else []))
            self.c("ve", (lambda v, b=b: v.tensor_tensor(
                t["dqrow"][:, b, 3:4], t["stats"][:, 16 + b:17 + b],
                t["swg"][:, 3:4], ALU.mult)),
                reads=[("stats", "isx", b), ("swg",)], writes=[("dqrow", b, 3)],
                tag=f"odq{b}")
        x1 = self.vw_xres()
        extrab2 = self.claim("arB")
        for b in range(NB):
            self.dma("sy", (lambda sy, b=b, x1=x1: sy.dma_start(
                out=x1[:, b, :], in_=dr["xspill"][128 * b:128 * (b + 1), :])),
                reads=[("dram:xspill", b)],
                writes=[("arB", "x1", b)] + (extrab2 if b == 0 else []), tag=f"ldx{b}")
        for dt_ in range(4):
            self.dma("sy", (lambda sy, dt_=dt_: sy.dma_start(
                out=t["mvt"][:, :, :],
                in_=dr["wt_o"].rearrange("(a p) o -> p a o", p=128)
                    [:, :, 512 * dt_:512 * (dt_ + 1)])),
                reads=[("dram:wt_o",)], writes=[("mvt",), ("mvt", "dn"), ("mvt", "z")],
                tag=f"mv_o{dt_}")
            for b in range(NB):
                ps = "ps_a" if b % 2 == 0 else "ps_b"
                for j in range(16):
                    mmf = (lambda te, j=j, b=b, ps=ps, oqT=oqT: te.matmul(
                        self.t[ps][:], oqT[:, j, 128 * b:128 * (b + 1)],
                        t["mvt"][:, j, :], start=(j == 0), stop=(j == 15)))
                    if j == 0:
                        self.mm0(mmf, reads=[("arA", "oqT", j, b), ("mvt",)],
                                 writes=[(ps,)], tag=f"oj{dt_}{b}")
                    else:
                        self.mmn(mmf, reads=[("arA", "oqT", j, b), ("mvt",)], keys=[(ps,)])
                self.c("ve", (lambda v, b=b, ps=ps: v.tensor_scalar(
                    t["scr0"][:, 0:512], self.t[ps][:], t["dqrow"][:, b, 3:4], None, ALU.mult)),
                    reads=[(ps,), ("dqrow", b, 3)], writes=[("scr0",)], tag=f"oe{dt_}{b}")
                self.c("ve", (lambda v, b=b, dt_=dt_, x1=x1: v.tensor_tensor(
                    x1[:, b, 512 * dt_:512 * (dt_ + 1)], x1[:, b, 512 * dt_:512 * (dt_ + 1)],
                    t["scr0"][:, 0:512], ALU.add)),
                    reads=[("arB", "x1", b), ("scr0",)], writes=[("arB", "x1", b)],
                    tag=f"oa{dt_}{b}")
        if "dbg_x1" in self.taps:
            for b in range(NB):
                self.dma("sy", (lambda sy, b=b, x1=x1: sy.dma_start(
                    out=self.dbgt["dbg_x1"][128 * b:128 * (b + 1), :], in_=x1[:, b, :])),
                    reads=[("arB", "x1", b)], writes=[("dram:dbgx1", b)])

    def ph_ffn_pre(self):
        t = self.t
        dr = self.dr
        x1 = self.vw_xres()
        xrT = self.vw_xrT()
        mv32 = self.vw_mvt32()
        extra = self.claim("arA")
        self.dma("sy", lambda sy: sy.dma_start(out=t["scr0"][0:1, :], in_=self.normw_in[:]),
                 reads=[("scr0",)], writes=[("scr0",), ("tok_ffn",)] + extra, tag="ld_nw")
        nw = self.af32("arA")[:, 18432:20480]
        for cc in range(4):
            self.mm0((lambda te, cc=cc: te.matmul(
                self.t["ps_a"][:], t["onesf"][0:1, :], t["scr0"][0:1, 512 * cc:512 * (cc + 1)],
                start=True, stop=True)),
                reads=[("scr0",), ("onesf",), ("tok_ffn",)], writes=[("ps_a",)], tag=f"nwb{cc}")
            self.c("ve", (lambda v, cc=cc, nw=nw: v.tensor_copy(
                out=nw[:, 512 * cc:512 * (cc + 1)], in_=self.t["ps_a"][:])),
                reads=[("ps_a",)], writes=[("arA", "nw", cc)], tag=f"nwc{cc}")
        gw = self.af32("arA")[0:32, 16384:18432]
        gwT = self.af32("arA")[:, 20480:20992]
        self.dma("sy", (lambda sy, gw=gw: sy.dma_start(
            out=gw, in_=self.gate_in[:, :])),
            reads=[("tok_ffn",)], writes=[("arA", "gw")], tag="ld_gw")
        for j in range(16):
            self.mm0((lambda te, j=j, gw=gw: te.transpose(
                self.t["ps_f0"][:, 0:32], gw[:, 128 * j:128 * (j + 1)],
                t["identf"][0:32, 0:32])),
                reads=[("arA", "gw"), ("identf",)], writes=[("ps_f0",)], tag=f"gwT{j}")
            self.c("ve", (lambda v, j=j, gwT=gwT: v.tensor_copy(
                out=gwT[:, 32 * j:32 * (j + 1)], in_=self.t["ps_f0"][:, 0:32])),
                reads=[("ps_f0",)], writes=[("arA", "gwT", j)], tag=f"gwc{j}")
        self.c("ve", lambda v: v.memset(t["scrb"][:], 0.0),
               reads=[("tok_ffn",)], writes=[("scrb",)], tag="zjunk")
        for half in range(2):
            self.dma("gp", (lambda g, half=half: g.dma_start(
                out=dr["xr_in"][1024:1040, 2048 * half:2048 * (half + 1)],
                in_=t["scrb"][0:16, :])),
                reads=[("scrb",)], writes=[("dram:xr_in", "junk", half)], tag=f"zj{half}")
        self.c("ve", lambda v: v.memset(t["gwt"][:], 0.0),
               reads=[("tok_ffn",)], writes=[("gwt", "z")], tag="zrj0")
        self.dma("gp", lambda g: g.dma_start(out=dr["rt_in"][1024:1040, :],
                                             in_=t["gwt"][0:16, 0:8]),
                 reads=[("gwt", "z")], writes=[("dram:rt_in", "junk")], tag="zrj1")
        for b in range(NB):
            rdx = [("arB", "x1", b)]
            self.c("sc", (lambda sc, b=b, x1=x1: sc.activation(
                t["scr1"][:, 0:D], x1[:, b, :], AF.Identity,
                accum_out=t["stats"][:, b:b + 1])),
                reads=rdx, writes=[("scr1",), ("stats", "m", b)], tag=f"l2m{b}")
            self.c("ve", (lambda v, b=b: v.tensor_scalar(
                t["stats"][:, b:b + 1], t["stats"][:, b:b + 1], 1.0 / D, None, ALU.mult)),
                reads=[("stats", "m", b)], writes=[("stats", "m", b)])
            self.c("ve", (lambda v, b=b, x1=x1: v.tensor_scalar(
                t["scr0"][:, :], x1[:, b, :], t["stats"][:, b:b + 1], None, ALU.subtract)),
                reads=rdx + [("stats", "m", b)], writes=[("scr0",)], tag=f"l2d{b}")
            self.c("sc", (lambda sc, b=b: sc.activation(
                t["scr1"][:, 0:D], t["scr0"][:, :], AF.Square,
                accum_out=t["stats"][:, 8 + b:9 + b])),
                reads=[("scr0",)], writes=[("scr1",), ("stats", "ss", b)], tag=f"l2s{b}")
            self.c("sc", (lambda sc, b=b: sc.activation(
                t["stats"][:, 16 + b:17 + b], t["stats"][:, 8 + b:9 + b], AF.Sqrt,
                bias=t["epsln"][:], scale=1.0 / D)),
                reads=[("stats", "ss", b), ("epsln",)], writes=[("stats", "std", b)])
            self.c("ve", (lambda v, b=b: v.reciprocal(
                t["stats"][:, 24 + b:25 + b], t["stats"][:, 16 + b:17 + b])),
                reads=[("stats", "std", b)], writes=[("stats", "rstd", b)])
            self.c("ve", (lambda v, b=b: v.tensor_scalar(
                t["scr0"][:, :], t["scr0"][:, :], t["stats"][:, 24 + b:25 + b], None, ALU.mult)),
                reads=[("scr0",), ("stats", "rstd", b)], writes=[("scr0",)], tag=f"ln2{b}")
            self.c("ve", (lambda v, b=b, x1=x1: v.tensor_tensor(
                x1[:, b, :], x1[:, b, :], t["scr0"][:, :], ALU.add)),
                reads=rdx + [("scr0",)], writes=[("arB", "oacc", b)], tag=f"oac{b}")
            self.c("sc", (lambda sc, b=b: sc.activation(
                t["scr1"][:, 0:D], t["scr0"][:, :], AF.Square,
                accum_out=t["stats"][:, 8 + b:9 + b])),
                reads=[("scr0",)], writes=[("scr1",), ("stats", "ss", b)], tag=f"rms{b}")
            self.c("sc", (lambda sc, b=b: sc.activation(
                t["stats"][:, 16 + b:17 + b], t["stats"][:, 8 + b:9 + b], AF.Sqrt,
                bias=t["epsrms"][:], scale=1.0 / D)),
                reads=[("stats", "ss", b), ("epsrms",)], writes=[("stats", "std", b)])
            self.c("ve", (lambda v, b=b: v.reciprocal(
                t["stats"][:, 24 + b:25 + b], t["stats"][:, 16 + b:17 + b])),
                reads=[("stats", "std", b)], writes=[("stats", "rstd", b)])
            self.c("ve", (lambda v, b=b: v.tensor_scalar(
                t["scr1"][:, :], t["scr0"][:, :], t["stats"][:, 24 + b:25 + b], None, ALU.mult)),
                reads=[("scr0",), ("stats", "rstd", b)], writes=[("scr1",)], tag=f"xr0{b}")
            self.c("ve", (lambda v, b=b, nw=nw: v.tensor_tensor(
                t["scr1"][:, :], t["scr1"][:, :], nw[:, :], ALU.mult)),
                reads=[("scr1",)] + [("arA", "nw", cc_) for cc_ in range(4)],
                writes=[("scr1",)], tag=f"xr{b}")
            self.c("sc", lambda sc: sc.activation(t["scrb"][:], t["scr1"][:, :], AF.Identity),
                   reads=[("scr1",)], writes=[("scrb",)], tag=f"xh{b}")
            scrhb = t["scrh"].ap().bitcast(BF16)
            self.c("ve", (lambda v, scrhb=scrhb: v.tensor_tensor(
                scrhb[:, :], t["scr1"][:, :], t["scrb"][:], ALU.subtract)),
                reads=[("scr1",), ("scrb",)], writes=[("scrh",)], tag=f"xl{b}")
            for hl, srck in ((0, "scrb"), (1, "scrh")):
                src = t["scrb"].ap() if hl == 0 else scrhb
                self.dma("gp", (lambda g, b=b, hl=hl, src=src: g.dma_start(
                    out=dr["xr_in"][128 * b:128 * (b + 1), 2048 * hl:2048 * (hl + 1)],
                    in_=src[:, :])),
                    reads=[(srck,)], writes=[("dram:xr_in", b, hl)], tag=f"xst{b}{hl}")
                for j in range(16):
                    ps = "ps_t0" if j % 2 == 0 else "ps_t1"
                    self.mm0((lambda te, j=j, src=src, ps=ps: te.transpose(
                        self.t[ps][:], src[:, 128 * j:128 * (j + 1)], t["ident"][:])),
                        reads=[(srck,), ("ident",)], writes=[(ps,)], tag="xrT_t")
                    self.c("ve", (lambda v, j=j, b=b, hl=hl, ps=ps, xrT=xrT: v.tensor_copy(
                        out=xrT[:, hl, j, 128 * b:128 * (b + 1)], in_=self.t[ps][:])),
                        reads=[(ps,)], writes=[("arA", "xrT", hl, j, b)])
            self.c("ve", (lambda v, b=b, xrT=xrT, mv32=mv32: v.tensor_tensor(
                mv32[:, 0:2048].rearrange("p (j c) -> p j c", j=16),
                xrT[:, 0, :, 128 * b:128 * (b + 1)], xrT[:, 1, :, 128 * b:128 * (b + 1)],
                ALU.add)),
                reads=[("arA", "xrT", hl_, j_, b) for hl_ in range(2) for j_ in range(16)],
                writes=[("mvt",)], tag=f"x32{b}")
            for j in range(16):
                fn = (lambda te, j=j, gwT=gwT, mv32=mv32: te.matmul(
                    self.t["ps_r"][:, 0:32], mv32[:, 128 * j:128 * (j + 1)],
                    gwT[:, 32 * j:32 * (j + 1)], start=(j == 0), stop=(j == 15)))
                if j == 0:
                    self.mm0(fn, reads=[("mvt",), ("arA", "gwT", j)],
                             writes=[("ps_r", "lg")], tag=f"lg{b}")
                else:
                    self.mmn(fn, reads=[("mvt",), ("arA", "gwT", j)], keys=[("ps_r", "lg")])
            self.c("ve", lambda v: v.tensor_copy(out=t["nrmloc"][:, 0:32],
                                                 in_=self.t["ps_r"][:, 0:32]),
                   reads=[("ps_r", "lg")], writes=[("nrm", "lgv")], tag=f"lge{b}")
            self.c("ve", lambda v: v.reduce_max(t["nrmloc"][:, 40:41], t["nrmloc"][:, 0:32],
                                                axis=AXX),
                   reads=[("nrm", "lgv")], writes=[("nrm", "mx")], tag=f"lgm{b}")
            self.c("ve", lambda v: v.tensor_scalar(t["nrmloc"][:, 41:42], t["nrmloc"][:, 40:41],
                                                   -1.0, None, ALU.mult),
                   reads=[("nrm", "mx")], writes=[("nrm", "nmx")])
            self.c("sc", lambda sc: sc.activation(t["scr1"][:, 0:32], t["nrmloc"][:, 0:32],
                                                  AF.Exp, bias=t["nrmloc"][:, 41:42]),
                   reads=[("nrm", "lgv"), ("nrm", "nmx")], writes=[("scr1",)], tag=f"lge2{b}")
            self.c("ve", lambda v: v.max(t["nrmloc"][:, 32:40], t["scr1"][:, 0:32]),
                   reads=[("scr1",)], writes=[("nrm", "t8")], tag=f"t8{b}")
            self.c("ve", (lambda v: v.max_index(
                self.vw_idsu(), t["nrmloc"][:, 32:40], t["scr1"][:, 0:32])),
                reads=[("nrm", "t8"), ("scr1",)], writes=[("idsu",)], tag=f"mi{b}")
            self.c("ve", lambda v: v.tensor_copy(out=self.vw_rtloc()[:, 0:4],
                                                 in_=self.vw_idsu()[:, 0:4]),
                   reads=[("idsu",)], writes=[("rtloc", "ids")], tag=f"idc{b}")
            self.c("sc", lambda sc: sc.activation(t["scr1"][:, 32:36], t["nrmloc"][:, 32:36],
                                                  AF.Identity,
                                                  accum_out=t["nrmloc"][:, 42:43]),
                   reads=[("nrm", "t8")], writes=[("scr1",), ("nrm", "s4")], tag=f"s4{b}")
            self.c("ve", lambda v: v.reciprocal(t["nrmloc"][:, 43:44], t["nrmloc"][:, 42:43]),
                   reads=[("nrm", "s4")], writes=[("nrm", "rs4")])
            self.c("ve", lambda v: v.tensor_scalar(self.vw_rtloc()[:, 4:8], t["nrmloc"][:, 32:36],
                                                   t["nrmloc"][:, 43:44], None, ALU.mult),
                   reads=[("nrm", "t8"), ("nrm", "rs4")], writes=[("rtloc", "w")], tag=f"wt{b}")
            self.dma("gp", (lambda g, b=b: g.dma_start(
                out=dr["rt_in"][128 * b:128 * (b + 1), :], in_=self.vw_rtloc()[:, :])),
                reads=[("rtloc", "ids"), ("rtloc", "w")], writes=[("dram:rt_in", b)],
                tag=f"rst{b}")
        self.coll(lambda g: g.collective_compute(
            "AllGather", ALU.bypass, replica_groups=[list(range(NCORE))],
            ins=[dr["xr_in"][:]], outs=[dr["xrtab"][:]]),
            reads=[("dram:xr_in", b_, hl_) for b_ in range(NB) for hl_ in range(2)] +
                  [("dram:xr_in", "junk", hl_) for hl_ in range(2)],
            writes=[("dram:xrtab",)], tag="ag_xr")
        self.coll(lambda g: g.collective_compute(
            "AllGather", ALU.bypass, replica_groups=[list(range(NCORE))],
            ins=[dr["rt_in"][:]], outs=[dr["rttab"][:]]),
            reads=[("dram:rt_in", b_) for b_ in range(NB)] + [("dram:rt_in", "junk")],
            writes=[("dram:rttab",)], tag="ag_rt")
        for b in range(NB):
            self.dma("gp", (lambda g, b=b, x1=x1: g.dma_start(
                out=dr["oaspill"][128 * b:128 * (b + 1), :], in_=x1[:, b, :])),
                reads=[("arB", "oacc", b)], writes=[("dram:oaspill", b)], tag=f"spo{b}")
        self.dma("gp", lambda g: g.dma_start(
            out=dr["xrT_sp"].rearrange("p t -> p t"), in_=self.abf("arA")[:, 0:32768]),
            reads=[("arA", "xrT", hl_, j_, b_) for hl_ in range(2) for j_ in range(16)
                   for b_ in range(NB)],
            writes=[("dram:xrT_sp",)], tag="spxrT")
        if "dbg_xr" in self.taps:
            for b in range(NB):
                self.dma("gp", (lambda sy, b=b: sy.dma_start(
                    out=self.dbgt["dbg_xr"][128 * b:128 * (b + 1), :],
                    in_=dr["xr_in"][128 * b:128 * (b + 1), 0:2048])),
                    reads=[("dram:xr_in", b, 0)], writes=[("dram:dbgxr", b)])
        if "dbg_rt" in self.taps:
            for b in range(NB):
                self.dma("sy", (lambda sy, b=b: sy.dma_start(
                    out=self.dbgt["dbg_rt"][128 * b:128 * (b + 1), :],
                    in_=dr["rt_in"][128 * b:128 * (b + 1), :])),
                    reads=[("dram:rt_in", b)], writes=[("dram:dbgrt", b)])

    def ph_routing(self):
        t = self.t
        dr = self.dr
        A = self.af32("arA")
        extra = self.claim("arA")
        first = True
        for cc in range(NCORE):
            s_ = cc // 4
            o = cc % 4
            for rn in range(2):
                pbase = s_ * 64 + (8 * o if rn == 0 else 56 - 8 * o)
                src = dr["rttab"][ROWS * cc + 512 * rn:ROWS * cc + 512 * rn + 512, :]
                self.dma("sy", (lambda sy, pbase=pbase, src=src, A=A: sy.dma_start(
                    out=A[pbase:pbase + 8, 0:512],
                    in_=src.rearrange("(pp i) k -> pp (i k)", pp=8))),
                    reads=[("dram:rttab",)],
                    writes=[("arA", "rt", "raw", cc, rn)] + (extra + [("tok_rt",)] if first else []),
                    tag=f"rtl{cc}{rn}")
                first = False
        rawk = [("arA", "rt", "raw", cc_, rn_) for cc_ in range(NCORE) for rn_ in range(2)]
        rv = A[:, 0:512].rearrange("p (i k) -> p i k", i=64)
        self.c("ve", lambda v, rv=rv, A=A: v.tensor_copy(
            out=A[:, 512:768].rearrange("p (i k) -> p i k", i=64), in_=rv[:, :, 0:4]),
            reads=rawk, writes=[("arA", "rt", "ids")], tag="ids")
        self.c("ve", lambda v, rv=rv, A=A: v.tensor_copy(
            out=A[:, 768:1024].rearrange("p (i k) -> p i k", i=64), in_=rv[:, :, 4:8]),
            reads=rawk, writes=[("arA", "rt", "wts")], tag="wts")
        self.c("ve", lambda v, A=A: v.memset(A[:, 1024:1280], 0.0),
               reads=[("tok_rt",)], writes=[("arA", "rt", "z")], tag="rz")
        self.c("ve", lambda v, A=A: v.memset(A[:, 2048:2304], 0.0),
               reads=[("tok_rt",)], writes=[("arA", "rt", "tgt")], tag="tz")
        self.c("ve", lambda v, A=A: v.memset(A[:, 2304:2560], 0.0),
               reads=[("tok_rt",)], writes=[("arA", "rt", "ovr")], tag="oz")
        self.c("gp", lambda g, A=A: g.iota(A[:, 3328:3584], pattern=[[1, 64], [0, 4]],
                                           base=0, channel_multiplier=0,
                                           allow_small_or_imprecise_dtypes=True),
               reads=[("tok_rt",)], writes=[("arA", "rt", "iota")], tag="iota")
        self.c("gp", lambda g, A=A: g.iota(A[:, 3072:3328], pattern=[[0, 256]],
                                           base=NE * CAP, channel_multiplier=1,
                                           allow_small_or_imprecise_dtypes=True),
               reads=[("tok_rt",)], writes=[("arA", "rt", "junk")], tag="junkr")
        self.c("ve", lambda v, A=A: v.tensor_scalar(
            A[:, 2816:3072], A[:, 3328:3584], t["rowb"][:], None, ALU.add),
            reads=[("arA", "rt", "iota"), ("rowb",)], writes=[("arA", "rt", "gv")], tag="gvals")
        for e in range(NE):
            self.c("ve", (lambda v, e=e, A=A: v.tensor_scalar(
                A[:, 1536:1792], A[:, 512:768], float(e), None, ALU.is_equal)),
                reads=[("arA", "rt", "ids")], writes=[("arA", "rt", "mask")], tag=f"msk{e}")
            self.c("ve", lambda v, A=A: v.tensor_tensor_scan(
                A[:, 1280:1536], A[:, 1536:1792], A[:, 1024:1280], 0.0, ALU.add, ALU.add),
                reads=[("arA", "rt", "mask"), ("arA", "rt", "z")], writes=[("arA", "rt", "pref")], tag=f"scan{e}")
            self.c("sc", (lambda sc, A=A: sc.activation(
                t["scr0"][:, 0:256], A[:, 1536:1792], AF.Identity,
                accum_out=A[:, 3584:3585])),
                reads=[("arA", "rt", "mask")], writes=[("scr0",), ("arA", "rt", "s")], tag=f"rs{e}")
            self.c("ve", lambda v, A=A: v.tensor_copy(out=t["scrb"][:, 0:1],
                                                      in_=A[:, 3584:3585]),
                   reads=[("arA", "rt", "s")], writes=[("scrb",)], tag=f"sb16{e}")
            self.mm0((lambda te: te.matmul(self.t["ps_r"][:, 16:17], t["ltri"][:],
                                           t["scrb"][:, 0:1], start=True, stop=True)),
                     reads=[("ltri",), ("scrb",)], writes=[("ps_r", "sb")], tag=f"sbm{e}")
            self.c("ve", lambda v, A=A: v.tensor_copy(out=A[:, 3585:3586],
                                                      in_=self.t["ps_r"][:, 16:17]),
                   reads=[("ps_r", "sb")], writes=[("arA", "rt", "sbase")], tag=f"sbc{e}")
            self.c("ve", lambda v, A=A: v.tensor_tensor(
                A[:, 2560:2816], A[:, 1280:1536], A[:, 1536:1792], ALU.subtract),
                reads=[("arA", "rt", "pref"), ("arA", "rt", "mask")], writes=[("arA", "rt", "pos")], tag=f"p0{e}")
            self.c("ve", lambda v, A=A: v.tensor_scalar(
                A[:, 2560:2816], A[:, 2560:2816], A[:, 3585:3586], None, ALU.add),
                reads=[("arA", "rt", "pos"), ("arA", "rt", "sbase")], writes=[("arA", "rt", "pos")], tag=f"p1{e}")
            self.c("ve", lambda v, A=A: v.tensor_scalar(
                t["scr0"][:, 256:512], A[:, 2560:2816], float(CAP), None, ALU.is_ge),
                reads=[("arA", "rt", "pos")], writes=[("scr0",)], tag=f"ov{e}")
            self.c("ve", (lambda v, e=e, A=A: v.tensor_scalar(
                t["scr1"][:, 0:256], A[:, 2560:2816], float(CAP - 1), float(e * CAP),
                ALU.min, ALU.add)),
                reads=[("arA", "rt", "pos")], writes=[("scr1",)], tag=f"tg{e}")
            self.c("ve", lambda v, A=A: v.tensor_tensor(
                t["scr1"][:, 256:512], A[:, 1536:1792], t["scr1"][:, 0:256], ALU.mult),
                reads=[("arA", "rt", "mask"), ("scr1",)], writes=[("scr1",)], tag=f"tm{e}")
            self.c("ve", lambda v, A=A: v.tensor_tensor(
                A[:, 2048:2304], A[:, 2048:2304], t["scr1"][:, 256:512], ALU.add),
                reads=[("arA", "rt", "tgt"), ("scr1",)], writes=[("arA", "rt", "tgt")], tag=f"ta{e}")
            self.c("ve", lambda v, A=A: v.tensor_tensor(
                t["scr0"][:, 512:768], A[:, 1536:1792], t["scr0"][:, 256:512], ALU.mult),
                reads=[("arA", "rt", "mask"), ("scr0",)], writes=[("scr0",)], tag=f"om{e}")
            self.c("ve", lambda v, A=A: v.tensor_tensor(
                A[:, 2304:2560], A[:, 2304:2560], t["scr0"][:, 512:768], ALU.add),
                reads=[("arA", "rt", "ovr"), ("scr0",)], writes=[("arA", "rt", "ovr")], tag=f"oa{e}")
        self.c("ve", lambda v, A=A: v.tensor_tensor(
            t["scr0"][:, 0:256], A[:, 3072:3328], A[:, 2048:2304], ALU.subtract),
            reads=[("arA", "rt", "junk"), ("arA", "rt", "tgt")], writes=[("scr0",)], tag="j0")
        self.c("ve", lambda v, A=A: v.tensor_tensor(
            t["scr0"][:, 256:512], A[:, 2304:2560], t["scr0"][:, 0:256], ALU.mult),
            reads=[("arA", "rt", "ovr"), ("scr0",)], writes=[("scr0",)], tag="j1")
        self.c("ve", lambda v, A=A: v.tensor_tensor(
            A[:, 2048:2304], A[:, 2048:2304], t["scr0"][:, 256:512], ALU.add),
            reads=[("arA", "rt", "tgt"), ("scr0",)], writes=[("arA", "rt", "tgt")], tag="j2")
        payloads = [("gv", 2816), ("wts", 768), ("tgt", 2048)]
        dsts = {"gv": 3970, "wts": 4002, "tgt": 4034}
        for nm, off in payloads:
            for ch in range(2):
                self.mm0((lambda te, off=off, ch=ch, A=A: te.transpose(
                    self.t["ps_f0"][:], A[:, off + 128 * ch:off + 128 * (ch + 1)],
                    t["identf"][:])),
                    reads=[("arA", "rt", nm)], writes=[("ps_f0",)], tag=f"cx{nm}{ch}")
                self.c("ve", (lambda v, ch=ch, A=A: v.tensor_copy(
                    out=A[:, 3586 + 128 * ch:3586 + 128 * (ch + 1)], in_=self.t["ps_f0"][:])),
                    reads=[("ps_f0",)], writes=[("arA", "rt", "xT", ch)], tag=f"cxe{nm}{ch}")
            for ch in range(2):
                fn = (lambda te, ch=ch, A=A: te.matmul(
                    self.t["ps_f0"][0:32, :],
                    t["eselsb"][:, 32 * ch:32 * (ch + 1)],
                    A[:, 3586 + 128 * ch:3586 + 128 * (ch + 1)],
                    start=(ch == 0), stop=(ch == 1)))
                if ch == 0:
                    self.mm0(fn, reads=[("eselsb",), ("arA", "rt", "xT", 0),
                                        ("arA", "rt", "xT", 1)],
                             writes=[("ps_f0",)], tag=f"cs{nm}")
                else:
                    self.mmn(fn, reads=[("eselsb",), ("arA", "rt", "xT", 0),
                                        ("arA", "rt", "xT", 1)], keys=[("ps_f0",)])
            self.c("ve", (lambda v, A=A: v.tensor_copy(
                out=A[0:32, 3842:3970], in_=self.t["ps_f0"][0:32, :])),
                reads=[("ps_f0",)], writes=[("arA", "rt", "gT")], tag=f"cge{nm}")
            self.mm0((lambda te, A=A: te.transpose(
                self.t["ps_f0"][:, 0:32], A[0:32, 3842:3970], t["identf"][0:32, 0:32])),
                reads=[("arA", "rt", "gT")], writes=[("ps_f0",)], tag=f"cgt{nm}")
            self.c("ve", (lambda v, nm=nm, A=A, dsts=dsts: v.tensor_copy(
                out=A[:, dsts[nm]:dsts[nm] + 32], in_=self.t["ps_f0"][:, 0:32])),
                reads=[("ps_f0",)], writes=[("arA", "rt", "c", nm)], tag=f"cf{nm}")
        self.c("ve", lambda v, A=A: v.tensor_copy(out=t["ti32"][:, 0:32],
                                                  in_=A[:, 4034:4066]),
               reads=[("arA", "rt", "c", "tgt")], writes=[("ti32",)], tag="ti32")
        gl = self.vw_glwv().rearrange("p (f b) -> p f b", b=2)
        self.c("ve", lambda v, A=A, gl=gl: v.tensor_copy(
            out=gl[:, 0:32, 0:1], in_=A[:, 3970:4002].rearrange("p (f o) -> p f o", o=1)),
            reads=[("arA", "rt", "c", "gv")], writes=[("arA", "rt", "glwv")], tag="glv0")
        self.c("ve", lambda v, A=A, gl=gl: v.tensor_copy(
            out=gl[:, 0:32, 1:2], in_=A[:, 4002:4034].rearrange("p (f o) -> p f o", o=1)),
            reads=[("arA", "rt", "wts"), ("arA", "rt", "c", "wts")],
            writes=[("arA", "rt", "glwv")], tag="glv1")
        mv32 = self.vw_mvt32()
        self.c("ve", lambda v, mv32=mv32: v.memset(mv32[:, 0:1026], 0.0),
               reads=[("mvt",)], writes=[("mvt", "sf")], tag="sf0")
        self.dma("gp", lambda g, mv32=mv32: g.dma_start(
            out=dr["glw_in"].rearrange("(p a) b -> p (a b)", p=128), in_=mv32[:, 0:1026]),
            reads=[("mvt", "sf")], writes=[("dram:glw_in",)], tag="glwz")
        prev = ("dram:glw_in",)
        for f in range(32):
            k = ("dram:glw_in", "s", f)
            self.dma("gp", (lambda g, f=f, gl=gl: g.indirect_dma_start(
                out=dr["glw_in"][:, :],
                out_offset=IndirectOffsetOnAxis(t["ti32"][:, f:f + 1], axis=0),
                in_=gl[:, f, :], in_offset=None)),
                reads=[("ti32",), ("arA", "rt", "glwv"), prev],
                writes=[k], tag=f"gsc{f}")
            prev = k
        self.coll(lambda g: g.collective_compute(
            "AllReduce", ALU.add, replica_groups=[list(range(NCORE))],
            ins=[dr["glw_in"][:]], outs=[dr["glw"][:]]),
            reads=[prev], writes=[("dram:glw",)], tag="ar_glw")

    def ph_shared_gu(self):
        t = self.t
        dr = self.dr
        mv32 = self.vw_mvt32()
        xrT = self.vw_xrTB()
        extrab = self.claim("arB")
        self.dma("sy", lambda sy: sy.dma_start(
            out=self.abf("arB")[:, 0:32768], in_=dr["xrT_sp"][:, :]),
            reads=[("dram:xrT_sp",)], writes=[("arB", "xrT"), ("tok_sh",)] + extrab,
            tag="ldxrT")
        slab = self.abf("arA")[:, 0:32768].rearrange("p (m hl j c) -> p m hl j c",
                                                     m=2, hl=2, j=16)
        extra = self.claim("arA", exclude=("rt",))
        for ic in range(2):
            for mi_, wsrc in ((0, self.shg_in), (1, self.shu_in)):
                for qh in range(4):
                    self.dma("sy", (lambda sy, ic=ic, wsrc=wsrc, qh=qh, mv32=mv32:
                                    sy.dma_start(
                                        out=mv32[:, 0:2048].rearrange(
                                            "p (j f) -> p j f", j=4),
                                        in_=wsrc[512 * qh:512 * (qh + 1),
                                                 512 * ic:512 * (ic + 1)]
                                        .rearrange("(j p) f -> p j f", p=128))),
                             reads=[("tok_sh",)],
                             writes=[("mvt",), ("mvt", "sf"), ("mvt", "z"), ("mvt", "dn")] +
                                    (extra if (ic == 0 and mi_ == 0 and qh == 0) else []),
                             tag=f"shw{ic}{mi_}{qh}")
                    hi = slab[:, mi_, 0, 4 * qh:4 * qh + 4, :].rearrange("p j c -> p (j c)")
                    lo = slab[:, mi_, 1, 4 * qh:4 * qh + 4, :].rearrange("p j c -> p (j c)")
                    self.c("ve", (lambda v, hi=hi, mv32=mv32: v.tensor_copy(
                        out=hi, in_=mv32[:, 0:2048])),
                        reads=[("mvt",)], writes=[("arA", "sl", mi_, 0, qh)], tag="shh")
                    self.c("ve", (lambda v, hi=hi, lo=lo, mv32=mv32: v.tensor_tensor(
                        lo, mv32[:, 0:2048], hi, ALU.subtract)),
                        reads=[("mvt",), ("arA", "sl", mi_, 0, qh)],
                        writes=[("arA", "sl", mi_, 1, qh)], tag="shl")
            xkeys = [("arB", "xrT")]
            for b in range(NB):
                for mi_, psn in ((0, "ps_a"), (1, "ps_b")):
                    first = True
                    for j in range(16):
                        slkeys = [("arA", "sl", mi_, 0, j // 4), ("arA", "sl", mi_, 1, j // 4)]
                        for hx, hw in ((0, 0), (0, 1), (1, 0)):
                            fn = (lambda te, b=b, j=j, hx=hx, hw=hw, mi_=mi_, psn=psn,
                                  xrT=xrT, slab=slab, first=first:
                                  te.matmul(self.t[psn][:],
                                            xrT[:, hx, j, 128 * b:128 * (b + 1)],
                                            slab[:, mi_, hw, j, :],
                                            start=first, stop=(j == 15 and hx == 1)))
                            if first:
                                self.mm0(fn, reads=xkeys + slkeys, writes=[(psn,)],
                                         tag=f"sg{ic}{b}{mi_}")
                                first = False
                            else:
                                self.mmn(fn, reads=xkeys + slkeys, keys=[(psn,)])
                self.c("sc", lambda sc: sc.activation(t["scr0"][:, 0:512],
                                                      self.t["ps_a"][:], AF.Silu),
                       reads=[("ps_a",)], writes=[("scr0",)], tag=f"sil{ic}{b}")
                self.c("ve", lambda v: v.tensor_tensor(
                    t["scr1"][:, 0:512], t["scr0"][:, 0:512], self.t["ps_b"][:], ALU.mult),
                    reads=[("scr0",), ("ps_b",)], writes=[("scr1",)], tag=f"au{ic}{b}")
                self.dma("sy", (lambda sy, ic=ic, b=b: sy.dma_start(
                    out=dr["act_sp"][128 * b:128 * (b + 1), 512 * ic:512 * (ic + 1)],
                    in_=t["scr1"][:, 0:512])),
                    reads=[("scr1",)], writes=[("dram:act_sp", b, ic)], tag=f"acst{ic}{b}")

    def ph_experts(self):
        t = self.t
        dr = self.dr
        self._last_rs = None
        mv32 = self.vw_mvt32()
        bufv = self.abf("arB")[:, 0:32768].rearrange("p (sb c) -> p sb c", sb=8)
        bufT = self.abf("arA")[:, 0:32768].rearrange("p (hl hc s) -> p hl hc s", hl=2, hc=16)
        sl13 = self.abf("arA")[:, 32768:49152].rearrange("p (hl j c) -> p hl j c", hl=2, j=16)
        actT = self.abf("arA")[:, 0:16384].rearrange("p (hl ic s) -> p hl ic s", hl=2, ic=8)
        sl2 = self.abf("arA")[:, 16384:24576].rearrange("p (hl i c) -> p hl i c", hl=2, i=8)
        guv = self.af32("arB").rearrange("p (sb c) -> p sb c", sb=8)
        scrhb = t["scrh"].ap().bitcast(BF16)
        for el in range(EPC):
            for sup in range(2):
                eid = (el, sup)
                for sb in range(8):
                    col = el * 16 + sup * 8 + sb
                    self.dma("gp", (lambda g, col=col, sb=sb: g.indirect_dma_start(
                        out=t["gwt"][:, 2 * sb:2 * sb + 2], out_offset=None,
                        in_=dr["glw"][:, :],
                        in_offset=IndirectOffsetOnAxis(t["exix"][:, col:col + 1], axis=0))),
                        reads=[("dram:glw",), ("exix",)],
                        writes=[("gwt", sb)], tag=f"gl{el}{sup}{sb}")
                gw2 = t["gwt"].ap().rearrange("p (s b) -> p s b", s=8)
                self.c("ve", lambda v, gw2=gw2: v.tensor_copy(
                    out=t["gidx"].ap().rearrange("p (s o) -> p s o", o=1), in_=gw2[:, :, 0:1]),
                    reads=[("gwt", sb_) for sb_ in range(8)], writes=[("gidx",)],
                    tag=f"gi{el}{sup}")
                self.c("ve", lambda v, gw2=gw2: v.tensor_copy(
                    out=t["wloc"].ap().rearrange("p (s o) -> p s o", o=1), in_=gw2[:, :, 1:2]),
                    reads=[("gwt", sb_) for sb_ in range(8)], writes=[("wloc",)],
                    tag=f"gww{el}{sup}")
                exb = self.claim("arB") if True else []
                for sb in range(8):
                    self.dma("gp", (lambda g, sb=sb, bufv=bufv: g.indirect_dma_start(
                        out=bufv[:, sb, :], out_offset=None, in_=dr["xrtab"][:, :],
                        in_offset=IndirectOffsetOnAxis(t["gidx"][:, sb:sb + 1], axis=0))),
                        reads=[("dram:xrtab",), ("gidx",)],
                        writes=[("arB", "buf", eid, sb)] + (exb if sb == 0 else []),
                        tag=f"gx{el}{sup}{sb}")
                exa = self.claim("arA")
                for sb in range(8):
                    for hl in range(2):
                        for hc in range(16):
                            ps = "ps_t0" if (hc % 2 == 0) else "ps_t1"
                            src = bufv[:, sb, hl * 2048 + hc * 128:hl * 2048 + (hc + 1) * 128]
                            self.mm0((lambda te, src=src, ps=ps: te.transpose(
                                self.t[ps][:], src, t["ident"][:])),
                                reads=[("arB", "buf", eid, sb), ("ident",)],
                                writes=[(ps,)], tag="bT")
                            self.c("ve", (lambda v, sb=sb, hl=hl, hc=hc, ps=ps, bufT=bufT:
                                          v.tensor_copy(
                                              out=bufT[:, hl, hc, 128 * sb:128 * (sb + 1)],
                                              in_=self.t[ps][:])),
                                reads=[(ps,)],
                                writes=[("arA", "bufT", eid, hl, hc, sb)] +
                                       (exa if (sb == 0 and hl == 0 and hc == 0) else []),
                                tag="bTe")
                btk = [("arA", "bufT", eid, hl_, hc_, sb_) for hl_ in range(2)
                       for hc_ in range(16) for sb_ in range(8)]
                exg = self.claim("arB", exclude=("buf",))
                for fcol in range(4):
                    for qh in range(4):
                        self.dma("sy", (lambda sy, el=el, fcol=fcol, qh=qh, mv32=mv32:
                                        sy.dma_start(
                                            out=mv32[:, 0:2048].rearrange(
                                                "p (j f) -> p j f", j=4),
                                            in_=self.w13_in[el, 512 * qh:512 * (qh + 1),
                                                            512 * fcol:512 * (fcol + 1)]
                                            .rearrange("(j p) f -> p j f", p=128))),
                                 reads=[("gidx",)],
                                 writes=[("mvt",), ("mvt", "sf"), ("mvt", "z"), ("mvt", "dn")],
                                 tag=f"w13{el}{sup}{fcol}{qh}")
                        hi = sl13[:, 0, 4 * qh:4 * qh + 4, :].rearrange("p j c -> p (j c)")
                        lo = sl13[:, 1, 4 * qh:4 * qh + 4, :].rearrange("p j c -> p (j c)")
                        self.c("ve", (lambda v, hi=hi, mv32=mv32: v.tensor_copy(
                            out=hi, in_=mv32[:, 0:2048])),
                            reads=[("mvt",)], writes=[("arA", "s13", 0, qh)], tag="w13h")
                        self.c("ve", (lambda v, hi=hi, lo=lo, mv32=mv32: v.tensor_tensor(
                            lo, mv32[:, 0:2048], hi, ALU.subtract)),
                            reads=[("mvt",), ("arA", "s13", 0, qh)],
                            writes=[("arA", "s13", 1, qh)], tag="w13l")
                    for sb in range(8):
                        psn = "ps_a" if sb % 2 == 0 else "ps_b"
                        first = True
                        for hc in range(16):
                            slk = [("arA", "s13", 0, hc // 4), ("arA", "s13", 1, hc // 4)]
                            for hb, hw in ((0, 0), (0, 1), (1, 0)):
                                rds = [("arA", "bufT", eid, hb, hc, sb)] + slk
                                fn = (lambda te, sb=sb, hc=hc, hb=hb, hw=hw, psn=psn,
                                      bufT=bufT, sl13=sl13, first=first:
                                      te.matmul(self.t[psn][:],
                                                bufT[:, hb, hc, 128 * sb:128 * (sb + 1)],
                                                sl13[:, hw, hc, :],
                                                start=first, stop=(hc == 15 and hb == 1)))
                                if first:
                                    self.mm0(fn, reads=rds, writes=[(psn,)],
                                             tag=f"g13{el}{sup}{fcol}{sb}")
                                    first = False
                                else:
                                    self.mmn(fn, reads=rds, keys=[(psn,)])
                        self.c("ve", (lambda v, sb=sb, fcol=fcol, psn=psn, guv=guv:
                                      v.tensor_copy(
                                          out=guv[:, sb, 512 * fcol:512 * (fcol + 1)],
                                          in_=self.t[psn][:])),
                            reads=[(psn,)],
                            writes=[("arB", "gu", eid, sb, fcol)] +
                                   (exg if (fcol == 0 and sb == 0) else []),
                            tag=f"gue{el}{sup}{fcol}{sb}")
                for sb in range(8):
                    for half in range(2):
                        self.c("sc", (lambda sc, sb=sb, half=half, guv=guv: sc.activation(
                            t["scr0"][:, 512 * half:512 * (half + 1)],
                            guv[:, sb, 512 * half:512 * (half + 1)], AF.Silu)),
                            reads=[("arB", "gu", eid, sb, half)], writes=[("scr0",)],
                            tag=f"si{el}{sup}{sb}{half}")
                        self.c("ve", (lambda v, sb=sb, half=half, guv=guv: v.tensor_tensor(
                            guv[:, sb, 512 * half:512 * (half + 1)],
                            t["scr0"][:, 512 * half:512 * (half + 1)],
                            guv[:, sb, 1024 + 512 * half:1024 + 512 * (half + 1)], ALU.mult)),
                            reads=[("scr0",), ("arB", "gu", eid, sb, half),
                                   ("arB", "gu", eid, sb, 2 + half)],
                            writes=[("arB", "act", eid, sb, half)], tag=f"am{el}{sup}{sb}{half}")
                exa2 = self.claim("arA", exclude=("s13",))
                for sb in range(8):
                    ak = [("arB", "act", eid, sb, 0), ("arB", "act", eid, sb, 1)]
                    self.c("sc", (lambda sc, sb=sb, guv=guv: sc.activation(
                        t["scrb"][:, 0:1024], guv[:, sb, 0:1024], AF.Identity)),
                        reads=ak, writes=[("scrb",)], tag=f"ash{el}{sup}{sb}")
                    self.c("ve", (lambda v, sb=sb, guv=guv, scrhb=scrhb: v.tensor_tensor(
                        scrhb[:, 0:1024], guv[:, sb, 0:1024], t["scrb"][:, 0:1024],
                        ALU.subtract)),
                        reads=ak + [("scrb",)], writes=[("scrh",)], tag=f"asl{el}{sup}{sb}")
                    for hl, srck in ((0, "scrb"), (1, "scrh")):
                        src = t["scrb"].ap() if hl == 0 else scrhb
                        for icn in range(8):
                            ps = "ps_t0" if icn % 2 == 0 else "ps_t1"
                            self.mm0((lambda te, src=src, icn=icn, ps=ps: te.transpose(
                                self.t[ps][:], src[:, 128 * icn:128 * (icn + 1)],
                                t["ident"][:])),
                                reads=[(srck,), ("ident",)], writes=[(ps,)], tag="aT")
                            self.c("ve", (lambda v, sb=sb, hl=hl, icn=icn, ps=ps, actT=actT:
                                          v.tensor_copy(
                                              out=actT[:, hl, icn, 128 * sb:128 * (sb + 1)],
                                              in_=self.t[ps][:])),
                                reads=[(ps,)],
                                writes=[("arA", "actT", eid, hl, icn, sb)] +
                                       (exa2 if (sb == 0 and hl == 0 and icn == 0) else []),
                                tag="aTe")
                atk = [("arA", "actT", eid, hl_, ic_, sb_) for hl_ in range(2)
                       for ic_ in range(8) for sb_ in range(8)]
                exe = self.claim("arB", exclude=("buf",))
                eov = self.af32("arB").rearrange("p (sb c) -> p sb c", sb=8)
                for hcol in range(4):
                    for qh in range(4):
                        self.dma("sy", (lambda sy, el=el, hcol=hcol, qh=qh, mv32=mv32:
                                        sy.dma_start(
                                            out=mv32[:, 0:1024].rearrange(
                                                "p (i h) -> p i h", i=2),
                                            in_=self.w2_in[el, 256 * qh:256 * (qh + 1),
                                                           512 * hcol:512 * (hcol + 1)]
                                            .rearrange("(i p) h -> p i h", p=128))),
                            reads=[("gidx",)],
                            writes=[("mvt",), ("mvt", "sf"), ("mvt", "z"), ("mvt", "dn")],
                            tag=f"w2{el}{sup}{hcol}{qh}")
                        hi2 = sl2[:, 0, 2 * qh:2 * qh + 2, :].rearrange("p i c -> p (i c)")
                        lo2 = sl2[:, 1, 2 * qh:2 * qh + 2, :].rearrange("p i c -> p (i c)")
                        self.c("ve", (lambda v, hi2=hi2, mv32=mv32: v.tensor_copy(
                            out=hi2, in_=mv32[:, 0:1024])),
                            reads=[("mvt",)], writes=[("arA", "s2", 0, qh)], tag="w2h")
                        self.c("ve", (lambda v, hi2=hi2, lo2=lo2, mv32=mv32: v.tensor_tensor(
                            lo2, mv32[:, 0:1024], hi2, ALU.subtract)),
                            reads=[("mvt",), ("arA", "s2", 0, qh)],
                            writes=[("arA", "s2", 1, qh)], tag="w2l")
                    for sb in range(8):
                        psn = "ps_a" if sb % 2 == 0 else "ps_b"
                        first = True
                        for icn in range(8):
                            slk2 = [("arA", "s2", 0, icn // 2), ("arA", "s2", 1, icn // 2)]
                            for ha, hw in ((0, 0), (0, 1), (1, 0)):
                                rds = [("arA", "actT", eid, ha, icn, sb)] + slk2
                                fn = (lambda te, sb=sb, icn=icn, ha=ha, hw=hw, psn=psn,
                                      actT=actT, sl2=sl2, first=first:
                                      te.matmul(self.t[psn][:],
                                                actT[:, ha, icn, 128 * sb:128 * (sb + 1)],
                                                sl2[:, hw, icn, :],
                                                start=first, stop=(icn == 7 and ha == 1)))
                                if first:
                                    self.mm0(fn, reads=rds, writes=[(psn,)],
                                             tag=f"e2{el}{sup}{hcol}{sb}")
                                    first = False
                                else:
                                    self.mmn(fn, reads=rds, keys=[(psn,)])
                        self.c("ve", (lambda v, sb=sb, hcol=hcol, psn=psn, eov=eov:
                                      v.tensor_scalar(
                                          eov[:, sb, 512 * hcol:512 * (hcol + 1)],
                                          self.t[psn][:], t["wloc"][:, sb:sb + 1], None,
                                          ALU.mult)),
                            reads=[(psn,), ("wloc",)],
                            writes=[("arB", "eo", eid, sb, hcol)] +
                                   (exe if (hcol == 0 and sb == 0) else []),
                            tag=f"eoe{el}{sup}{hcol}{sb}")
                for sb in range(8):
                    k = ("dram:rs_in", eid, sb)
                    self.dma("gp", (lambda g, sb=sb, eov=eov: g.indirect_dma_start(
                        out=dr["rs_in"][:, :],
                        out_offset=IndirectOffsetOnAxis(t["gidx"][:, sb:sb + 1], axis=0),
                        in_=eov[:, sb, :], in_offset=None, compute_op=ALU.add)),
                        reads=[("gidx",)] + [("arB", "eo", eid, sb, hc_) for hc_ in range(4)] +
                              [("dram:rs_in", "z", r_) for r_ in range(65)] +
                              ([self._last_rs] if self._last_rs else []),
                        writes=[k], tag=f"rsc{el}{sup}{sb}")
                    self._last_rs = k

    def ph_return(self):
        t = self.t
        dr = self.dr
        mv32 = self.vw_mvt32()
        scrhb = t["scrh"].ap().bitcast(BF16)
        self.coll(lambda g: g.collective_compute(
            "ReduceScatter", ALU.add, replica_groups=[list(range(NCORE))],
            ins=[dr["rs_in"][:]], outs=[dr["rs_out"][:]]),
            reads=[("dram:rs_in", (el_, sup_), sb_) for el_ in range(EPC)
                   for sup_ in range(2) for sb_ in range(8)] +
                  [("dram:rs_in", "z", r_) for r_ in range(65)],
            writes=[("dram:rs_out",)], tag="rs")
        actT = self.abf("arB")[:, 0:16384].rearrange("p (hl ic s) -> p hl ic s", hl=2, ic=8)
        sld = self.abf("arB")[:, 16384:24576].rearrange("p (hl i c) -> p hl i c", hl=2, i=8)
        extrab = self.claim("arB")
        for b in range(NB):
            self.dma("sy", (lambda sy, b=b, mv32=mv32: sy.dma_start(
                out=mv32[:, 0:1024], in_=dr["act_sp"][128 * b:128 * (b + 1), :])),
                reads=[("dram:act_sp", b, 0), ("dram:act_sp", b, 1)],
                writes=[("mvt",), ("mvt", "sf"), ("mvt", "z"), ("mvt", "dn")] +
                       (extrab if b == 0 else []),
                tag=f"lda{b}")
            self.c("sc", (lambda sc, mv32=mv32: sc.activation(
                t["scrb"][:, 0:1024], mv32[:, 0:1024], AF.Identity)),
                reads=[("mvt",)], writes=[("scrb",)], tag=f"dsh{b}")
            self.c("ve", (lambda v, mv32=mv32, scrhb=scrhb: v.tensor_tensor(
                scrhb[:, 0:1024], mv32[:, 0:1024], t["scrb"][:, 0:1024], ALU.subtract)),
                reads=[("mvt",), ("scrb",)], writes=[("scrh",)], tag=f"dsl{b}")
            for hl, srck in ((0, "scrb"), (1, "scrh")):
                src = t["scrb"].ap() if hl == 0 else scrhb
                for icn in range(8):
                    ps = "ps_t0" if icn % 2 == 0 else "ps_t1"
                    self.mm0((lambda te, src=src, icn=icn, ps=ps: te.transpose(
                        self.t[ps][:], src[:, 128 * icn:128 * (icn + 1)], t["ident"][:])),
                        reads=[(srck,), ("ident",)], writes=[(ps,)], tag="dT")
                    self.c("ve", (lambda v, b=b, hl=hl, icn=icn, ps=ps, actT=actT:
                                  v.tensor_copy(
                                      out=actT[:, hl, icn, 128 * b:128 * (b + 1)],
                                      in_=self.t[ps][:])),
                        reads=[(ps,)], writes=[("arB", "dactT", hl, icn, b)], tag="dTe")
        atk = [("arB", "dactT", hl_, ic_, b_) for hl_ in range(2) for ic_ in range(8)
               for b_ in range(NB)]
        for hcol in range(4):
            self.dma("sy", (lambda sy, hcol=hcol, mv32=mv32: sy.dma_start(
                out=mv32[:, 0:4096].rearrange("p (i h) -> p i h", i=8),
                in_=self.shd_in[:, 512 * hcol:512 * (hcol + 1)]
                .rearrange("(i p) h -> p i h", p=128))),
                reads=atk[:1],
                writes=[("mvt",), ("mvt", "sf"), ("mvt", "z"), ("mvt", "dn")],
                tag=f"wd{hcol}")
            hi2 = sld[:, 0, :, :].rearrange("p i c -> p (i c)")
            lo2 = sld[:, 1, :, :].rearrange("p i c -> p (i c)")
            self.c("ve", (lambda v, hi2=hi2, mv32=mv32: v.tensor_copy(
                out=hi2, in_=mv32[:, 0:4096])),
                reads=[("mvt",)], writes=[("arB", "sd", 0)], tag="wdh")
            self.c("ve", (lambda v, hi2=hi2, lo2=lo2, mv32=mv32: v.tensor_tensor(
                lo2, mv32[:, 0:4096], hi2, ALU.subtract)),
                reads=[("mvt",), ("arB", "sd", 0)], writes=[("arB", "sd", 1)], tag="wdl")
            for b in range(NB):
                psn = "ps_a" if b % 2 == 0 else "ps_b"
                first = True
                for icn in range(8):
                    for ha, hw in ((0, 0), (0, 1), (1, 0)):
                        rds = [("arB", "dactT", ha, icn, b), ("arB", "sd", 0),
                               ("arB", "sd", 1)]
                        fn = (lambda te, b=b, icn=icn, ha=ha, hw=hw, psn=psn,
                              actT=actT, sld=sld, first=first:
                              te.matmul(self.t[psn][:],
                                        actT[:, ha, icn, 128 * b:128 * (b + 1)],
                                        sld[:, hw, icn, :],
                                        start=first, stop=(icn == 7 and ha == 1)))
                        if first:
                            self.mm0(fn, reads=rds, writes=[(psn,)], tag=f"dn{hcol}{b}")
                            first = False
                        else:
                            self.mmn(fn, reads=rds, keys=[(psn,)])
                self.c("ve", (lambda v, psn=psn: v.tensor_copy(
                    out=t["scr0"][:, 0:512], in_=self.t[psn][:])),
                    reads=[(psn,)], writes=[("scr0",)], tag=f"dne{hcol}{b}")
                self.dma("sy", (lambda sy, b=b, hcol=hcol: sy.dma_start(
                    out=dr["shared_sp"][128 * b:128 * (b + 1), 512 * hcol:512 * (hcol + 1)],
                    in_=t["scr0"][:, 0:512])),
                    reads=[("scr0",)], writes=[("dram:shared_sp", b, hcol)],
                    tag=f"dns{hcol}{b}")
        for b in range(NB):
            self.dma("sy", (lambda sy, b=b, mv32=mv32: sy.dma_start(
                out=mv32[:, 0:2048], in_=dr["oaspill"][128 * b:128 * (b + 1), :])),
                reads=[("dram:oaspill", b)],
                writes=[("mvt",), ("mvt", "sf"), ("mvt", "z"), ("mvt", "dn")],
                tag=f"f0{b}")
            self.dma("sy", (lambda sy, b=b: sy.dma_start(
                out=t["scr0"][:, :], in_=dr["shared_sp"][128 * b:128 * (b + 1), :])),
                reads=[("dram:shared_sp", b, hc_) for hc_ in range(4)],
                writes=[("scr0",)], tag=f"f1{b}")
            self.dma("sy", (lambda sy, b=b: sy.dma_start(
                out=t["scr1"][:, :], in_=dr["rs_out"][128 * b:128 * (b + 1), :])),
                reads=[("dram:rs_out",)], writes=[("scr1",)], tag=f"f2{b}")
            self.c("ve", (lambda v, mv32=mv32: v.tensor_tensor(
                t["scr0"][:, :], t["scr0"][:, :], mv32[:, 0:2048], ALU.add)),
                reads=[("scr0",), ("mvt",)], writes=[("scr0",)], tag=f"fa{b}")
            self.c("ve", lambda v: v.tensor_tensor(
                t["scr0"][:, :], t["scr0"][:, :], t["scr1"][:, :], ALU.add),
                reads=[("scr0",), ("scr1",)], writes=[("scr0",)], tag=f"fb{b}")
            self.dma("sy", (lambda sy, b=b: sy.dma_start(
                out=self.out[128 * b:128 * (b + 1), :], in_=t["scr0"][:, :])),
                reads=[("scr0",)], writes=[("dram:out", b)], tag=f"fout{b}")

    def ph_out_partial(self):
        dr = self.dr
        self.dma("sy", lambda sy: sy.dma_start(out=self.out[:, :], in_=dr["xspill"][:, :]),
                 reads=[("dram:xspill", b_) for b_ in range(NB)],
                 writes=[("dram:out",)], tag="st_out")


def make_layout_inputs(core):
    rank = core % 4
    blks = blocks_of(core)
    attidx = np.zeros((128, 288), np.int32)
    for pr in range(4):
        h = 4 * rank + pr
        for r in range(4):
            for hl in range(2):
                attidx[:, pr * 8 + r * 2 + hl] = (r * 4096 + hl * 2048 + h * 128 +
                                                  np.arange(128))
                for bi in range(8):
                    attidx[:, 32 + pr * 64 + r * 16 + hl * 8 + bi] = (
                        r * 32768 + hl * 16384 + h * 1024 + bi * 128 + np.arange(128))
    avidx = np.zeros((128, 32), np.int32)
    for r in range(4):
        for b in range(NB):
            avidx[:, r * 8 + b] = r * 4096 + blks[b] * 128 + np.arange(128)
    exidx = np.zeros((128, 64), np.int32)
    for el in range(EPC):
        for sup in range(2):
            for sb in range(8):
                exidx[:, el * 16 + sup * 8 + sb] = ((4 * core + el) * CAP + sup * 1024 +
                                                    sb * 128 + np.arange(128))
    esel = np.zeros((128, 64), np.float32)
    for i in range(32):
        f = 32 * core + i
        esel[f % 128, (f // 128) * 32 + i] = 1.0
    hsel = np.zeros((16, 4), np.float32)
    for pr in range(4):
        hsel[4 * rank + pr, pr] = 1.0
    rowbase = np.zeros((128, 1), np.float32)
    for p in range(128):
        s_ = p // 64
        bglob = (p % 64) // 2
        half = p % 2
        oc = 4 * s_ + BLK_OWNER[bglob]
        l = BLK_IDX[bglob] * 128 + half * 64
        rowbase[p, 0] = oc * ROWS + l
    return {"attidx": attidx, "avidx": avidx, "exidx": exidx, "hsel": hsel,
            "esel": esel, "rowbase": rowbase}


def shard_inputs(inputs):
    x = np.asarray(inputs["L0_attn_x"])
    ws = {m: np.asarray(inputs[f"L0_attn_w_{m}"]) for m in "qkvo"}
    gw = np.asarray(inputs["L0_ffn_gate_w"])
    nw = np.asarray(inputs["L0_ffn_norm_w"]).reshape(1, D)
    w13 = np.asarray(inputs["L0_ffn_w13"])
    w2 = np.asarray(inputs["L0_ffn_w2"])
    shg = np.asarray(inputs["L0_ffn_shared_gate_w"])
    shu = np.asarray(inputs["L0_ffn_shared_up_w"])
    shd = np.asarray(inputs["L0_ffn_shared_down_w"])
    maps = []
    for c in range(NCORE):
        seq = c // 4
        rows = np.concatenate([x[seq, 128 * b:128 * (b + 1), :] for b in blocks_of(c)], 0)
        maps.append({
            "x_in": np.ascontiguousarray(rows),
            **{f"w_{m}": np.ascontiguousarray(ws[m][:, 256 * c:256 * (c + 1)]) for m in "qkvo"},
            "gate_w": gw, "norm_w": nw,
            "w13": np.ascontiguousarray(w13[EPC * c:EPC * (c + 1)]),
            "w2": np.ascontiguousarray(w2[EPC * c:EPC * (c + 1)]),
            "sh_g": shg, "sh_u": shu, "sh_d": shd,
            **make_layout_inputs(c),
        })
    return maps


def gather_output(results):
    out = np.zeros((2, S, D), np.float32)
    for c in range(NCORE):
        seq = c // 4
        r = np.asarray(results[c]["out"])
        for i, b in enumerate(blocks_of(c)):
            out[seq, 128 * b:128 * (b + 1), :] = r[128 * i:128 * (i + 1), :]
    return out


def kernel(**inputs):
    import os
    os.environ.setdefault("NEURON_SCRATCHPAD_PAGE_SIZE", "512")
    mk = MK(upto=UPTO)
    nc = mk.emit()
    maps = shard_inputs(inputs)
    res = run_bass_kernel_spmd(nc, maps, core_ids=list(range(NCORE)))
    return gather_output(res.results)


# version 06 (final)
# speedup vs baseline: 1.0744x; 1.0001x over previous
import contextlib
import numpy as np
import ml_dtypes
import concourse.bass as bass
import concourse.mybir as mybir
from concourse.bass import IndirectOffsetOnAxis
from concourse.bass_utils import run_bass_kernel_spmd

F32 = mybir.dt.float32
BF16 = mybir.dt.bfloat16
FP16 = mybir.dt.float16
I32 = mybir.dt.int32
U32 = mybir.dt.uint32
AF = mybir.ActivationFunctionType
ALU = mybir.AluOpType
AXX = mybir.AxisListType.X

NCORE = 8
D = 2048
NB = 8
TPC = NB * 128
NH = 16
HD = 128
S = 4096
SBLK = 32
NE = 32
EPC = 4
CAP = 2048
NEG = -1e30
RC = 1536.0
EPS_LN = 1e-5
EPS_RMS = 1e-6
ROWS = 1040
GT = NCORE * ROWS
JUNK_XR = 1024.0
UPTO = "all"

M_IDX = {"q": 0, "k": 1, "v": 2, "o": 3}


def blocks_of(core):
    o = core % 4
    return sorted(list(range(4 * o, 4 * o + 4)) + list(range(28 - 4 * o, 32 - 4 * o)))


BLK_OWNER = [0] * SBLK
BLK_IDX = [0] * SBLK
for _o in range(4):
    for _i, _b in enumerate(blocks_of(_o)):
        BLK_OWNER[_b] = _o
        BLK_IDX[_b] = _i
COLOF = [BLK_OWNER[g] * 1024 + BLK_IDX[g] * 128 for g in range(SBLK)]

ENGS = ("te", "ve", "sc", "gp", "sy")


class Op:
    __slots__ = ("eng", "fn", "deps", "marked", "chainval", "kind", "tag", "idx",
                 "_acc_key")

    def __init__(self, eng, fn, kind, tag):
        self.eng = eng
        self.fn = fn
        self.deps = set()
        self.marked = False
        self.chainval = None
        self.kind = kind
        self.tag = tag
        self.idx = None
        self._acc_key = None


class Prog:

    def __init__(self):
        self.ops = []
        self.lw = {}
        self.readers = {}
        self.spaces = {}

    def add(self, eng, fn, reads=(), writes=(), kind="c", tag=""):
        op = Op(eng, fn, kind, tag)
        op.idx = len(self.ops)
        for k in list(reads) + list(writes):
            self.spaces.setdefault(k[0], set()).add(k)
        for k in reads:
            w = self.lw.get(k)
            if w is not None:
                op.deps.add(w)
            self.readers.setdefault(k, []).append(op)
        for k in writes:
            w = self.lw.get(k)
            if w is not None:
                op.deps.add(w)
            for r in self.readers.get(k, ()):
                op.deps.add(r)
            self.lw[k] = op
            self.readers[k] = []
        op.deps.discard(op)
        self.ops.append(op)
        return op

    def add_acc_first(self, eng, fn, reads=(), writes=(), tag=""):
        op = self.add(eng, fn, reads=reads, writes=writes, kind="c", tag=tag)
        for k in writes:
            op._acc_key = k
        return op

    def add_acc_next(self, eng, fn, reads=(), keys=(), tag=""):
        op = Op(eng, fn, "c", tag)
        op.idx = len(self.ops)
        for k in reads:
            w = self.lw.get(k)
            if w is not None:
                op.deps.add(w)
            self.readers.setdefault(k, []).append(op)
        for k in keys:
            self.lw[k] = op
            self.readers[k] = []
            op._acc_key = k
        op.deps.discard(op)
        self.ops.append(op)
        return op


class MK:
    def __init__(self, upto="all", taps=()):
        self.nc = bass.Bass("TRN2", target_bir_lowering=False)
        self.stack = contextlib.ExitStack()
        self.upto = upto
        self.taps = set(taps)
        self.dr = {}
        self.t = {}
        self.prog = Prog()

    def sbuf(self, name, shape, dtype):
        t = self.stack.enter_context(self.nc.sbuf_tensor(name, shape, dtype))
        self.t[name] = t
        return t

    def psumt(self, name, shape, dtype):
        t = self.stack.enter_context(self.nc.psum_tensor(name, shape, dtype))
        self.t[name] = t
        return t

    def dram(self, name, shape, dtype, kind="Internal"):
        ap = self.nc.dram_tensor(name, shape, dtype, kind=kind).ap()
        self.dr[name] = ap
        return ap

    def claim(self, *arenas, exclude=()):
        keys = []
        for a in arenas:
            sp = self.prog.spaces.get(a, set())
            take = {k for k in sp if not (len(k) > 1 and k[1] in exclude)}
            keys.extend(sorted(take, key=repr))
            self.prog.spaces[a] = sp - take
        return keys

    def c(self, eng, fn, reads=(), writes=(), tag=""):
        return self.prog.add(eng, fn, reads=reads, writes=writes, kind="c", tag=tag)

    def mm0(self, fn, reads=(), writes=(), tag=""):
        return self.prog.add_acc_first("te", fn, reads=reads, writes=writes, tag=tag)

    def mmn(self, fn, reads=(), keys=(), tag=""):
        return self.prog.add_acc_next("te", fn, reads=reads, keys=keys, tag=tag)

    def dma(self, eng, fn, reads=(), writes=(), tag=""):
        return self.prog.add(eng, fn, reads=reads, writes=writes, kind="d", tag=tag)

    def coll(self, fn, reads=(), writes=(), tag=""):
        return self.prog.add("gp", fn, reads=reads, writes=writes, kind="cc", tag=tag)

    NDMASEM = 8
    NCCSEM = 4

    def finalize(self):
        for op in self.prog.ops:
            for d in op.deps:
                d.marked = True
        cc_counts = {}
        dma_counts = {}
        for op in self.prog.ops:
            if op.kind == "c":
                if op.marked:
                    cc_counts[op.eng] = cc_counts.get(op.eng, 0) + 1
                    op.chainval = ("c", op.eng, cc_counts[op.eng])
            elif op.kind == "d":
                i = dma_counts.get(op.eng, 0)
                dma_counts[op.eng] = i + 1
                slot = i % self.NDMASEM
                rnd = i // self.NDMASEM
                op.chainval = ("d", op.eng, slot, 16 * (rnd + 1))
            elif op.kind == "cc":
                i = dma_counts.get("cc", 0)
                dma_counts["cc"] = i + 1
                slot = i % self.NCCSEM
                rnd = i // self.NCCSEM
                op.chainval = ("cc", "gp", slot, rnd + 1)
        sems = {}
        for e, n in cc_counts.items():
            if n:
                sems[("c", e)] = self.stack.enter_context(self.nc.semaphore(f"s_c_{e}"))
        for q, n in dma_counts.items():
            if q == "cc":
                for s in range(min(n, self.NCCSEM)):
                    sems[("cc", "gp", s)] = self.stack.enter_context(
                        self.nc.semaphore(f"s_cc_{s}"))
            else:
                for s in range(min(n, self.NDMASEM)):
                    sems[("d", q, s)] = self.stack.enter_context(
                        self.nc.semaphore(f"s_d_{q}_{s}"))
        self.chain_sems = sems
        finals = []
        for op in self.prog.ops:
            if op.kind in ("d", "cc") and op.chainval is not None:
                k, e, slot, val = op.chainval
                finals.append(((k, e, slot), val))
        fin = {}
        for key, val in finals:
            fin[key] = max(fin.get(key, 0), val)
        for e, n in cc_counts.items():
            fin[("c", e)] = n
        self.final_chain = fin

        nc = self.nc
        mkself = self

        def wait_key(op):
            cv = op.chainval
            if cv is None:
                return None
            if cv[0] == "c":
                return (("c", cv[1]), cv[2])
            return ((cv[0], cv[1], cv[2]), cv[3])

        with nc.Block() as block:
            for eng_name in ENGS:
                ops = [o for o in self.prog.ops if o.eng == eng_name]
                if not ops and eng_name != "sy":
                    continue

                def section(engobj, eng_name=eng_name, ops=ops):
                    watermark = {}
                    for op in ops:
                        if op.kind in ("d", "cc"):
                            cv = op.chainval
                            skey = (cv[0], cv[1], cv[2])
                            prev = cv[3] - (16 if cv[0] == "d" else 1)
                            if prev > 0 and watermark.get(skey, 0) < prev:
                                engobj.wait_ge(mkself.chain_sems[skey], prev)
                                watermark[skey] = prev
                        for dop in sorted(op.deps, key=lambda o: o.idx):
                            wk = wait_key(dop)
                            if wk is None:
                                continue
                            skey, val = wk
                            if watermark.get(skey, 0) >= val:
                                continue
                            engobj.wait_ge(mkself.chain_sems[skey], val)
                            watermark[skey] = val
                        instr = op.fn(engobj)
                        wk = wait_key(op)
                        if wk is not None:
                            assert instr is not None, f"marked op {op.tag} returned None"
                            instr.then_inc(mkself.chain_sems[wk[0]],
                                           16 if wk[0][0] == "d" else 1)
                    if eng_name == "sy":
                        for skey, v in mkself.final_chain.items():
                            if v and watermark.get(skey, 0) < v:
                                engobj.wait_ge(mkself.chain_sems[skey], v)

                getattr(block, {"te": "tensor", "ve": "vector", "sc": "scalar",
                                "gp": "gpsimd", "sy": "sync"}[eng_name])(section)
        return nc

    def emit(self):
        self.declare_io()
        self.alloc_tiles()
        self.ph_consts()
        self.ph_wquant()
        self.ph_lnquant()
        if self.upto not in ("m1", "m2"):
            self.ph_proj_kv()
        if self.upto in ("m4", "m5", "all"):
            self.ph_attn()
            self.ph_wo()
        if self.upto in ("m5", "all"):
            self.ph_ffn_pre()
            self.ph_routing()
        if self.upto == "all":
            self.ph_shared_gu()
            self.ph_experts()
            self.ph_return()
        else:
            self.ph_out_partial()
        with self.stack:
            return self.finalize()

    def declare_io(self):
        nc = self.nc
        d = self.dram
        self.x_in = nc.dram_tensor("x_in", [TPC, D], F32, kind="ExternalInput").ap()
        self.wcol = {m: nc.dram_tensor(f"w_{m}", [D, 256], F32, kind="ExternalInput").ap()
                     for m in "qkvo"}
        self.gate_in = nc.dram_tensor("gate_w", [NE, D], F32, kind="ExternalInput").ap()
        self.normw_in = nc.dram_tensor("norm_w", [1, D], F32, kind="ExternalInput").ap()
        self.w13_in = nc.dram_tensor("w13", [EPC, D, D], F32, kind="ExternalInput").ap()
        self.w2_in = nc.dram_tensor("w2", [EPC, 1024, D], F32, kind="ExternalInput").ap()
        self.shg_in = nc.dram_tensor("sh_g", [D, 1024], F32, kind="ExternalInput").ap()
        self.shu_in = nc.dram_tensor("sh_u", [D, 1024], F32, kind="ExternalInput").ap()
        self.shd_in = nc.dram_tensor("sh_d", [1024, D], F32, kind="ExternalInput").ap()
        self.rowbase_in = nc.dram_tensor("rowbase", [128, 1], F32, kind="ExternalInput").ap()
        self.attidx_in = nc.dram_tensor("attidx", [128, 288], I32, kind="ExternalInput").ap()
        self.avidx_in = nc.dram_tensor("avidx", [128, 32], I32, kind="ExternalInput").ap()
        self.exidx_in = nc.dram_tensor("exidx", [128, 64], I32, kind="ExternalInput").ap()
        self.hsel_in = nc.dram_tensor("hsel", [16, 4], F32, kind="ExternalInput").ap()
        self.esel_in = nc.dram_tensor("esel", [128, 64], F32, kind="ExternalInput").ap()
        self.out = nc.dram_tensor("out", [TPC, D], F32, kind="ExternalOutput").ap()
        d("ar_in", [1, 4], F32)
        d("ar_out", [1, 4], F32)
        for m in "qkvo":
            d(f"wt_{m}_in", [256, D], BF16)
            d(f"wt_{m}", [D, D], BF16)
        d("kT_in", [2 * NH * 128, 1024], BF16)
        d("kT_tab", [4 * 2 * NH * 128, 1024], BF16)
        d("qT_in", [2 * NH * 128, 1024], BF16)
        d("qT_tab", [4 * 2 * NH * 128, 1024], BF16)
        d("v_in", [2 * NH * 1024, 128], BF16)
        d("v_tab", [4 * 2 * NH * 1024, 128], BF16)
        d("nrm_in", [1, 32], F32)
        d("nrm_tab", [4, 32], F32)
        d("av_in", [S, 1024], BF16)
        d("av_tab", [4 * S, 1024], BF16)
        d("xr_in", [ROWS, 2 * D], BF16)
        d("xrtab", [GT, 2 * D], BF16)
        d("rt_in", [ROWS, 8], F32)
        d("rttab", [GT, 8], F32)
        d("glw_in", [NE * CAP + 128, 2], F32)
        d("glw", [NE * CAP + 128, 2], F32)
        d("rs_in", [GT, D], F32)
        d("rs_out", [ROWS, D], F32)
        d("xspill", [TPC, D], F32)
        d("oaspill", [TPC, D], F32)
        d("xrT_sp", [128, 2 * NH * 1024], BF16)
        d("act_sp", [TPC, 1024], F32)
        d("shared_sp", [TPC, D], F32)
        self.dbgt = {}
        for name, shape in [
            ("dbg_xqT", [16 * 128, TPC]), ("dbg_sxq", [128, NB]),
            ("dbg_x1", [TPC, D]), ("dbg_xr", [TPC, D]),
            ("dbg_rt", [TPC, 8]), ("dbg_av", [TPC, D]),
        ]:
            if name in self.taps:
                self.dbgt[name] = nc.dram_tensor(name, shape, F32, kind="ExternalOutput").ap()

    def alloc_tiles(self):
        s = self.sbuf
        s("ones128", [128, 128], BF16)
        s("onesf", [128, 128], F32)
        s("ltri", [128, 128], BF16)
        s("ident", [128, 128], BF16)
        s("identf", [128, 128], F32)
        s("zerob", [128, 128], BF16)
        s("epsln", [128, 1], F32)
        s("epsrms", [128, 1], F32)
        s("arA", [128, 24576], F32)
        s("arB", [128, 16384], F32)
        s("scr0", [128, D], F32)
        s("scr1", [128, D], F32)
        s("scrh", [128, D], FP16)
        s("scrb", [128, D], BF16)
        s("mvt", [128, 16, 512], BF16)
        s("qst", [128, 1024], BF16)
        s("ti32", [128, 256], I32)
        s("stats", [128, 32], F32)
        s("sxq", [128, NB], F32)
        s("dqrow", [128, NB, 4], F32)
        s("thr", [128, 4], F32)
        s("swg", [128, 4], F32)
        s("nrmloc", [128, 64], F32)
        s("mh", [128, 4], F32)
        s("gwt", [128, 16], F32)
        s("wloc", [128, 8], F32)
        s("gidx", [128, 8], I32)
        s("attix", [128, 288], I32)
        s("avix", [128, 32], I32)
        s("exix", [128, 64], I32)
        s("rowb", [128, 1], F32)
        s("hselsb", [16, 4], F32)
        s("eselsb", [128, 64], F32)
        self.psumt("ps_a", [128, 512], F32)
        self.psumt("ps_b", [128, 512], F32)
        self.psumt("ps_c", [128, 512], F32)
        self.psumt("ps_d", [128, 512], F32)
        self.psumt("ps_t0", [128, 128], BF16)
        self.psumt("ps_t1", [128, 128], BF16)
        self.psumt("ps_f0", [128, 128], F32)
        self.psumt("ps_r", [128, 32], F32)

    def abf(self, name):
        return self.t[name].ap().bitcast(BF16)

    def af32(self, name):
        return self.t[name].ap()

    def vw_mvt32(self):
        return self.t["mvt"].ap().rearrange("p a b -> p (a b)").bitcast(F32)

    def vw_glwv(self):
        return self.af32("arA")[:, 20992:21504]

    def vw_rtloc(self):
        return self.t["nrmloc"].ap()[:, 56:64]

    def vw_idsu(self):
        return self.t["nrmloc"].ap().bitcast(U32)[:, 44:52]

    def vw_maskt(self):
        return self.t["mvt"].ap().rearrange("p a b -> p (a b)")[:, 2048:4096]

    def vw_wslab(self, mi):
        return self.af32("arA")[:, 0:16384].rearrange("p (m j c) -> p m j c", m=4, j=16)[:, mi]

    def vw_pq32(self):
        return self.af32("arA")[:, 0:16384].rearrange("p (b d) -> p b d", b=NB)

    def vw_xres(self):
        return self.af32("arB").rearrange("p (b d) -> p b d", b=NB)

    def vw_wtern(self):
        return self.abf("arB")[:, 0:4096].rearrange("p (j c) -> p j c", j=16)

    def vw_wtsl(self):
        return self.abf("arB")[:, 4096:8192].rearrange("p (e o) -> p e o", e=2)

    def vw_xqT(self):
        return self.abf("arA")[:, 32768:49152].rearrange("p (j t) -> p j t", j=16)

    def vw_kTb(self):
        return self.t["scr1"].ap().bitcast(BF16)[:, 0:2048]

    def vw_pair(self, slot, which):
        base = slot * 24576 + {"q": 0, "k": 8192, "v": 16384}[which]
        return self.abf("arA")[:, base:base + 8192].rearrange("p (hl t) -> p hl t", hl=2)

    def vw_avloc(self):
        return self.abf("arA")[:, 0:32768].rearrange("p (b r hl c) -> p b r hl c",
                                                     b=NB, r=4, hl=2)

    def vw_oqT(self):
        return self.abf("arA")[:, 0:16384].rearrange("p (j t) -> p j t", j=16)

    def vw_xrT(self):
        return self.abf("arA")[:, 0:32768].rearrange("p (hl j t) -> p hl j t", hl=2, j=16)

    def vw_xrTB(self):
        return self.abf("arB")[:, 0:32768].rearrange("p (hl j t) -> p hl j t", hl=2, j=16)

    def ph_consts(self):
        t = self.t
        self.c("ve", lambda v: v.memset(t["ones128"][:], 1.0), writes=[("ones128",)])
        self.c("ve", lambda v: v.memset(t["onesf"][:], 1.0), writes=[("onesf",)])
        self.c("ve", lambda v: v.memset(t["zerob"][:], 0.0), writes=[("zerob",)])
        self.c("ve", lambda v: v.memset(t["epsln"][:], EPS_LN), writes=[("epsln",)])
        self.c("ve", lambda v: v.memset(t["epsrms"][:], EPS_RMS), writes=[("epsrms",)])
        self.c("gp", lambda g: g.affine_select(t["ident"][:], t["ones128"][:],
                                               pattern=[[1, 128]], compare_op=ALU.is_equal,
                                               fill=0.0, base=0, channel_multiplier=-1),
               reads=[("ones128",)], writes=[("ident",)])
        self.c("gp", lambda g: g.affine_select(t["ltri"][:], t["ones128"][:],
                                               pattern=[[1, 128]], compare_op=ALU.is_gt,
                                               fill=0.0, base=0, channel_multiplier=-1),
               reads=[("ones128",)], writes=[("ltri",)])
        self.c("ve", lambda v: v.tensor_copy(out=t["identf"][:], in_=t["ident"][:]),
               reads=[("ident",)], writes=[("identf",)])
        self.dma("sy", lambda sy: sy.dma_start(out=t["attix"][:], in_=self.attidx_in[:]),
                 writes=[("attix",)])
        self.dma("sy", lambda sy: sy.dma_start(out=t["avix"][:], in_=self.avidx_in[:]),
                 writes=[("avix",)])
        self.dma("sy", lambda sy: sy.dma_start(out=t["exix"][:], in_=self.exidx_in[:]),
                 writes=[("exix",)])
        self.dma("sy", lambda sy: sy.dma_start(out=t["rowb"][:], in_=self.rowbase_in[:]),
                 writes=[("rowb",)])
        self.dma("sy", lambda sy: sy.dma_start(out=t["hselsb"][:], in_=self.hsel_in[:]),
                 writes=[("hselsb",)])
        self.dma("sy", lambda sy: sy.dma_start(out=t["eselsb"][:], in_=self.esel_in[:]),
                 writes=[("eselsb",)])

    def ph_wquant(self):
        t = self.t
        dr = self.dr
        for m in "qkvo":
            mi = M_IDX[m]
            self.dma("sy", (lambda sy, mi=mi, m=m: sy.dma_start(
                out=self.vw_wslab(mi),
                in_=self.wcol[m].rearrange("(a p) b -> p a b", p=128))),
                writes=[("arA", "wsl", mi)], tag=f"ld_w{m}")
        for mi in range(4):
            for h in range(2):
                self.c("sc", (lambda sc, mi=mi, h=h: sc.activation(
                    t["scr1"][:, 0:D],
                    self.vw_wslab(mi).rearrange("p a b -> p (a b)")[:, D * h:D * (h + 1)],
                    AF.Abs, accum_out=t["stats"][:, 2 * mi + h:2 * mi + h + 1])),
                    reads=[("arA", "wsl", mi)], writes=[("scr1",), ("stats", "abs", mi, h)],
                    tag=f"abs{mi}{h}")
        self.c("ve", lambda v: v.tensor_tensor(
            t["stats"][:, 8:12], t["stats"][:, 0:8:2], t["stats"][:, 1:8:2], ALU.add),
            reads=[("stats", "abs", mi_, h_) for mi_ in range(4) for h_ in range(2)],
            writes=[("stats", "abss")], tag="abssum")
        self.mm0(lambda te: te.matmul(self.t["ps_r"][:, 0:4], t["onesf"][:],
                                      t["stats"][:, 8:12], start=True, stop=True),
                 reads=[("onesf",), ("stats", "abss")], writes=[("ps_r",)], tag="abs_mm")
        self.c("ve", lambda v: v.tensor_copy(out=t["stats"][:, 12:16], in_=self.t["ps_r"][:, 0:4]),
               reads=[("ps_r",)], writes=[("stats", "abst")])
        self.dma("gp", lambda g: g.dma_start(out=dr["ar_in"][:], in_=t["stats"][0:1, 12:16]),
                 reads=[("stats", "abst")], writes=[("dram:ar_in",)])
        self.coll(lambda g: g.collective_compute(
            "AllReduce", ALU.add, replica_groups=[list(range(NCORE))],
            ins=[dr["ar_in"][:]], outs=[dr["ar_out"][:]]),
            reads=[("dram:ar_in",)], writes=[("dram:ar_out",)], tag="ar_abs")
        self.dma("gp", lambda g: g.dma_start(out=t["stats"][0:1, 16:20], in_=dr["ar_out"][:]),
                 reads=[("dram:ar_out",)], writes=[("stats", "arout")])
        self.mm0(lambda te: te.matmul(self.t["ps_r"][:, 4:8], t["onesf"][0:1, :],
                                      t["stats"][0:1, 16:20], start=True, stop=True),
                 reads=[("onesf",), ("stats", "arout")], writes=[("ps_r", "bc")], tag="bc_mm")
        self.c("ve", lambda v: v.tensor_scalar(t["swg"][:], self.t["ps_r"][:, 4:8],
                                               1.0 / (D * D), 1e-5, ALU.mult, ALU.max),
               reads=[("ps_r", "bc")], writes=[("swg",)])
        self.c("ve", lambda v: v.tensor_scalar(t["thr"][:], t["swg"][:], 0.5, None, ALU.mult),
               reads=[("swg",)], writes=[("thr",)])
        for m in "kvqo":
            mi = M_IDX[m]
            wsl = self.vw_wslab(mi).rearrange("p a b -> p (a b)")
            wtf = self.vw_wtern().rearrange("p j c -> p (j c)")
            for h in range(2):
                self.c("ve", (lambda v, wsl=wsl, h=h, mi=mi: v.tensor_scalar(
                    t["scr0"][:, 0:D], wsl[:, D * h:D * (h + 1)],
                    t["thr"][:, mi:mi + 1], None, ALU.is_ge)),
                    reads=[("arA", "wsl", mi), ("thr",)], writes=[("scr0",)])
                self.c("ve", (lambda v, wsl=wsl, h=h, mi=mi: v.tensor_scalar(
                    t["scr1"][:, 0:D], wsl[:, D * h:D * (h + 1)], -1.0,
                    t["thr"][:, mi:mi + 1], ALU.mult, ALU.is_ge)),
                    reads=[("arA", "wsl", mi), ("thr",)], writes=[("scr1",)])
                self.c("ve", (lambda v, wtf=wtf, h=h: v.tensor_tensor(
                    wtf[:, D * h:D * (h + 1)], t["scr0"][:, 0:D], t["scr1"][:, 0:D],
                    ALU.subtract)),
                    reads=[("scr0",), ("scr1",)], writes=[("arB", "wtern", h)])
            for j in range(16):
                for eb in range(2):
                    ps = "ps_t0" if ((j * 2 + eb) % 2 == 0) else "ps_t1"
                    self.mm0((lambda te, j=j, eb=eb, ps=ps: te.transpose(
                        self.t[ps][:], self.vw_wtern()[:, j, 128 * eb:128 * (eb + 1)],
                        t["ident"][:])),
                        reads=[("arB", "wtern", eb), ("ident",)], writes=[(ps,)], tag="wtT")
                    self.c("ve", (lambda v, j=j, eb=eb, ps=ps: v.tensor_copy(
                        out=self.vw_wtsl()[:, eb, 128 * j:128 * (j + 1)], in_=self.t[ps][:])),
                        reads=[(ps,)], writes=[("arB", "wtsl", eb, j)])
            self.dma("gp", (lambda g, m=m: g.dma_start(
                out=dr[f"wt_{m}_in"].rearrange("(eb p) o -> p eb o", p=128),
                in_=self.vw_wtsl())),
                reads=[("arB", "wtsl", eb_, j_) for eb_ in range(2) for j_ in range(16)],
                writes=[(f"dram:wt_{m}_in",)])
            self.coll((lambda g, m=m: g.collective_compute(
                "AllGather", ALU.bypass, replica_groups=[list(range(NCORE))],
                ins=[dr[f"wt_{m}_in"][:]], outs=[dr[f"wt_{m}"][:]])),
                reads=[(f"dram:wt_{m}_in",)], writes=[(f"dram:wt_{m}",)], tag=f"ag_w{m}")

    def ph_lnquant(self):
        t = self.t
        xr = self.vw_xres()
        for b in range(NB):
            extra = self.claim("arB") if b == 0 else []
            self.dma("sy", (lambda sy, b=b, xr=xr: sy.dma_start(
                out=xr[:, b, :], in_=self.x_in[128 * b:128 * (b + 1), :])),
                writes=[("arB", "x", b)] + extra, tag=f"ld_x{b}")
        for b in range(NB):
            self.c("sc", (lambda sc, b=b, xr=xr: sc.activation(
                t["scr1"][:, 0:D], xr[:, b, :], AF.Identity,
                accum_out=t["stats"][:, b:b + 1])),
                reads=[("arB", "x", b)], writes=[("scr1",), ("stats", "m", b)], tag=f"mean{b}")
        for b in range(NB):
            self.c("ve", (lambda v, b=b: v.tensor_scalar(
                t["stats"][:, b:b + 1], t["stats"][:, b:b + 1], 1.0 / D, None, ALU.mult)),
                reads=[("stats", "m", b)], writes=[("stats", "m", b)])
            self.c("ve", (lambda v, b=b, xr=xr: v.tensor_scalar(
                t["scr0"][:, :], xr[:, b, :], t["stats"][:, b:b + 1], None, ALU.subtract)),
                reads=[("arB", "x", b), ("stats", "m", b)], writes=[("scr0",)], tag=f"d{b}")
            self.c("sc", (lambda sc, b=b: sc.activation(
                t["scr1"][:, 0:D], t["scr0"][:, :], AF.Square,
                accum_out=t["stats"][:, 8 + b:9 + b])),
                reads=[("scr0",)], writes=[("scr1",), ("stats", "ss", b)], tag=f"ssq{b}")
            self.c("sc", (lambda sc, b=b: sc.activation(
                t["stats"][:, 16 + b:17 + b], t["stats"][:, 8 + b:9 + b], AF.Sqrt,
                bias=t["epsln"][:], scale=1.0 / D)),
                reads=[("stats", "ss", b), ("epsln",)], writes=[("stats", "std", b)])
            self.c("ve", (lambda v, b=b: v.reciprocal(
                t["stats"][:, 24 + b:25 + b], t["stats"][:, 16 + b:17 + b])),
                reads=[("stats", "std", b)], writes=[("stats", "rstd", b)])
            self.c("ve", (lambda v, b=b: v.reduce_max(
                t["sxq"][:, b:b + 1], t["scr0"][:, :], axis=AXX, apply_absolute_value=True)),
                reads=[("scr0",)], writes=[("sxq", b)])
            self.c("ve", (lambda v, b=b: v.tensor_tensor(
                t["sxq"][:, b:b + 1], t["sxq"][:, b:b + 1], t["stats"][:, 24 + b:25 + b], ALU.mult)),
                reads=[("sxq", b), ("stats", "rstd", b)], writes=[("sxq", b)])
            self.c("ve", (lambda v, b=b: v.tensor_scalar(
                t["sxq"][:, b:b + 1], t["sxq"][:, b:b + 1], 1e-5, None, ALU.max)),
                reads=[("sxq", b)], writes=[("sxq", b)])
            self.c("ve", (lambda v, b=b: v.tensor_scalar(
                t["stats"][:, 16 + b:17 + b], t["sxq"][:, b:b + 1], 1.0 / 127.0, None, ALU.mult)),
                reads=[("sxq", b)], writes=[("stats", "isx", b)])
            self.c("ve", (lambda v, b=b: v.reciprocal(
                t["sxq"][:, b:b + 1], t["sxq"][:, b:b + 1])),
                reads=[("sxq", b)], writes=[("sxq", b)])
            self.c("ve", (lambda v, b=b: v.tensor_scalar(
                t["sxq"][:, b:b + 1], t["sxq"][:, b:b + 1], 127.0, None, ALU.mult)),
                reads=[("sxq", b)], writes=[("sxq", b)], tag=f"sx{b}")
            self.c("ve", (lambda v, b=b: v.tensor_tensor(
                t["stats"][:, 24 + b:25 + b], t["stats"][:, 24 + b:25 + b],
                t["sxq"][:, b:b + 1], ALU.mult)),
                reads=[("stats", "rstd", b), ("sxq", b)], writes=[("stats", "qsc", b)])
            self.c("ve", (lambda v, b=b: v.tensor_scalar(
                t["scrh"][:, :], t["scr0"][:, :], t["stats"][:, 24 + b:25 + b], RC,
                ALU.mult, ALU.add)),
                reads=[("scr0",), ("stats", "qsc", b)], writes=[("scrh",)])
            self.c("ve", (lambda v, b=b: v.tensor_scalar(
                t["scrb"][:, :], t["scrh"][:, :], RC, None, ALU.subtract)),
                reads=[("scrh",)], writes=[("scrb",)], tag=f"xq{b}")
            for j in range(16):
                ps = "ps_t0" if j % 2 == 0 else "ps_t1"
                self.mm0((lambda te, j=j, ps=ps: te.transpose(
                    self.t[ps][:], t["scrb"][:, 128 * j:128 * (j + 1)], t["ident"][:])),
                    reads=[("scrb",), ("ident",)], writes=[(ps,)], tag="xqT_t")
                self.c("ve", (lambda v, j=j, b=b, ps=ps: v.tensor_copy(
                    out=self.vw_xqT()[:, j, 128 * b:128 * (b + 1)], in_=self.t[ps][:])),
                    reads=[(ps,)], writes=[("arA", "xqT", j, b)])
            for mi in range(4):
                self.c("ve", (lambda v, b=b, mi=mi: v.tensor_tensor(
                    t["dqrow"][:, b, mi:mi + 1], t["stats"][:, 16 + b:17 + b],
                    t["swg"][:, mi:mi + 1], ALU.mult)),
                    reads=[("stats", "isx", b), ("swg",)], writes=[("dqrow", b, mi)],
                    tag=f"dq{b}{mi}")
            self.dma("gp", (lambda g, b=b, xr=xr: g.dma_start(
                out=self.dr["xspill"][128 * b:128 * (b + 1), :], in_=xr[:, b, :])),
                reads=[("arB", "x", b)], writes=[("dram:xspill", b)], tag=f"spx{b}")
        if "dbg_xqT" in self.taps:
            for j in range(16):
                self.dma("sy", (lambda sy, j=j: sy.dma_start(
                    out=self.dbgt["dbg_xqT"][128 * j:128 * (j + 1), :], in_=self.vw_xqT()[:, j, :])),
                    reads=[("arA", "xqT", j, b_) for b_ in range(NB)], writes=[("dram:dbgxqT", j)])
        if "dbg_sxq" in self.taps:
            self.dma("sy", lambda sy: sy.dma_start(out=self.dbgt["dbg_sxq"][:], in_=t["sxq"][:]),
                     reads=[("sxq", b_) for b_ in range(NB)], writes=[("dram:dbgsxq",)])

    def proj_matmuls(self, m, first_claim=False):
        t = self.t
        dr = self.dr
        mi = M_IDX[m]
        pq = self.vw_pq32()
        for dt_ in range(4):
            extra = (self.claim("arA", exclude=("xqT",)) if (first_claim and dt_ == 0) else [])
            for jh in range(2):
                self.dma("sy", (lambda sy, m=m, dt_=dt_, jh=jh: sy.dma_start(
                    out=t["mvt"][:, 8 * jh:8 * jh + 8, :],
                    in_=dr[f"wt_{m}"].rearrange("(a p) o -> p a o", p=128)
                        [:, 8 * jh:8 * jh + 8, 512 * dt_:512 * (dt_ + 1)])),
                    reads=[(f"dram:wt_{m}",)],
                    writes=[("mvt", "h", jh)] + (extra if jh == 0 else []),
                    tag=f"mv_{m}{dt_}{jh}")
            for b in range(NB):
                ps = ["ps_a", "ps_b", "ps_c", "ps_d"][b % 4]
                for j in range(16):
                    mmf = (lambda te, j=j, b=b, ps=ps: te.matmul(
                        self.t[ps][:], self.vw_xqT()[:, j, 128 * b:128 * (b + 1)],
                        t["mvt"][:, j, :], start=(j == 0), stop=(j == 15)))
                    if j == 0:
                        self.mm0(mmf, reads=[("arA", "xqT", j, b), ("mvt", "h", j // 8)],
                                 writes=[(ps,)], tag=f"pj_{m}{dt_}{b}")
                    else:
                        self.mmn(mmf, reads=[("arA", "xqT", j, b), ("mvt", "h", j // 8)],
                                 keys=[(ps,)])
                self.c("ve", (lambda v, b=b, dt_=dt_, ps=ps, pq=pq, mi=mi: v.tensor_scalar(
                    pq[:, b, 512 * dt_:512 * (dt_ + 1)], self.t[ps][:],
                    t["dqrow"][:, b, mi:mi + 1], None, ALU.mult)),
                    reads=[(ps,), ("dqrow", b, mi)], writes=[("arA", "pq", b, dt_)],
                    tag=f"{m}ev{b}{dt_}")
        return pq

    def split_hilo(self, src_ap, rd, tag):
        t = self.t
        kvh = t["scrb"].ap()
        kvl = t["scrh"].ap().bitcast(BF16)
        self.c("sc", (lambda sc, src_ap=src_ap, kvh=kvh: sc.activation(
            kvh[:, :], src_ap, AF.Identity)),
            reads=rd, writes=[("scrb",)], tag=f"{tag}hi")
        self.c("ve", (lambda v, src_ap=src_ap, kvh=kvh, kvl=kvl: v.tensor_tensor(
            kvl[:, :], src_ap, kvh[:, :], ALU.subtract)),
            reads=rd + [("scrb",)], writes=[("scrh",)], tag=f"{tag}lo")
        return kvh, kvl

    def head_norms(self, st, b, dstoff, tag):
        t = self.t
        rd = [("arA", "pq", b, d_) for d_ in range(4)]
        for h in range(NH):
            self.c("sc", (lambda sc, b=b, h=h, st=st: sc.activation(
                t["scr0"][:, 0:HD], st[:, b, HD * h:HD * (h + 1)], AF.Square,
                accum_out=t["nrmloc"][:, 32 + h:33 + h])),
                reads=rd, writes=[("scr0",), ("nrm", "tmp", h)], tag=f"{tag}{b}{h}")
        key = ("nrm", "kmax") if dstoff == 0 else ("nrm", "qmax")
        self.c("ve", (lambda v, b=b, dstoff=dstoff: (
            v.tensor_copy(out=t["nrmloc"][:, dstoff:dstoff + 16], in_=t["nrmloc"][:, 32:48])
            if b == 0 else
            v.tensor_tensor(t["nrmloc"][:, dstoff:dstoff + 16],
                            t["nrmloc"][:, dstoff:dstoff + 16],
                            t["nrmloc"][:, 32:48], ALU.max))),
            reads=[("nrm", "tmp", h_) for h_ in range(NH)] + ([key] if b else []),
            writes=[key], tag=f"{tag}m{b}")

    def kqT_store(self, name, b, tag):
        t = self.t
        dr = self.dr
        tab = dr[f"{name}_in"].rearrange("(hl j p) t -> p hl j t", hl=2, j=16)
        for hl, srck in ((0, "scrb"), (1, "scrh")):
            srcv = t["scrb"].ap() if hl == 0 else t["scrh"].ap().bitcast(BF16)
            for j in range(16):
                ps = "ps_t0" if j % 2 == 0 else "ps_t1"
                self.mm0((lambda te, j=j, srcv=srcv, ps=ps: te.transpose(
                    self.t[ps][:], srcv[:, 128 * j:128 * (j + 1)], t["ident"][:])),
                    reads=[(srck,), ("ident",)], writes=[(ps,)], tag=f"{tag}T")
                self.c("ve", (lambda v, j=j, ps=ps: v.tensor_copy(
                    out=self.vw_kTb()[:, 128 * j:128 * (j + 1)], in_=self.t[ps][:])),
                    reads=[(ps,)], writes=[("scr1", "kT", j)])
            self.dma("gp", (lambda g, b=b, hl=hl, tab=tab: g.dma_start(
                out=tab[:, hl, :, 128 * b:128 * (b + 1)],
                in_=self.vw_kTb().rearrange("p (j c) -> p j c", j=16))),
                reads=[("scr1", "kT", j_) for j_ in range(16)],
                writes=[(f"dram:{name}_in", b, hl)], tag=f"{tag}st{hl}{b}")

    def ph_proj_kv(self):
        t = self.t
        dr = self.dr
        grp = [[0, 1, 2, 3], [4, 5, 6, 7]]
        kst = self.proj_matmuls("k", first_claim=True)
        for b in range(NB):
            rd = [("arA", "pq", b, d_) for d_ in range(4)]
            self.head_norms(kst, b, 0, "kn")
            self.split_hilo(kst[:, b, :], rd, f"k{b}")
            self.kqT_store("kT", b, f"k{b}")
        self.coll(lambda g: g.collective_compute(
            "AllGather", ALU.bypass, replica_groups=grp,
            ins=[dr["kT_in"][:]], outs=[dr["kT_tab"][:]]),
            reads=[("dram:kT_in", b_, hl_) for b_ in range(NB) for hl_ in range(2)],
            writes=[("dram:kT_tab",)], tag="ag_k")
        vst = self.proj_matmuls("v")
        vtab = dr["v_in"].rearrange("(hl h b k) d -> k hl h b d", hl=2, h=16, b=NB)
        for b in range(NB):
            rd = [("arA", "pq", b, d_) for d_ in range(4)]
            kvh, kvl = self.split_hilo(vst[:, b, :], rd, f"v{b}")
            for hl, srck, src in ((0, "scrb", kvh), (1, "scrh", kvl)):
                self.dma("gp", (lambda g, b=b, hl=hl, src=src, vtab=vtab: g.dma_start(
                    out=vtab[:, hl, :, b, :],
                    in_=src.rearrange("p (h d) -> p h d", h=16))),
                    reads=[(srck,)], writes=[("dram:v_in", b, hl)])
        self.coll(lambda g: g.collective_compute(
            "AllGather", ALU.bypass, replica_groups=grp,
            ins=[dr["v_in"][:]], outs=[dr["v_tab"][:]]),
            reads=[("dram:v_in", b_, hl_) for b_ in range(NB) for hl_ in range(2)],
            writes=[("dram:v_tab",)], tag="ag_v")
        for b in range(NB):
            self.c("ve", (lambda v, b=b: v.tensor_scalar(
                t["dqrow"][:, b, 0:1], t["dqrow"][:, b, 0:1],
                1.0 / float(np.sqrt(HD)), None, ALU.mult)),
                reads=[("dqrow", b, 0)], writes=[("dqrow", b, 0)])
        qst = self.proj_matmuls("q")
        for b in range(NB):
            rd = [("arA", "pq", b, d_) for d_ in range(4)]
            self.head_norms(qst, b, 16, "qn")
            self.split_hilo(qst[:, b, :], rd, f"q{b}")
            self.kqT_store("qT", b, f"q{b}")
        self.coll(lambda g: g.collective_compute(
            "AllGather", ALU.bypass, replica_groups=grp,
            ins=[dr["qT_in"][:]], outs=[dr["qT_tab"][:]]),
            reads=[("dram:qT_in", b_, hl_) for b_ in range(NB) for hl_ in range(2)],
            writes=[("dram:qT_tab",)], tag="ag_q")
        self.mm0(lambda te: te.transpose(self.t["ps_f0"][0:32, :],
                                         t["nrmloc"][:, 0:32], t["identf"][:]),
                 reads=[("nrm", "kmax"), ("nrm", "qmax"), ("identf",)],
                 writes=[("ps_f0",)], tag="nrm_T")
        self.c("ve", lambda v: v.reduce_max(t["nrmloc"][0:32, 60:61], self.t["ps_f0"][0:32, :],
                                            axis=AXX),
               reads=[("ps_f0",)], writes=[("nrm", "red")], tag="nrm_red")
        self.dma("gp", lambda g: g.dma_start(out=dr["nrm_in"][0:1, 0:32],
                                             in_=t["nrmloc"][0:32, 60:61]),
                 reads=[("nrm", "red")], writes=[("dram:nrm_in",)])
        self.coll(lambda g: g.collective_compute(
            "AllGather", ALU.bypass, replica_groups=grp,
            ins=[dr["nrm_in"][:]], outs=[dr["nrm_tab"][:]]),
            reads=[("dram:nrm_in",)], writes=[("dram:nrm_tab",)], tag="ag_nrm")

    def attn_mh(self):
        t = self.t
        dr = self.dr
        self.dma("sy", lambda sy: sy.dma_start(out=t["nrmloc"][0:4, 0:32],
                                               in_=dr["nrm_tab"][:, :]),
                 reads=[("dram:nrm_tab",)], writes=[("nrm", "tab")], tag="ld_nrm")
        self.mm0(lambda te: te.transpose(self.t["ps_f0"][0:32, 0:4],
                                         t["nrmloc"][0:4, 0:32], t["identf"][0:4, 0:4]),
                 reads=[("nrm", "tab"), ("identf",)], writes=[("ps_f0",)], tag="mh_T1")
        self.c("ve", lambda v: v.reduce_max(t["nrmloc"][0:32, 32:33],
                                            self.t["ps_f0"][0:32, 0:4], axis=AXX),
               reads=[("ps_f0",)], writes=[("nrm", "red2")], tag="mh_red")
        self.mm0(lambda te: te.transpose(self.t["ps_f0"][0:1, 0:32],
                                         t["nrmloc"][0:32, 32:33], t["identf"][0:32, 0:32]),
                 reads=[("nrm", "red2"), ("identf",)], writes=[("ps_f0",)], tag="mh_T2")
        self.c("ve", lambda v: v.tensor_copy(out=t["scr0"][0:1, 0:32],
                                             in_=self.t["ps_f0"][0:1, 0:32]),
               reads=[("ps_f0",)], writes=[("scr0",)], tag="mh_cp")
        self.c("ve", lambda v: v.tensor_tensor(
            t["nrmloc"][0:1, 34:50], t["scr0"][0:1, 0:16],
            t["scr0"][0:1, 16:32], ALU.mult),
            reads=[("scr0",)], writes=[("nrm", "m2")], tag="mh_m2")
        self.c("sc", lambda sc: sc.activation(t["nrmloc"][0:1, 34:50],
                                              t["nrmloc"][0:1, 34:50], AF.Sqrt),
               reads=[("nrm", "m2")], writes=[("nrm", "m2")], tag="mh_sqrt")
        self.c("ve", lambda v: v.tensor_scalar(t["nrmloc"][0:1, 34:50],
                                               t["nrmloc"][0:1, 34:50], -1.0, None, ALU.mult),
               reads=[("nrm", "m2")], writes=[("nrm", "m2")], tag="mh_neg")
        self.mm0(lambda te: te.transpose(self.t["ps_f0"][0:16, 16:17],
                                         t["nrmloc"][0:1, 34:50], t["identf"][0:1, 0:1]),
                 reads=[("nrm", "m2"), ("identf",)], writes=[("ps_f0",)], tag="mh_T3")
        self.c("ve", lambda v: v.tensor_copy(out=t["nrmloc"][0:16, 52:53],
                                             in_=self.t["ps_f0"][0:16, 16:17]),
               reads=[("ps_f0",)], writes=[("nrm", "mcol")], tag="mh_c1")
        self.mm0(lambda te: te.matmul(self.t["ps_r"][0:4, 8:9], t["hselsb"][:, :],
                                      t["nrmloc"][0:16, 52:53], start=True, stop=True),
                 reads=[("hselsb",), ("nrm", "mcol")], writes=[("ps_r", "ms")], tag="mh_sel")
        self.c("ve", lambda v: v.tensor_copy(out=t["nrmloc"][0:4, 60:61],
                                             in_=self.t["ps_r"][0:4, 8:9]),
               reads=[("ps_r", "ms")], writes=[("nrm", "msel")], tag="mh_c2")
        self.mm0(lambda te: te.transpose(self.t["ps_f0"][0:1, 20:24],
                                         t["nrmloc"][0:4, 60:61], t["identf"][0:4, 0:4]),
                 reads=[("nrm", "msel"), ("identf",)], writes=[("ps_f0",)], tag="mh_T4")
        self.c("ve", lambda v: v.tensor_copy(out=t["nrmloc"][0:1, 56:60],
                                             in_=self.t["ps_f0"][0:1, 20:24]),
               reads=[("ps_f0",)], writes=[("nrm", "mrow")], tag="mh_c3")
        self.mm0(lambda te: te.matmul(self.t["ps_r"][:, 12:16], t["onesf"][0:1, :],
                                      t["nrmloc"][0:1, 56:60], start=True, stop=True),
                 reads=[("onesf",), ("nrm", "mrow")], writes=[("ps_r", "mb")], tag="mh_bc")
        self.c("ve", lambda v: v.tensor_copy(out=t["mh"][:], in_=self.t["ps_r"][:, 12:16]),
               reads=[("ps_r", "mb")], writes=[("mh",)], tag="mh_fin")

    def attn_pair_loads(self, pr, slot, extra):
        t = self.t
        dr = self.dr
        wkeys = []
        for which, tabn in (("q", "qT_tab"), ("k", "kT_tab")):
            dst = self.vw_pair(slot, which)
            for r in range(4):
                for hl in range(2):
                    col = pr * 8 + r * 2 + hl
                    k = ("arA", "pair", slot, which, hl, r)
                    self.dma("gp", (lambda g, dst=dst, r=r, hl=hl, col=col, tabn=tabn:
                                    g.indirect_dma_start(
                                        out=dst[:, hl, 1024 * r:1024 * (r + 1)],
                                        out_offset=None, in_=dr[tabn][:, :],
                                        in_offset=IndirectOffsetOnAxis(
                                            t["attix"][:, col:col + 1], axis=0))),
                             reads=[(f"dram:{tabn}",), ("attix",)] + (extra if not wkeys else []),
                             writes=[k], tag=f"ld{which}{pr}")
                    wkeys.append(k)
        dst = self.vw_pair(slot, "v")
        for r in range(4):
            for hl in range(2):
                for bi in range(8):
                    col = 32 + pr * 64 + r * 16 + hl * 8 + bi
                    k = ("arA", "pair", slot, "v", hl, r, bi)
                    self.dma("gp", (lambda g, dst=dst, r=r, hl=hl, bi=bi, col=col:
                                    g.indirect_dma_start(
                                        out=dst[:, hl, (r * 8 + bi) * 128:(r * 8 + bi) * 128 + 128],
                                        out_offset=None, in_=dr["v_tab"][:, :],
                                        in_offset=IndirectOffsetOnAxis(
                                            t["attix"][:, col:col + 1], axis=0))),
                             reads=[("dram:v_tab",), ("attix",)],
                             writes=[k], tag=f"ldv{pr}")
                    wkeys.append(k)
        return wkeys

    def attn_pair_compute(self, pr, slot):
        t = self.t
        dr = self.dr
        kp = self.vw_pair(slot, "k")
        qp = self.vw_pair(slot, "q")
        vp = self.vw_pair(slot, "v")
        scrhb = t["scrh"].ap().bitcast(BF16)
        qkeys = [("arA", "pair", slot, "q", hl_, r_) for hl_ in range(2) for r_ in range(4)]

        def emit_S(qc, kb):
            psS = "ps_a" if kb % 2 == 0 else "ps_b"
            co = COLOF[kb]
            kkey = [("arA", "pair", slot, "k", hl_, BLK_OWNER[kb]) for hl_ in range(2)]
            for pi, (hk, hq) in enumerate([(0, 0), (0, 1), (1, 0)]):
                sp = (pi == 2 and kb < 4 * qc)
                fn = (lambda te, hk=hk, hq=hq, co=co, psS=psS, pi=pi, kp=kp, sp=sp:
                      te.matmul(self.t[psS][:], kp[:, hk, co:co + 128],
                                t["qst"][:, hq * 512:(hq + 1) * 512],
                                start=(pi == 0), stop=sp))
                if pi == 0:
                    self.mm0(fn, reads=kkey + [("qst",)], writes=[(psS,)],
                             tag=f"S{pr}_{qc}_{kb}")
                else:
                    self.mmn(fn, reads=kkey + [("qst",)], keys=[(psS,)])
            if kb >= 4 * qc:
                i = kb - 4 * qc
                self.mmn((lambda te, i=i, psS=psS: te.matmul(
                    self.t[psS][:], t["ident"][:], self.vw_maskt()[:, i * 512:(i + 1) * 512],
                    start=False, stop=True)),
                    reads=[("ident",)] + [("maskt", i, cb_) for cb_ in range(4)],
                    keys=[(psS,)], tag=f"Sm{pr}_{qc}_{kb}")

        def emit_PAV(qc, kb, nkb):
            psS = "ps_a" if kb % 2 == 0 else "ps_b"
            co = COLOF[kb]
            pb = (kb % 2) * 512
            self.c("sc", (lambda sc, psS=psS, pb=pb: sc.activation(
                t["scr0"][:, pb:pb + 512], self.t[psS][:], AF.Exp,
                bias=t["mh"][:, pr:pr + 1])),
                reads=[(psS,), ("mh",)], writes=[("scr0",)], tag=f"ex{pr}_{qc}_{kb}")
            self.c("sc", (lambda sc, pb=pb: sc.activation(
                t["scrb"][:, pb:pb + 512], t["scr0"][:, pb:pb + 512], AF.Identity)),
                reads=[("scr0",)], writes=[("scrb",)], tag=f"ph{pr}_{qc}_{kb}")
            self.c("ve", (lambda v, pb=pb, scrhb=scrhb: v.tensor_tensor(
                scrhb[:, pb:pb + 512], t["scr0"][:, pb:pb + 512],
                t["scrb"][:, pb:pb + 512], ALU.subtract)),
                reads=[("scr0",), ("scrb",)], writes=[("scrh",)], tag=f"pl{pr}_{qc}_{kb}")
            vkey = [("arA", "pair", slot, "v", hl_, BLK_OWNER[kb], BLK_IDX[kb])
                    for hl_ in range(2)]
            for pi, (hv, pm) in enumerate([(0, "scrb"), (0, "scrh"), (1, "scrb")]):
                pmv = t["scrb"].ap() if pm == "scrb" else scrhb
                st = (kb == 0 and pi == 0)
                sp = (kb == nkb - 1 and pi == 2)
                fn = (lambda te, hv=hv, pmv=pmv, co=co, pb=pb, st=st, sp=sp, vp=vp:
                      te.matmul(self.t["ps_c"][:], vp[:, hv, co:co + 128],
                                pmv[:, pb:pb + 512], start=st, stop=sp))
                if st:
                    self.mm0(fn, reads=vkey + [(pm,)], writes=[("ps_c",)],
                             tag=f"AV{pr}_{qc}")
                else:
                    self.mmn(fn, reads=vkey + [(pm,)], keys=[("ps_c",)])
            for pi, pm in enumerate(("scrb", "scrh")):
                pmv = t["scrb"].ap() if pm == "scrb" else scrhb
                st = (kb == 0 and pi == 0)
                sp = (kb == nkb - 1 and pi == 1)
                fn = (lambda te, pmv=pmv, pb=pb, st=st, sp=sp: te.matmul(
                    self.t["ps_d"][0:1, :], t["ones128"][:, 0:1], pmv[:, pb:pb + 512],
                    start=st, stop=sp))
                if st:
                    self.mm0(fn, reads=[("ones128",), (pm,)], writes=[("ps_d",)],
                             tag=f"DN{pr}_{qc}")
                else:
                    self.mmn(fn, reads=[("ones128",), (pm,)], keys=[("ps_d",)])

        for qc in range(8):
            for hl in range(2):
                for i in range(4):
                    g = 4 * qc + i
                    self.c("ve", (lambda v, hl=hl, i=i, g=g, qp=qp: v.tensor_copy(
                        out=t["qst"][:, hl * 512 + i * 128:hl * 512 + (i + 1) * 128],
                        in_=qp[:, hl, COLOF[g]:COLOF[g] + 128])),
                        reads=qkeys, writes=[("qst",)], tag=f"qa{pr}{qc}")
            nkb = 4 * qc + 4
            emit_S(qc, 0)
            for kb in range(nkb):
                if kb + 1 < nkb:
                    emit_S(qc, kb + 1)
                emit_PAV(qc, kb, nkb)
            mv32 = self.vw_mvt32()
            self.c("ve", lambda v, mv32=mv32: v.tensor_copy(out=mv32[:, 0:512],
                                                            in_=self.t["ps_c"][:]),
                   reads=[("ps_c",)], writes=[("mvt",)], tag=f"avv{pr}{qc}")
            self.c("ve", lambda v, mv32=mv32: v.tensor_copy(out=mv32[0:1, 512:1024],
                                                            in_=self.t["ps_d"][0:1, :]),
                   reads=[("ps_d",)], writes=[("mvt", "dn")], tag=f"dnv{pr}{qc}")
            for cb in range(4):
                self.mm0((lambda te, cb=cb, mv32=mv32: te.transpose(
                    self.t["ps_f0"][:, cb:cb + 1],
                    mv32[0:1, 512 + cb * 128:512 + (cb + 1) * 128], t["identf"][0:1, 0:1])),
                    reads=[("mvt", "dn"), ("identf",)], writes=[("ps_f0",)], tag=f"dT{pr}{qc}")
            self.c("ve", lambda v: v.tensor_copy(out=t["nrmloc"][:, 48:52],
                                                 in_=self.t["ps_f0"][:, 0:4]),
                   reads=[("ps_f0",)], writes=[("nrm", "dn4")], tag=f"dc{pr}{qc}")
            self.c("ve", lambda v: v.reciprocal(t["nrmloc"][:, 52:56], t["nrmloc"][:, 48:52]),
                   reads=[("nrm", "dn4")], writes=[("nrm", "rdn")], tag=f"dr{pr}{qc}")
            for cb in range(4):
                self.mm0((lambda te, cb=cb, mv32=mv32: te.transpose(
                    self.t["ps_f0"][:], mv32[:, cb * 128:(cb + 1) * 128], t["identf"][:])),
                    reads=[("mvt",), ("identf",)], writes=[("ps_f0",)], tag=f"aT{pr}{qc}")
                self.c("ve", (lambda v, cb=cb: v.tensor_scalar(
                    t["scr1"][:, cb * 128:(cb + 1) * 128], self.t["ps_f0"][:],
                    t["nrmloc"][:, 52 + cb:53 + cb], None, ALU.mult)),
                    reads=[("ps_f0",), ("nrm", "rdn")], writes=[("scr1",)], tag=f"ae{pr}{qc}")
            self.c("sc", lambda sc: sc.activation(t["scrb"][:, 1024:1536],
                                                  t["scr1"][:, 0:512], AF.Identity),
                   reads=[("scr1",)], writes=[("scrb",)], tag=f"ah{pr}{qc}")
            self.c("ve", (lambda v, scrhb=scrhb: v.tensor_tensor(
                scrhb[:, 1024:1536], t["scr1"][:, 0:512], t["scrb"][:, 1024:1536],
                ALU.subtract)),
                reads=[("scr1",), ("scrb",)], writes=[("scrh",)], tag=f"al{pr}{qc}")
            avt = dr["av_in"].rearrange("(q cb p) c -> p q cb c", q=8, cb=4)
            for hl, srck in ((0, "scrb"), (1, "scrh")):
                src = (t["scrb"].ap() if hl == 0 else scrhb)[:, 1024:1536]
                self.dma("gp", (lambda g, hl=hl, src=src, qc=qc, avt=avt: g.dma_start(
                    out=avt[:, qc, :, hl * 512 + pr * 128:hl * 512 + (pr + 1) * 128],
                    in_=src.rearrange("p (cb d) -> p cb d", cb=4))),
                    reads=[(srck,)], writes=[("dram:av_in", pr, qc, hl)],
                    tag=f"avst{pr}{qc}")

    def ph_attn(self):
        dr = self.dr
        t = self.t
        self.attn_mh()
        mk_ = self.vw_maskt()
        first_m = True
        for i in range(4):
            for cb in range(4):
                dst = mk_[:, i * 512 + cb * 128: i * 512 + (cb + 1) * 128]
                wk = [("maskt", i, cb)] + ([("mvt",), ("mvt", "h", 0),
                                            ("mvt", "h", 1)] if first_m else [])
                first_m = False
                if cb < i:
                    self.c("ve", (lambda v, dst=dst: v.memset(dst, NEG)), writes=wk)
                elif cb > i:
                    self.c("ve", (lambda v, dst=dst: v.memset(dst, 0.0)), writes=wk)
                else:
                    self.c("gp", (lambda g, dst=dst: g.affine_select(
                        dst, t["zerob"][:], pattern=[[1, 128]], compare_op=ALU.is_ge,
                        fill=NEG, base=0, channel_multiplier=-1)),
                        reads=[("zerob",)], writes=wk)
        mv32 = self.vw_mvt32()
        self.c("ve", lambda v, mv32=mv32: v.memset(mv32[:, 2048:4096], 0.0),
               reads=[("mvt",)], writes=[("mvt", "z")], tag="z_mk")
        if self.upto == "all":
            for r in range(65):
                self.dma("sy", (lambda sy, r=r, mv32=mv32: sy.dma_start(
                    out=dr["rs_in"][128 * r:128 * (r + 1), :], in_=mv32[:, 2048:4096])),
                    reads=[("mvt", "z")], writes=[("dram:rs_in", "z", r)], tag=f"z_rs{r}")
        extra = self.claim("arA")
        self.attn_pair_loads(0, 0, extra)
        for pr in range(4):
            if pr < 3:
                self.attn_pair_loads(pr + 1, (pr + 1) % 2, [])
            self.attn_pair_compute(pr, pr % 2)
        self.coll(lambda g: g.collective_compute(
            "AllGather", ALU.bypass, replica_groups=[[0, 1, 2, 3], [4, 5, 6, 7]],
            ins=[dr["av_in"][:]], outs=[dr["av_tab"][:]]),
            reads=[("dram:av_in", pr_, qc_, hl_) for pr_ in range(4) for qc_ in range(8)
                   for hl_ in range(2)],
            writes=[("dram:av_tab",)], tag="ag_av")

    def ph_wo(self):
        t = self.t
        dr = self.dr
        avloc = self.vw_avloc()
        extra = self.claim("arA")
        for r in range(4):
            for b in range(NB):
                col = r * 8 + b
                k = ("arA", "avloc", r, b)
                self.dma("gp", (lambda g, r=r, b=b, col=col, avloc=avloc:
                                g.indirect_dma_start(
                                    out=avloc[:, b, r, :, :].rearrange("p hl c -> p (hl c)"),
                                    out_offset=None, in_=dr["av_tab"][:, :],
                                    in_offset=IndirectOffsetOnAxis(
                                        t["avix"][:, col:col + 1], axis=0))),
                         reads=[("dram:av_tab",), ("avix",)] + (extra if (r == 0 and b == 0) else []),
                         writes=[k], tag=f"ldav{r}{b}")
        av = self.vw_xres()
        extrab = self.claim("arB")
        for b in range(NB):
            for r in range(4):
                self.c("ve", (lambda v, b=b, r=r, av=av, avloc=avloc: v.tensor_tensor(
                    av[:, b, r * 512:(r + 1) * 512], avloc[:, b, r, 0, :],
                    avloc[:, b, r, 1, :], ALU.add)),
                    reads=[("arA", "avloc", r, b)],
                    writes=[("arB", "av", b, r)] + (extrab if (b == 0 and r == 0) else []),
                    tag=f"av{b}{r}")
        if "dbg_av" in self.taps:
            for b in range(NB):
                self.dma("sy", (lambda sy, b=b, av=av: sy.dma_start(
                    out=self.dbgt["dbg_av"][128 * b:128 * (b + 1), :], in_=av[:, b, :])),
                    reads=[("arB", "av", b, r_) for r_ in range(4)], writes=[("dram:dbgav", b)])
        oqT = self.vw_oqT()
        extra2 = self.claim("arA")
        for b in range(NB):
            rdav = [("arB", "av", b, r_) for r_ in range(4)]
            self.c("ve", (lambda v, b=b, av=av: v.reduce_max(
                t["sxq"][:, b:b + 1], av[:, b, :], axis=AXX, apply_absolute_value=True)),
                reads=rdav, writes=[("sxq", b)], tag=f"oam{b}")
            self.c("ve", (lambda v, b=b: v.tensor_scalar(
                t["sxq"][:, b:b + 1], t["sxq"][:, b:b + 1], 1e-5, None, ALU.max)),
                reads=[("sxq", b)], writes=[("sxq", b)])
            self.c("ve", (lambda v, b=b: v.tensor_scalar(
                t["stats"][:, 16 + b:17 + b], t["sxq"][:, b:b + 1], 1.0 / 127.0, None, ALU.mult)),
                reads=[("sxq", b)], writes=[("stats", "isx", b)])
            self.c("ve", (lambda v, b=b: v.reciprocal(
                t["sxq"][:, b:b + 1], t["sxq"][:, b:b + 1])),
                reads=[("sxq", b)], writes=[("sxq", b)])
            self.c("ve", (lambda v, b=b: v.tensor_scalar(
                t["sxq"][:, b:b + 1], t["sxq"][:, b:b + 1], 127.0, None, ALU.mult)),
                reads=[("sxq", b)], writes=[("sxq", b)], tag=f"osx{b}")
            self.c("ve", (lambda v, b=b, av=av: v.tensor_scalar(
                t["scrh"][:, :], av[:, b, :], t["sxq"][:, b:b + 1], RC, ALU.mult, ALU.add)),
                reads=rdav + [("sxq", b)], writes=[("scrh",)])
            self.c("ve", (lambda v, b=b: v.tensor_scalar(
                t["scrb"][:, :], t["scrh"][:, :], RC, None, ALU.subtract)),
                reads=[("scrh",)], writes=[("scrb",)], tag=f"oq{b}")
            for j in range(16):
                ps = "ps_t0" if j % 2 == 0 else "ps_t1"
                self.mm0((lambda te, j=j, ps=ps: te.transpose(
                    self.t[ps][:], t["scrb"][:, 128 * j:128 * (j + 1)], t["ident"][:])),
                    reads=[("scrb",), ("ident",)], writes=[(ps,)], tag="oqT_t")
                self.c("ve", (lambda v, j=j, b=b, ps=ps, oqT=oqT: v.tensor_copy(
                    out=oqT[:, j, 128 * b:128 * (b + 1)], in_=self.t[ps][:])),
                    reads=[(ps,)], writes=[("arA", "oqT", j, b)] +
                    (extra2 if (b == 0 and j == 0) else []))
            self.c("ve", (lambda v, b=b: v.tensor_tensor(
                t["dqrow"][:, b, 3:4], t["stats"][:, 16 + b:17 + b],
                t["swg"][:, 3:4], ALU.mult)),
                reads=[("stats", "isx", b), ("swg",)], writes=[("dqrow", b, 3)],
                tag=f"odq{b}")
        x1 = self.vw_xres()
        extrab2 = self.claim("arB")
        for b in range(NB):
            self.dma("sy", (lambda sy, b=b, x1=x1: sy.dma_start(
                out=x1[:, b, :], in_=dr["xspill"][128 * b:128 * (b + 1), :])),
                reads=[("dram:xspill", b)],
                writes=[("arB", "x1", b)] + (extrab2 if b == 0 else []), tag=f"ldx{b}")
        for dt_ in range(4):
            for jh in range(2):
                self.dma("sy", (lambda sy, dt_=dt_, jh=jh: sy.dma_start(
                    out=t["mvt"][:, 8 * jh:8 * jh + 8, :],
                    in_=dr["wt_o"].rearrange("(a p) o -> p a o", p=128)
                        [:, 8 * jh:8 * jh + 8, 512 * dt_:512 * (dt_ + 1)])),
                    reads=[("dram:wt_o",)],
                    writes=[("mvt", "h", jh)] +
                           ([("mvt",), ("mvt", "dn"), ("mvt", "z")] if jh == 0 else []),
                    tag=f"mv_o{dt_}{jh}")
            for b in range(NB):
                ps = ["ps_a", "ps_b", "ps_c", "ps_d"][b % 4]
                for j in range(16):
                    mmf = (lambda te, j=j, b=b, ps=ps, oqT=oqT: te.matmul(
                        self.t[ps][:], oqT[:, j, 128 * b:128 * (b + 1)],
                        t["mvt"][:, j, :], start=(j == 0), stop=(j == 15)))
                    if j == 0:
                        self.mm0(mmf, reads=[("arA", "oqT", j, b), ("mvt", "h", j // 8)],
                                 writes=[(ps,)], tag=f"oj{dt_}{b}")
                    else:
                        self.mmn(mmf, reads=[("arA", "oqT", j, b), ("mvt", "h", j // 8)],
                                 keys=[(ps,)])
                self.c("ve", (lambda v, b=b, ps=ps: v.tensor_scalar(
                    t["scr0"][:, 0:512], self.t[ps][:], t["dqrow"][:, b, 3:4], None, ALU.mult)),
                    reads=[(ps,), ("dqrow", b, 3)], writes=[("scr0",)], tag=f"oe{dt_}{b}")
                self.c("ve", (lambda v, b=b, dt_=dt_, x1=x1: v.tensor_tensor(
                    x1[:, b, 512 * dt_:512 * (dt_ + 1)], x1[:, b, 512 * dt_:512 * (dt_ + 1)],
                    t["scr0"][:, 0:512], ALU.add)),
                    reads=[("arB", "x1", b), ("scr0",)], writes=[("arB", "x1", b)],
                    tag=f"oa{dt_}{b}")
        if "dbg_x1" in self.taps:
            for b in range(NB):
                self.dma("sy", (lambda sy, b=b, x1=x1: sy.dma_start(
                    out=self.dbgt["dbg_x1"][128 * b:128 * (b + 1), :], in_=x1[:, b, :])),
                    reads=[("arB", "x1", b)], writes=[("dram:dbgx1", b)])

    def ph_ffn_pre(self):
        t = self.t
        dr = self.dr
        x1 = self.vw_xres()
        xrT = self.vw_xrT()
        mv32 = self.vw_mvt32()
        extra = self.claim("arA")
        self.dma("sy", lambda sy: sy.dma_start(out=t["scr0"][0:1, :], in_=self.normw_in[:]),
                 reads=[("scr0",)], writes=[("scr0",), ("tok_ffn",)] + extra, tag="ld_nw")
        nw = self.af32("arA")[:, 18432:20480]
        for cc in range(4):
            self.mm0((lambda te, cc=cc: te.matmul(
                self.t["ps_a"][:], t["onesf"][0:1, :], t["scr0"][0:1, 512 * cc:512 * (cc + 1)],
                start=True, stop=True)),
                reads=[("scr0",), ("onesf",), ("tok_ffn",)], writes=[("ps_a",)], tag=f"nwb{cc}")
            self.c("ve", (lambda v, cc=cc, nw=nw: v.tensor_copy(
                out=nw[:, 512 * cc:512 * (cc + 1)], in_=self.t["ps_a"][:])),
                reads=[("ps_a",)], writes=[("arA", "nw", cc)], tag=f"nwc{cc}")
        gw = self.af32("arA")[0:32, 16384:18432]
        gwT = self.af32("arA")[:, 20480:20992]
        self.dma("sy", (lambda sy, gw=gw: sy.dma_start(
            out=gw, in_=self.gate_in[:, :])),
            reads=[("tok_ffn",)], writes=[("arA", "gw")], tag="ld_gw")
        for j in range(16):
            self.mm0((lambda te, j=j, gw=gw: te.transpose(
                self.t["ps_f0"][:, 0:32], gw[:, 128 * j:128 * (j + 1)],
                t["identf"][0:32, 0:32])),
                reads=[("arA", "gw"), ("identf",)], writes=[("ps_f0",)], tag=f"gwT{j}")
            self.c("ve", (lambda v, j=j, gwT=gwT: v.tensor_copy(
                out=gwT[:, 32 * j:32 * (j + 1)], in_=self.t["ps_f0"][:, 0:32])),
                reads=[("ps_f0",)], writes=[("arA", "gwT", j)], tag=f"gwc{j}")
        self.c("ve", lambda v: v.memset(t["scrb"][:], 0.0),
               reads=[("tok_ffn",)], writes=[("scrb",)], tag="zjunk")
        for half in range(2):
            self.dma("gp", (lambda g, half=half: g.dma_start(
                out=dr["xr_in"][1024:1040, 2048 * half:2048 * (half + 1)],
                in_=t["scrb"][0:16, :])),
                reads=[("scrb",)], writes=[("dram:xr_in", "junk", half)], tag=f"zj{half}")
        self.c("ve", lambda v: v.memset(t["gwt"][:], 0.0),
               reads=[("tok_ffn",)], writes=[("gwt", "z")], tag="zrj0")
        self.dma("gp", lambda g: g.dma_start(out=dr["rt_in"][1024:1040, :],
                                             in_=t["gwt"][0:16, 0:8]),
                 reads=[("gwt", "z")], writes=[("dram:rt_in", "junk")], tag="zrj1")
        for b in range(NB):
            rdx = [("arB", "x1", b)]
            self.c("sc", (lambda sc, b=b, x1=x1: sc.activation(
                t["scr1"][:, 0:D], x1[:, b, :], AF.Identity,
                accum_out=t["stats"][:, b:b + 1])),
                reads=rdx, writes=[("scr1",), ("stats", "m", b)], tag=f"l2m{b}")
            self.c("ve", (lambda v, b=b: v.tensor_scalar(
                t["stats"][:, b:b + 1], t["stats"][:, b:b + 1], 1.0 / D, None, ALU.mult)),
                reads=[("stats", "m", b)], writes=[("stats", "m", b)])
            self.c("ve", (lambda v, b=b, x1=x1: v.tensor_scalar(
                t["scr0"][:, :], x1[:, b, :], t["stats"][:, b:b + 1], None, ALU.subtract)),
                reads=rdx + [("stats", "m", b)], writes=[("scr0",)], tag=f"l2d{b}")
            self.c("sc", (lambda sc, b=b: sc.activation(
                t["scr1"][:, 0:D], t["scr0"][:, :], AF.Square,
                accum_out=t["stats"][:, 8 + b:9 + b])),
                reads=[("scr0",)], writes=[("scr1",), ("stats", "ss", b)], tag=f"l2s{b}")
            self.c("sc", (lambda sc, b=b: sc.activation(
                t["stats"][:, 16 + b:17 + b], t["stats"][:, 8 + b:9 + b], AF.Sqrt,
                bias=t["epsln"][:], scale=1.0 / D)),
                reads=[("stats", "ss", b), ("epsln",)], writes=[("stats", "std", b)])
            self.c("ve", (lambda v, b=b: v.reciprocal(
                t["stats"][:, 24 + b:25 + b], t["stats"][:, 16 + b:17 + b])),
                reads=[("stats", "std", b)], writes=[("stats", "rstd", b)])
            self.c("ve", (lambda v, b=b: v.tensor_scalar(
                t["scr0"][:, :], t["scr0"][:, :], t["stats"][:, 24 + b:25 + b], None, ALU.mult)),
                reads=[("scr0",), ("stats", "rstd", b)], writes=[("scr0",)], tag=f"ln2{b}")
            self.c("ve", (lambda v, b=b, x1=x1: v.tensor_tensor(
                x1[:, b, :], x1[:, b, :], t["scr0"][:, :], ALU.add)),
                reads=rdx + [("scr0",)], writes=[("arB", "oacc", b)], tag=f"oac{b}")
            self.c("sc", (lambda sc, b=b: sc.activation(
                t["scr1"][:, 0:D], t["scr0"][:, :], AF.Square,
                accum_out=t["stats"][:, 8 + b:9 + b])),
                reads=[("scr0",)], writes=[("scr1",), ("stats", "ss", b)], tag=f"rms{b}")
            self.c("sc", (lambda sc, b=b: sc.activation(
                t["stats"][:, 16 + b:17 + b], t["stats"][:, 8 + b:9 + b], AF.Sqrt,
                bias=t["epsrms"][:], scale=1.0 / D)),
                reads=[("stats", "ss", b), ("epsrms",)], writes=[("stats", "std", b)])
            self.c("ve", (lambda v, b=b: v.reciprocal(
                t["stats"][:, 24 + b:25 + b], t["stats"][:, 16 + b:17 + b])),
                reads=[("stats", "std", b)], writes=[("stats", "rstd", b)])
            self.c("ve", (lambda v, b=b: v.tensor_scalar(
                t["scr1"][:, :], t["scr0"][:, :], t["stats"][:, 24 + b:25 + b], None, ALU.mult)),
                reads=[("scr0",), ("stats", "rstd", b)], writes=[("scr1",)], tag=f"xr0{b}")
            self.c("ve", (lambda v, b=b, nw=nw: v.tensor_tensor(
                t["scr1"][:, :], t["scr1"][:, :], nw[:, :], ALU.mult)),
                reads=[("scr1",)] + [("arA", "nw", cc_) for cc_ in range(4)],
                writes=[("scr1",)], tag=f"xr{b}")
            self.c("sc", lambda sc: sc.activation(t["scrb"][:], t["scr1"][:, :], AF.Identity),
                   reads=[("scr1",)], writes=[("scrb",)], tag=f"xh{b}")
            scrhb = t["scrh"].ap().bitcast(BF16)
            self.c("ve", (lambda v, scrhb=scrhb: v.tensor_tensor(
                scrhb[:, :], t["scr1"][:, :], t["scrb"][:], ALU.subtract)),
                reads=[("scr1",), ("scrb",)], writes=[("scrh",)], tag=f"xl{b}")
            for hl, srck in ((0, "scrb"), (1, "scrh")):
                src = t["scrb"].ap() if hl == 0 else scrhb
                self.dma("gp", (lambda g, b=b, hl=hl, src=src: g.dma_start(
                    out=dr["xr_in"][128 * b:128 * (b + 1), 2048 * hl:2048 * (hl + 1)],
                    in_=src[:, :])),
                    reads=[(srck,)], writes=[("dram:xr_in", b, hl)], tag=f"xst{b}{hl}")
                for j in range(16):
                    ps = "ps_t0" if j % 2 == 0 else "ps_t1"
                    self.mm0((lambda te, j=j, src=src, ps=ps: te.transpose(
                        self.t[ps][:], src[:, 128 * j:128 * (j + 1)], t["ident"][:])),
                        reads=[(srck,), ("ident",)], writes=[(ps,)], tag="xrT_t")
                    self.c("ve", (lambda v, j=j, b=b, hl=hl, ps=ps, xrT=xrT: v.tensor_copy(
                        out=xrT[:, hl, j, 128 * b:128 * (b + 1)], in_=self.t[ps][:])),
                        reads=[(ps,)], writes=[("arA", "xrT", hl, j, b)])
            self.c("ve", (lambda v, b=b, xrT=xrT, mv32=mv32: v.tensor_tensor(
                mv32[:, 0:2048].rearrange("p (j c) -> p j c", j=16),
                xrT[:, 0, :, 128 * b:128 * (b + 1)], xrT[:, 1, :, 128 * b:128 * (b + 1)],
                ALU.add)),
                reads=[("arA", "xrT", hl_, j_, b) for hl_ in range(2) for j_ in range(16)],
                writes=[("mvt",)], tag=f"x32{b}")
            for j in range(16):
                fn = (lambda te, j=j, gwT=gwT, mv32=mv32: te.matmul(
                    self.t["ps_r"][:, 0:32], mv32[:, 128 * j:128 * (j + 1)],
                    gwT[:, 32 * j:32 * (j + 1)], start=(j == 0), stop=(j == 15)))
                if j == 0:
                    self.mm0(fn, reads=[("mvt",), ("arA", "gwT", j)],
                             writes=[("ps_r", "lg")], tag=f"lg{b}")
                else:
                    self.mmn(fn, reads=[("mvt",), ("arA", "gwT", j)], keys=[("ps_r", "lg")])
            self.c("ve", lambda v: v.tensor_copy(out=t["nrmloc"][:, 0:32],
                                                 in_=self.t["ps_r"][:, 0:32]),
                   reads=[("ps_r", "lg")], writes=[("nrm", "lgv")], tag=f"lge{b}")
            self.c("ve", lambda v: v.reduce_max(t["nrmloc"][:, 40:41], t["nrmloc"][:, 0:32],
                                                axis=AXX),
                   reads=[("nrm", "lgv")], writes=[("nrm", "mx")], tag=f"lgm{b}")
            self.c("ve", lambda v: v.tensor_scalar(t["nrmloc"][:, 41:42], t["nrmloc"][:, 40:41],
                                                   -1.0, None, ALU.mult),
                   reads=[("nrm", "mx")], writes=[("nrm", "nmx")])
            self.c("sc", lambda sc: sc.activation(t["scr1"][:, 0:32], t["nrmloc"][:, 0:32],
                                                  AF.Exp, bias=t["nrmloc"][:, 41:42]),
                   reads=[("nrm", "lgv"), ("nrm", "nmx")], writes=[("scr1",)], tag=f"lge2{b}")
            self.c("ve", lambda v: v.max(t["nrmloc"][:, 32:40], t["scr1"][:, 0:32]),
                   reads=[("scr1",)], writes=[("nrm", "t8")], tag=f"t8{b}")
            self.c("ve", (lambda v: v.max_index(
                self.vw_idsu(), t["nrmloc"][:, 32:40], t["scr1"][:, 0:32])),
                reads=[("nrm", "t8"), ("scr1",)], writes=[("idsu",)], tag=f"mi{b}")
            self.c("ve", lambda v: v.tensor_copy(out=self.vw_rtloc()[:, 0:4],
                                                 in_=self.vw_idsu()[:, 0:4]),
                   reads=[("idsu",)], writes=[("rtloc", "ids")], tag=f"idc{b}")
            self.c("sc", lambda sc: sc.activation(t["scr1"][:, 32:36], t["nrmloc"][:, 32:36],
                                                  AF.Identity,
                                                  accum_out=t["nrmloc"][:, 42:43]),
                   reads=[("nrm", "t8")], writes=[("scr1",), ("nrm", "s4")], tag=f"s4{b}")
            self.c("ve", lambda v: v.reciprocal(t["nrmloc"][:, 43:44], t["nrmloc"][:, 42:43]),
                   reads=[("nrm", "s4")], writes=[("nrm", "rs4")])
            self.c("ve", lambda v: v.tensor_scalar(self.vw_rtloc()[:, 4:8], t["nrmloc"][:, 32:36],
                                                   t["nrmloc"][:, 43:44], None, ALU.mult),
                   reads=[("nrm", "t8"), ("nrm", "rs4")], writes=[("rtloc", "w")], tag=f"wt{b}")
            self.dma("gp", (lambda g, b=b: g.dma_start(
                out=dr["rt_in"][128 * b:128 * (b + 1), :], in_=self.vw_rtloc()[:, :])),
                reads=[("rtloc", "ids"), ("rtloc", "w")], writes=[("dram:rt_in", b)],
                tag=f"rst{b}")
        self.coll(lambda g: g.collective_compute(
            "AllGather", ALU.bypass, replica_groups=[list(range(NCORE))],
            ins=[dr["xr_in"][:]], outs=[dr["xrtab"][:]]),
            reads=[("dram:xr_in", b_, hl_) for b_ in range(NB) for hl_ in range(2)] +
                  [("dram:xr_in", "junk", hl_) for hl_ in range(2)],
            writes=[("dram:xrtab",)], tag="ag_xr")
        self.coll(lambda g: g.collective_compute(
            "AllGather", ALU.bypass, replica_groups=[list(range(NCORE))],
            ins=[dr["rt_in"][:]], outs=[dr["rttab"][:]]),
            reads=[("dram:rt_in", b_) for b_ in range(NB)] + [("dram:rt_in", "junk")],
            writes=[("dram:rttab",)], tag="ag_rt")
        for b in range(NB):
            self.dma("gp", (lambda g, b=b, x1=x1: g.dma_start(
                out=dr["oaspill"][128 * b:128 * (b + 1), :], in_=x1[:, b, :])),
                reads=[("arB", "oacc", b)], writes=[("dram:oaspill", b)], tag=f"spo{b}")
        self.dma("gp", lambda g: g.dma_start(
            out=dr["xrT_sp"].rearrange("p t -> p t"), in_=self.abf("arA")[:, 0:32768]),
            reads=[("arA", "xrT", hl_, j_, b_) for hl_ in range(2) for j_ in range(16)
                   for b_ in range(NB)],
            writes=[("dram:xrT_sp",)], tag="spxrT")
        if "dbg_xr" in self.taps:
            for b in range(NB):
                self.dma("gp", (lambda sy, b=b: sy.dma_start(
                    out=self.dbgt["dbg_xr"][128 * b:128 * (b + 1), :],
                    in_=dr["xr_in"][128 * b:128 * (b + 1), 0:2048])),
                    reads=[("dram:xr_in", b, 0)], writes=[("dram:dbgxr", b)])
        if "dbg_rt" in self.taps:
            for b in range(NB):
                self.dma("sy", (lambda sy, b=b: sy.dma_start(
                    out=self.dbgt["dbg_rt"][128 * b:128 * (b + 1), :],
                    in_=dr["rt_in"][128 * b:128 * (b + 1), :])),
                    reads=[("dram:rt_in", b)], writes=[("dram:dbgrt", b)])

    def ph_routing(self):
        t = self.t
        dr = self.dr
        A = self.af32("arA")
        extra = self.claim("arA")
        first = True
        for cc in range(NCORE):
            s_ = cc // 4
            o = cc % 4
            for rn in range(2):
                pbase = s_ * 64 + (8 * o if rn == 0 else 56 - 8 * o)
                src = dr["rttab"][ROWS * cc + 512 * rn:ROWS * cc + 512 * rn + 512, :]
                self.dma("sy", (lambda sy, pbase=pbase, src=src, A=A: sy.dma_start(
                    out=A[pbase:pbase + 8, 0:512],
                    in_=src.rearrange("(pp i) k -> pp (i k)", pp=8))),
                    reads=[("dram:rttab",)],
                    writes=[("arA", "rt", "raw", cc, rn)] + (extra + [("tok_rt",)] if first else []),
                    tag=f"rtl{cc}{rn}")
                first = False
        rawk = [("arA", "rt", "raw", cc_, rn_) for cc_ in range(NCORE) for rn_ in range(2)]
        rv = A[:, 0:512].rearrange("p (i k) -> p i k", i=64)
        self.c("ve", lambda v, rv=rv, A=A: v.tensor_copy(
            out=A[:, 512:768].rearrange("p (i k) -> p i k", i=64), in_=rv[:, :, 0:4]),
            reads=rawk, writes=[("arA", "rt", "ids")], tag="ids")
        self.c("ve", lambda v, rv=rv, A=A: v.tensor_copy(
            out=A[:, 768:1024].rearrange("p (i k) -> p i k", i=64), in_=rv[:, :, 4:8]),
            reads=rawk, writes=[("arA", "rt", "wts")], tag="wts")
        self.c("ve", lambda v, A=A: v.memset(A[:, 1024:1280], 0.0),
               reads=[("tok_rt",)], writes=[("arA", "rt", "z")], tag="rz")
        self.c("ve", lambda v, A=A: v.memset(A[:, 2048:2304], 0.0),
               reads=[("tok_rt",)], writes=[("arA", "rt", "tgt")], tag="tz")
        self.c("ve", lambda v, A=A: v.memset(A[:, 2304:2560], 0.0),
               reads=[("tok_rt",)], writes=[("arA", "rt", "ovr")], tag="oz")
        self.c("gp", lambda g, A=A: g.iota(A[:, 3328:3584], pattern=[[1, 64], [0, 4]],
                                           base=0, channel_multiplier=0,
                                           allow_small_or_imprecise_dtypes=True),
               reads=[("tok_rt",)], writes=[("arA", "rt", "iota")], tag="iota")
        self.c("gp", lambda g, A=A: g.iota(A[:, 3072:3328], pattern=[[0, 256]],
                                           base=NE * CAP, channel_multiplier=1,
                                           allow_small_or_imprecise_dtypes=True),
               reads=[("tok_rt",)], writes=[("arA", "rt", "junk")], tag="junkr")
        self.c("ve", lambda v, A=A: v.tensor_scalar(
            A[:, 2816:3072], A[:, 3328:3584], t["rowb"][:], None, ALU.add),
            reads=[("arA", "rt", "iota"), ("rowb",)], writes=[("arA", "rt", "gv")], tag="gvals")
        for gg in range(NE // 4):
            for i in range(4):
                e = 4 * gg + i
                self.c("ve", (lambda v, e=e, i=i, A=A: v.tensor_scalar(
                    A[:, 4096 + 256 * i:4096 + 256 * (i + 1)], A[:, 512:768],
                    float(e), None, ALU.is_equal)),
                    reads=[("arA", "rt", "ids")], writes=[("arA", "rt", "m4", i)],
                    tag=f"msk{e}")
            for i in range(4):
                self.c("ve", (lambda v, i=i, A=A: v.tensor_tensor_scan(
                    A[:, 5120 + 256 * i:5120 + 256 * (i + 1)],
                    A[:, 4096 + 256 * i:4096 + 256 * (i + 1)],
                    A[:, 1024:1280], 0.0, ALU.add, ALU.add)),
                    reads=[("arA", "rt", "m4", i), ("arA", "rt", "z")],
                    writes=[("arA", "rt", "p4", i)], tag=f"scan{4 * gg + i}")
            for i in range(4):
                self.c("sc", (lambda sc, i=i, A=A: sc.activation(
                    t["scr0"][:, 0:256], A[:, 4096 + 256 * i:4096 + 256 * (i + 1)],
                    AF.Identity, accum_out=A[:, 3588 + i:3589 + i])),
                    reads=[("arA", "rt", "m4", i)],
                    writes=[("scr0",), ("arA", "rt", "s4", i)], tag=f"rs{4 * gg + i}")
            self.c("ve", lambda v, A=A: v.tensor_copy(out=t["scrb"][:, 0:4],
                                                      in_=A[:, 3588:3592]),
                   reads=[("arA", "rt", "s4", i_) for i_ in range(4)],
                   writes=[("scrb",)], tag=f"sb16{gg}")
            self.mm0((lambda te: te.matmul(self.t["ps_r"][:, 16:20], t["ltri"][:],
                                           t["scrb"][:, 0:4], start=True, stop=True)),
                     reads=[("ltri",), ("scrb",)], writes=[("ps_r", "sb")], tag=f"sbm{gg}")
            self.c("ve", lambda v, A=A: v.tensor_copy(out=A[:, 3584:3588],
                                                      in_=self.t["ps_r"][:, 16:20]),
                   reads=[("ps_r", "sb")], writes=[("arA", "rt", "sbase")], tag=f"sbc{gg}")
            for i in range(4):
                e = 4 * gg + i
                mreg = A[:, 4096 + 256 * i:4096 + 256 * (i + 1)]
                preg = A[:, 5120 + 256 * i:5120 + 256 * (i + 1)]
                self.c("ve", (lambda v, mreg=mreg, preg=preg, A=A: v.tensor_tensor(
                    A[:, 2560:2816], preg, mreg, ALU.subtract)),
                    reads=[("arA", "rt", "p4", i), ("arA", "rt", "m4", i)],
                    writes=[("arA", "rt", "pos")], tag=f"p0{e}")
                self.c("ve", (lambda v, i=i, A=A: v.tensor_scalar(
                    A[:, 2560:2816], A[:, 2560:2816], A[:, 3584 + i:3585 + i], None,
                    ALU.add)),
                    reads=[("arA", "rt", "pos"), ("arA", "rt", "sbase")],
                    writes=[("arA", "rt", "pos")], tag=f"p1{e}")
                self.c("ve", (lambda v, A=A: v.tensor_scalar(
                    t["scr0"][:, 256:512], A[:, 2560:2816], float(CAP), None, ALU.is_ge)),
                    reads=[("arA", "rt", "pos")], writes=[("scr0",)], tag=f"ov{e}")
                self.c("ve", (lambda v, e=e, A=A: v.tensor_scalar(
                    t["scr1"][:, 0:256], A[:, 2560:2816], float(CAP - 1), float(e * CAP),
                    ALU.min, ALU.add)),
                    reads=[("arA", "rt", "pos")], writes=[("scr1",)], tag=f"tg{e}")
                self.c("ve", (lambda v, mreg=mreg: v.tensor_tensor(
                    t["scr1"][:, 256:512], mreg, t["scr1"][:, 0:256], ALU.mult)),
                    reads=[("arA", "rt", "m4", i), ("scr1",)], writes=[("scr1",)],
                    tag=f"tm{e}")
                self.c("ve", lambda v, A=A: v.tensor_tensor(
                    A[:, 2048:2304], A[:, 2048:2304], t["scr1"][:, 256:512], ALU.add),
                    reads=[("arA", "rt", "tgt"), ("scr1",)], writes=[("arA", "rt", "tgt")],
                    tag=f"ta{e}")
                self.c("ve", (lambda v, mreg=mreg: v.tensor_tensor(
                    t["scr0"][:, 512:768], mreg, t["scr0"][:, 256:512], ALU.mult)),
                    reads=[("arA", "rt", "m4", i), ("scr0",)], writes=[("scr0",)],
                    tag=f"om{e}")
                self.c("ve", lambda v, A=A: v.tensor_tensor(
                    A[:, 2304:2560], A[:, 2304:2560], t["scr0"][:, 512:768], ALU.add),
                    reads=[("arA", "rt", "ovr"), ("scr0",)], writes=[("arA", "rt", "ovr")],
                    tag=f"oa{e}")
        self.c("ve", lambda v, A=A: v.tensor_tensor(
            t["scr0"][:, 0:256], A[:, 3072:3328], A[:, 2048:2304], ALU.subtract),
            reads=[("arA", "rt", "junk"), ("arA", "rt", "tgt")], writes=[("scr0",)], tag="j0")
        self.c("ve", lambda v, A=A: v.tensor_tensor(
            t["scr0"][:, 256:512], A[:, 2304:2560], t["scr0"][:, 0:256], ALU.mult),
            reads=[("arA", "rt", "ovr"), ("scr0",)], writes=[("scr0",)], tag="j1")
        self.c("ve", lambda v, A=A: v.tensor_tensor(
            A[:, 2048:2304], A[:, 2048:2304], t["scr0"][:, 256:512], ALU.add),
            reads=[("arA", "rt", "tgt"), ("scr0",)], writes=[("arA", "rt", "tgt")], tag="j2")
        payloads = [("gv", 2816), ("wts", 768), ("tgt", 2048)]
        dsts = {"gv": 3970, "wts": 4002, "tgt": 4034}
        for nm, off in payloads:
            for ch in range(2):
                self.mm0((lambda te, off=off, ch=ch, A=A: te.transpose(
                    self.t["ps_f0"][:], A[:, off + 128 * ch:off + 128 * (ch + 1)],
                    t["identf"][:])),
                    reads=[("arA", "rt", nm)], writes=[("ps_f0",)], tag=f"cx{nm}{ch}")
                self.c("ve", (lambda v, ch=ch, A=A: v.tensor_copy(
                    out=A[:, 3586 + 128 * ch:3586 + 128 * (ch + 1)], in_=self.t["ps_f0"][:])),
                    reads=[("ps_f0",)], writes=[("arA", "rt", "xT", ch)], tag=f"cxe{nm}{ch}")
            for ch in range(2):
                fn = (lambda te, ch=ch, A=A: te.matmul(
                    self.t["ps_f0"][0:32, :],
                    t["eselsb"][:, 32 * ch:32 * (ch + 1)],
                    A[:, 3586 + 128 * ch:3586 + 128 * (ch + 1)],
                    start=(ch == 0), stop=(ch == 1)))
                if ch == 0:
                    self.mm0(fn, reads=[("eselsb",), ("arA", "rt", "xT", 0),
                                        ("arA", "rt", "xT", 1)],
                             writes=[("ps_f0",)], tag=f"cs{nm}")
                else:
                    self.mmn(fn, reads=[("eselsb",), ("arA", "rt", "xT", 0),
                                        ("arA", "rt", "xT", 1)], keys=[("ps_f0",)])
            self.c("ve", (lambda v, A=A: v.tensor_copy(
                out=A[0:32, 3842:3970], in_=self.t["ps_f0"][0:32, :])),
                reads=[("ps_f0",)], writes=[("arA", "rt", "gT")], tag=f"cge{nm}")
            self.mm0((lambda te, A=A: te.transpose(
                self.t["ps_f0"][:, 0:32], A[0:32, 3842:3970], t["identf"][0:32, 0:32])),
                reads=[("arA", "rt", "gT")], writes=[("ps_f0",)], tag=f"cgt{nm}")
            self.c("ve", (lambda v, nm=nm, A=A, dsts=dsts: v.tensor_copy(
                out=A[:, dsts[nm]:dsts[nm] + 32], in_=self.t["ps_f0"][:, 0:32])),
                reads=[("ps_f0",)], writes=[("arA", "rt", "c", nm)], tag=f"cf{nm}")
        self.c("ve", lambda v, A=A: v.tensor_copy(out=t["ti32"][:, 0:32],
                                                  in_=A[:, 4034:4066]),
               reads=[("arA", "rt", "c", "tgt")], writes=[("ti32",)], tag="ti32")
        gl = self.vw_glwv().rearrange("p (f b) -> p f b", b=2)
        self.c("ve", lambda v, A=A, gl=gl: v.tensor_copy(
            out=gl[:, 0:32, 0:1], in_=A[:, 3970:4002].rearrange("p (f o) -> p f o", o=1)),
            reads=[("arA", "rt", "c", "gv")], writes=[("arA", "rt", "glwv")], tag="glv0")
        self.c("ve", lambda v, A=A, gl=gl: v.tensor_copy(
            out=gl[:, 0:32, 1:2], in_=A[:, 4002:4034].rearrange("p (f o) -> p f o", o=1)),
            reads=[("arA", "rt", "wts"), ("arA", "rt", "c", "wts")],
            writes=[("arA", "rt", "glwv")], tag="glv1")
        mv32 = self.vw_mvt32()
        self.c("ve", lambda v, mv32=mv32: v.memset(mv32[:, 0:1026], 0.0),
               reads=[("mvt",)], writes=[("mvt", "sf")], tag="sf0")
        self.dma("gp", lambda g, mv32=mv32: g.dma_start(
            out=dr["glw_in"].rearrange("(p a) b -> p (a b)", p=128), in_=mv32[:, 0:1026]),
            reads=[("mvt", "sf")], writes=[("dram:glw_in",)], tag="glwz")
        prev = ("dram:glw_in",)
        for f in range(32):
            k = ("dram:glw_in", "s", f)
            self.dma("gp", (lambda g, f=f, gl=gl: g.indirect_dma_start(
                out=dr["glw_in"][:, :],
                out_offset=IndirectOffsetOnAxis(t["ti32"][:, f:f + 1], axis=0),
                in_=gl[:, f, :], in_offset=None)),
                reads=[("ti32",), ("arA", "rt", "glwv"), prev],
                writes=[k], tag=f"gsc{f}")
            prev = k
        self.coll(lambda g: g.collective_compute(
            "AllReduce", ALU.add, replica_groups=[list(range(NCORE))],
            ins=[dr["glw_in"][:]], outs=[dr["glw"][:]]),
            reads=[prev], writes=[("dram:glw",)], tag="ar_glw")

    def ph_shared_gu(self):
        t = self.t
        dr = self.dr
        mv32 = self.vw_mvt32()
        xrT = self.vw_xrTB()
        extrab = self.claim("arB")
        self.dma("sy", lambda sy: sy.dma_start(
            out=self.abf("arB")[:, 0:32768], in_=dr["xrT_sp"][:, :]),
            reads=[("dram:xrT_sp",)], writes=[("arB", "xrT"), ("tok_sh",)] + extrab,
            tag="ldxrT")
        slab = self.abf("arA")[:, 0:32768].rearrange("p (m hl j c) -> p m hl j c",
                                                     m=2, hl=2, j=16)
        extra = self.claim("arA", exclude=("rt",))
        for ic in range(2):
            for mi_, wsrc in ((0, self.shg_in), (1, self.shu_in)):
                for qh in range(4):
                    self.dma("sy", (lambda sy, ic=ic, wsrc=wsrc, qh=qh, mv32=mv32:
                                    sy.dma_start(
                                        out=mv32[:, 0:2048].rearrange(
                                            "p (j f) -> p j f", j=4),
                                        in_=wsrc[512 * qh:512 * (qh + 1),
                                                 512 * ic:512 * (ic + 1)]
                                        .rearrange("(j p) f -> p j f", p=128))),
                             reads=[("tok_sh",)],
                             writes=[("mvt",), ("mvt", "sf"), ("mvt", "z"), ("mvt", "dn")] +
                                    (extra if (ic == 0 and mi_ == 0 and qh == 0) else []),
                             tag=f"shw{ic}{mi_}{qh}")
                    hi = slab[:, mi_, 0, 4 * qh:4 * qh + 4, :].rearrange("p j c -> p (j c)")
                    lo = slab[:, mi_, 1, 4 * qh:4 * qh + 4, :].rearrange("p j c -> p (j c)")
                    self.c("ve", (lambda v, hi=hi, mv32=mv32: v.tensor_copy(
                        out=hi, in_=mv32[:, 0:2048])),
                        reads=[("mvt",)], writes=[("arA", "sl", mi_, 0, qh)], tag="shh")
                    self.c("ve", (lambda v, hi=hi, lo=lo, mv32=mv32: v.tensor_tensor(
                        lo, mv32[:, 0:2048], hi, ALU.subtract)),
                        reads=[("mvt",), ("arA", "sl", mi_, 0, qh)],
                        writes=[("arA", "sl", mi_, 1, qh)], tag="shl")
            xkeys = [("arB", "xrT")]
            for b in range(NB):
                psg, psu = (("ps_a", "ps_b") if b % 2 == 0 else ("ps_c", "ps_d"))
                for mi_, psn in ((0, psg), (1, psu)):
                    first = True
                    for j in range(16):
                        slkeys = [("arA", "sl", mi_, 0, j // 4), ("arA", "sl", mi_, 1, j // 4)]
                        for hx, hw in ((0, 0), (0, 1), (1, 0)):
                            fn = (lambda te, b=b, j=j, hx=hx, hw=hw, mi_=mi_, psn=psn,
                                  xrT=xrT, slab=slab, first=first:
                                  te.matmul(self.t[psn][:],
                                            xrT[:, hx, j, 128 * b:128 * (b + 1)],
                                            slab[:, mi_, hw, j, :],
                                            start=first, stop=(j == 15 and hx == 1)))
                            if first:
                                self.mm0(fn, reads=xkeys + slkeys, writes=[(psn,)],
                                         tag=f"sg{ic}{b}{mi_}")
                                first = False
                            else:
                                self.mmn(fn, reads=xkeys + slkeys, keys=[(psn,)])
                self.c("sc", (lambda sc, psg=psg: sc.activation(
                    t["scr0"][:, 0:512], self.t[psg][:], AF.Silu)),
                    reads=[(psg,)], writes=[("scr0",)], tag=f"sil{ic}{b}")
                self.c("ve", (lambda v, psu=psu: v.tensor_tensor(
                    t["scr1"][:, 0:512], t["scr0"][:, 0:512], self.t[psu][:], ALU.mult)),
                    reads=[("scr0",), (psu,)], writes=[("scr1",)], tag=f"au{ic}{b}")
                self.dma("sy", (lambda sy, ic=ic, b=b: sy.dma_start(
                    out=dr["act_sp"][128 * b:128 * (b + 1), 512 * ic:512 * (ic + 1)],
                    in_=t["scr1"][:, 0:512])),
                    reads=[("scr1",)], writes=[("dram:act_sp", b, ic)], tag=f"acst{ic}{b}")

    def ph_experts(self):
        t = self.t
        dr = self.dr
        self._last_rs = None
        mv32 = self.vw_mvt32()
        bufv = self.abf("arB")[:, 0:32768].rearrange("p (sb c) -> p sb c", sb=8)
        bufT = self.abf("arA")[:, 0:32768].rearrange("p (hl hc s) -> p hl hc s", hl=2, hc=16)
        sl13 = self.abf("arA")[:, 32768:49152].rearrange("p (hl j c) -> p hl j c", hl=2, j=16)
        actT = self.abf("arA")[:, 0:16384].rearrange("p (hl ic s) -> p hl ic s", hl=2, ic=8)
        sl2 = self.abf("arA")[:, 16384:24576].rearrange("p (hl i c) -> p hl i c", hl=2, i=8)
        guv = self.af32("arB").rearrange("p (sb c) -> p sb c", sb=8)
        scrhb = t["scrh"].ap().bitcast(BF16)
        for el in range(EPC):
            for sup in range(2):
                eid = (el, sup)
                for sb in range(8):
                    col = el * 16 + sup * 8 + sb
                    self.dma("gp", (lambda g, col=col, sb=sb: g.indirect_dma_start(
                        out=t["gwt"][:, 2 * sb:2 * sb + 2], out_offset=None,
                        in_=dr["glw"][:, :],
                        in_offset=IndirectOffsetOnAxis(t["exix"][:, col:col + 1], axis=0))),
                        reads=[("dram:glw",), ("exix",)],
                        writes=[("gwt", sb)], tag=f"gl{el}{sup}{sb}")
                gw2 = t["gwt"].ap().rearrange("p (s b) -> p s b", s=8)
                self.c("ve", lambda v, gw2=gw2: v.tensor_copy(
                    out=t["gidx"].ap().rearrange("p (s o) -> p s o", o=1), in_=gw2[:, :, 0:1]),
                    reads=[("gwt", sb_) for sb_ in range(8)], writes=[("gidx",)],
                    tag=f"gi{el}{sup}")
                self.c("ve", lambda v, gw2=gw2: v.tensor_copy(
                    out=t["wloc"].ap().rearrange("p (s o) -> p s o", o=1), in_=gw2[:, :, 1:2]),
                    reads=[("gwt", sb_) for sb_ in range(8)], writes=[("wloc",)],
                    tag=f"gww{el}{sup}")
                exb = self.claim("arB") if True else []
                for sb in range(8):
                    self.dma("gp", (lambda g, sb=sb, bufv=bufv: g.indirect_dma_start(
                        out=bufv[:, sb, :], out_offset=None, in_=dr["xrtab"][:, :],
                        in_offset=IndirectOffsetOnAxis(t["gidx"][:, sb:sb + 1], axis=0))),
                        reads=[("dram:xrtab",), ("gidx",)],
                        writes=[("arB", "buf", eid, sb)] + (exb if sb == 0 else []),
                        tag=f"gx{el}{sup}{sb}")
                exa = self.claim("arA")
                for sb in range(8):
                    for hl in range(2):
                        for hc in range(16):
                            ps = "ps_t0" if (hc % 2 == 0) else "ps_t1"
                            src = bufv[:, sb, hl * 2048 + hc * 128:hl * 2048 + (hc + 1) * 128]
                            self.mm0((lambda te, src=src, ps=ps: te.transpose(
                                self.t[ps][:], src, t["ident"][:])),
                                reads=[("arB", "buf", eid, sb), ("ident",)],
                                writes=[(ps,)], tag="bT")
                            self.c("ve", (lambda v, sb=sb, hl=hl, hc=hc, ps=ps, bufT=bufT:
                                          v.tensor_copy(
                                              out=bufT[:, hl, hc, 128 * sb:128 * (sb + 1)],
                                              in_=self.t[ps][:])),
                                reads=[(ps,)],
                                writes=[("arA", "bufT", eid, hl, hc, sb)] +
                                       (exa if (sb == 0 and hl == 0 and hc == 0) else []),
                                tag="bTe")
                btk = [("arA", "bufT", eid, hl_, hc_, sb_) for hl_ in range(2)
                       for hc_ in range(16) for sb_ in range(8)]
                exg = self.claim("arB", exclude=("buf",))
                for fcol in range(4):
                    for qh in range(4):
                        self.dma("sy", (lambda sy, el=el, fcol=fcol, qh=qh, mv32=mv32:
                                        sy.dma_start(
                                            out=mv32[:, 0:2048].rearrange(
                                                "p (j f) -> p j f", j=4),
                                            in_=self.w13_in[el, 512 * qh:512 * (qh + 1),
                                                            512 * fcol:512 * (fcol + 1)]
                                            .rearrange("(j p) f -> p j f", p=128))),
                                 reads=[("gidx",)],
                                 writes=[("mvt",), ("mvt", "sf"), ("mvt", "z"), ("mvt", "dn")],
                                 tag=f"w13{el}{sup}{fcol}{qh}")
                        hi = sl13[:, 0, 4 * qh:4 * qh + 4, :].rearrange("p j c -> p (j c)")
                        lo = sl13[:, 1, 4 * qh:4 * qh + 4, :].rearrange("p j c -> p (j c)")
                        self.c("ve", (lambda v, hi=hi, mv32=mv32: v.tensor_copy(
                            out=hi, in_=mv32[:, 0:2048])),
                            reads=[("mvt",)], writes=[("arA", "s13", 0, qh)], tag="w13h")
                        self.c("ve", (lambda v, hi=hi, lo=lo, mv32=mv32: v.tensor_tensor(
                            lo, mv32[:, 0:2048], hi, ALU.subtract)),
                            reads=[("mvt",), ("arA", "s13", 0, qh)],
                            writes=[("arA", "s13", 1, qh)], tag="w13l")
                    for sb in range(8):
                        psn = ["ps_a", "ps_b", "ps_c", "ps_d"][sb % 4]
                        first = True
                        for hc in range(16):
                            slk = [("arA", "s13", 0, hc // 4), ("arA", "s13", 1, hc // 4)]
                            for hb, hw in ((0, 0), (0, 1), (1, 0)):
                                rds = [("arA", "bufT", eid, hb, hc, sb)] + slk
                                fn = (lambda te, sb=sb, hc=hc, hb=hb, hw=hw, psn=psn,
                                      bufT=bufT, sl13=sl13, first=first:
                                      te.matmul(self.t[psn][:],
                                                bufT[:, hb, hc, 128 * sb:128 * (sb + 1)],
                                                sl13[:, hw, hc, :],
                                                start=first, stop=(hc == 15 and hb == 1)))
                                if first:
                                    self.mm0(fn, reads=rds, writes=[(psn,)],
                                             tag=f"g13{el}{sup}{fcol}{sb}")
                                    first = False
                                else:
                                    self.mmn(fn, reads=rds, keys=[(psn,)])
                        self.c("ve", (lambda v, sb=sb, fcol=fcol, psn=psn, guv=guv:
                                      v.tensor_copy(
                                          out=guv[:, sb, 512 * fcol:512 * (fcol + 1)],
                                          in_=self.t[psn][:])),
                            reads=[(psn,)],
                            writes=[("arB", "gu", eid, sb, fcol)] +
                                   (exg if (fcol == 0 and sb == 0) else []),
                            tag=f"gue{el}{sup}{fcol}{sb}")
                for sb in range(8):
                    for half in range(2):
                        self.c("sc", (lambda sc, sb=sb, half=half, guv=guv: sc.activation(
                            t["scr0"][:, 512 * half:512 * (half + 1)],
                            guv[:, sb, 512 * half:512 * (half + 1)], AF.Silu)),
                            reads=[("arB", "gu", eid, sb, half)], writes=[("scr0",)],
                            tag=f"si{el}{sup}{sb}{half}")
                        self.c("ve", (lambda v, sb=sb, half=half, guv=guv: v.tensor_tensor(
                            guv[:, sb, 512 * half:512 * (half + 1)],
                            t["scr0"][:, 512 * half:512 * (half + 1)],
                            guv[:, sb, 1024 + 512 * half:1024 + 512 * (half + 1)], ALU.mult)),
                            reads=[("scr0",), ("arB", "gu", eid, sb, half),
                                   ("arB", "gu", eid, sb, 2 + half)],
                            writes=[("arB", "act", eid, sb, half)], tag=f"am{el}{sup}{sb}{half}")
                exa2 = self.claim("arA", exclude=("s13",))
                for sb in range(8):
                    ak = [("arB", "act", eid, sb, 0), ("arB", "act", eid, sb, 1)]
                    self.c("sc", (lambda sc, sb=sb, guv=guv: sc.activation(
                        t["scrb"][:, 0:1024], guv[:, sb, 0:1024], AF.Identity)),
                        reads=ak, writes=[("scrb",)], tag=f"ash{el}{sup}{sb}")
                    self.c("ve", (lambda v, sb=sb, guv=guv, scrhb=scrhb: v.tensor_tensor(
                        scrhb[:, 0:1024], guv[:, sb, 0:1024], t["scrb"][:, 0:1024],
                        ALU.subtract)),
                        reads=ak + [("scrb",)], writes=[("scrh",)], tag=f"asl{el}{sup}{sb}")
                    for hl, srck in ((0, "scrb"), (1, "scrh")):
                        src = t["scrb"].ap() if hl == 0 else scrhb
                        for icn in range(8):
                            ps = "ps_t0" if icn % 2 == 0 else "ps_t1"
                            self.mm0((lambda te, src=src, icn=icn, ps=ps: te.transpose(
                                self.t[ps][:], src[:, 128 * icn:128 * (icn + 1)],
                                t["ident"][:])),
                                reads=[(srck,), ("ident",)], writes=[(ps,)], tag="aT")
                            self.c("ve", (lambda v, sb=sb, hl=hl, icn=icn, ps=ps, actT=actT:
                                          v.tensor_copy(
                                              out=actT[:, hl, icn, 128 * sb:128 * (sb + 1)],
                                              in_=self.t[ps][:])),
                                reads=[(ps,)],
                                writes=[("arA", "actT", eid, hl, icn, sb)] +
                                       (exa2 if (sb == 0 and hl == 0 and icn == 0) else []),
                                tag="aTe")
                atk = [("arA", "actT", eid, hl_, ic_, sb_) for hl_ in range(2)
                       for ic_ in range(8) for sb_ in range(8)]
                exe = self.claim("arB", exclude=("buf",))
                eov = self.af32("arB").rearrange("p (sb c) -> p sb c", sb=8)
                for hcol in range(4):
                    for qh in range(4):
                        self.dma("sy", (lambda sy, el=el, hcol=hcol, qh=qh, mv32=mv32:
                                        sy.dma_start(
                                            out=mv32[:, 0:1024].rearrange(
                                                "p (i h) -> p i h", i=2),
                                            in_=self.w2_in[el, 256 * qh:256 * (qh + 1),
                                                           512 * hcol:512 * (hcol + 1)]
                                            .rearrange("(i p) h -> p i h", p=128))),
                            reads=[("gidx",)],
                            writes=[("mvt",), ("mvt", "sf"), ("mvt", "z"), ("mvt", "dn")],
                            tag=f"w2{el}{sup}{hcol}{qh}")
                        hi2 = sl2[:, 0, 2 * qh:2 * qh + 2, :].rearrange("p i c -> p (i c)")
                        lo2 = sl2[:, 1, 2 * qh:2 * qh + 2, :].rearrange("p i c -> p (i c)")
                        self.c("ve", (lambda v, hi2=hi2, mv32=mv32: v.tensor_copy(
                            out=hi2, in_=mv32[:, 0:1024])),
                            reads=[("mvt",)], writes=[("arA", "s2", 0, qh)], tag="w2h")
                        self.c("ve", (lambda v, hi2=hi2, lo2=lo2, mv32=mv32: v.tensor_tensor(
                            lo2, mv32[:, 0:1024], hi2, ALU.subtract)),
                            reads=[("mvt",), ("arA", "s2", 0, qh)],
                            writes=[("arA", "s2", 1, qh)], tag="w2l")
                    for sb in range(8):
                        psn = ["ps_a", "ps_b", "ps_c", "ps_d"][sb % 4]
                        first = True
                        for icn in range(8):
                            slk2 = [("arA", "s2", 0, icn // 2), ("arA", "s2", 1, icn // 2)]
                            for ha, hw in ((0, 0), (0, 1), (1, 0)):
                                rds = [("arA", "actT", eid, ha, icn, sb)] + slk2
                                fn = (lambda te, sb=sb, icn=icn, ha=ha, hw=hw, psn=psn,
                                      actT=actT, sl2=sl2, first=first:
                                      te.matmul(self.t[psn][:],
                                                actT[:, ha, icn, 128 * sb:128 * (sb + 1)],
                                                sl2[:, hw, icn, :],
                                                start=first, stop=(icn == 7 and ha == 1)))
                                if first:
                                    self.mm0(fn, reads=rds, writes=[(psn,)],
                                             tag=f"e2{el}{sup}{hcol}{sb}")
                                    first = False
                                else:
                                    self.mmn(fn, reads=rds, keys=[(psn,)])
                        self.c("ve", (lambda v, sb=sb, hcol=hcol, psn=psn, eov=eov:
                                      v.tensor_scalar(
                                          eov[:, sb, 512 * hcol:512 * (hcol + 1)],
                                          self.t[psn][:], t["wloc"][:, sb:sb + 1], None,
                                          ALU.mult)),
                            reads=[(psn,), ("wloc",)],
                            writes=[("arB", "eo", eid, sb, hcol)] +
                                   (exe if (hcol == 0 and sb == 0) else []),
                            tag=f"eoe{el}{sup}{hcol}{sb}")
                for sb in range(8):
                    k = ("dram:rs_in", eid, sb)
                    self.dma("gp", (lambda g, sb=sb, eov=eov: g.indirect_dma_start(
                        out=dr["rs_in"][:, :],
                        out_offset=IndirectOffsetOnAxis(t["gidx"][:, sb:sb + 1], axis=0),
                        in_=eov[:, sb, :], in_offset=None, compute_op=ALU.add)),
                        reads=[("gidx",)] + [("arB", "eo", eid, sb, hc_) for hc_ in range(4)] +
                              [("dram:rs_in", "z", r_) for r_ in range(65)] +
                              ([self._last_rs] if self._last_rs else []),
                        writes=[k], tag=f"rsc{el}{sup}{sb}")
                    self._last_rs = k

    def ph_return(self):
        t = self.t
        dr = self.dr
        mv32 = self.vw_mvt32()
        scrhb = t["scrh"].ap().bitcast(BF16)
        self.coll(lambda g: g.collective_compute(
            "ReduceScatter", ALU.add, replica_groups=[list(range(NCORE))],
            ins=[dr["rs_in"][:]], outs=[dr["rs_out"][:]]),
            reads=[("dram:rs_in", (el_, sup_), sb_) for el_ in range(EPC)
                   for sup_ in range(2) for sb_ in range(8)] +
                  [("dram:rs_in", "z", r_) for r_ in range(65)],
            writes=[("dram:rs_out",)], tag="rs")
        actT = self.abf("arB")[:, 0:16384].rearrange("p (hl ic s) -> p hl ic s", hl=2, ic=8)
        sld = self.abf("arB")[:, 16384:24576].rearrange("p (hl i c) -> p hl i c", hl=2, i=8)
        extrab = self.claim("arB")
        for b in range(NB):
            self.dma("sy", (lambda sy, b=b, mv32=mv32: sy.dma_start(
                out=mv32[:, 0:1024], in_=dr["act_sp"][128 * b:128 * (b + 1), :])),
                reads=[("dram:act_sp", b, 0), ("dram:act_sp", b, 1)],
                writes=[("mvt",), ("mvt", "sf"), ("mvt", "z"), ("mvt", "dn")] +
                       (extrab if b == 0 else []),
                tag=f"lda{b}")
            self.c("sc", (lambda sc, mv32=mv32: sc.activation(
                t["scrb"][:, 0:1024], mv32[:, 0:1024], AF.Identity)),
                reads=[("mvt",)], writes=[("scrb",)], tag=f"dsh{b}")
            self.c("ve", (lambda v, mv32=mv32, scrhb=scrhb: v.tensor_tensor(
                scrhb[:, 0:1024], mv32[:, 0:1024], t["scrb"][:, 0:1024], ALU.subtract)),
                reads=[("mvt",), ("scrb",)], writes=[("scrh",)], tag=f"dsl{b}")
            for hl, srck in ((0, "scrb"), (1, "scrh")):
                src = t["scrb"].ap() if hl == 0 else scrhb
                for icn in range(8):
                    ps = "ps_t0" if icn % 2 == 0 else "ps_t1"
                    self.mm0((lambda te, src=src, icn=icn, ps=ps: te.transpose(
                        self.t[ps][:], src[:, 128 * icn:128 * (icn + 1)], t["ident"][:])),
                        reads=[(srck,), ("ident",)], writes=[(ps,)], tag="dT")
                    self.c("ve", (lambda v, b=b, hl=hl, icn=icn, ps=ps, actT=actT:
                                  v.tensor_copy(
                                      out=actT[:, hl, icn, 128 * b:128 * (b + 1)],
                                      in_=self.t[ps][:])),
                        reads=[(ps,)], writes=[("arB", "dactT", hl, icn, b)], tag="dTe")
        atk = [("arB", "dactT", hl_, ic_, b_) for hl_ in range(2) for ic_ in range(8)
               for b_ in range(NB)]
        for hcol in range(4):
            self.dma("sy", (lambda sy, hcol=hcol, mv32=mv32: sy.dma_start(
                out=mv32[:, 0:4096].rearrange("p (i h) -> p i h", i=8),
                in_=self.shd_in[:, 512 * hcol:512 * (hcol + 1)]
                .rearrange("(i p) h -> p i h", p=128))),
                reads=atk[:1],
                writes=[("mvt",), ("mvt", "sf"), ("mvt", "z"), ("mvt", "dn")],
                tag=f"wd{hcol}")
            hi2 = sld[:, 0, :, :].rearrange("p i c -> p (i c)")
            lo2 = sld[:, 1, :, :].rearrange("p i c -> p (i c)")
            self.c("ve", (lambda v, hi2=hi2, mv32=mv32: v.tensor_copy(
                out=hi2, in_=mv32[:, 0:4096])),
                reads=[("mvt",)], writes=[("arB", "sd", 0)], tag="wdh")
            self.c("ve", (lambda v, hi2=hi2, lo2=lo2, mv32=mv32: v.tensor_tensor(
                lo2, mv32[:, 0:4096], hi2, ALU.subtract)),
                reads=[("mvt",), ("arB", "sd", 0)], writes=[("arB", "sd", 1)], tag="wdl")
            for b in range(NB):
                psn = ["ps_a", "ps_b", "ps_c", "ps_d"][b % 4]
                first = True
                for icn in range(8):
                    for ha, hw in ((0, 0), (0, 1), (1, 0)):
                        rds = [("arB", "dactT", ha, icn, b), ("arB", "sd", 0),
                               ("arB", "sd", 1)]
                        fn = (lambda te, b=b, icn=icn, ha=ha, hw=hw, psn=psn,
                              actT=actT, sld=sld, first=first:
                              te.matmul(self.t[psn][:],
                                        actT[:, ha, icn, 128 * b:128 * (b + 1)],
                                        sld[:, hw, icn, :],
                                        start=first, stop=(icn == 7 and ha == 1)))
                        if first:
                            self.mm0(fn, reads=rds, writes=[(psn,)], tag=f"dn{hcol}{b}")
                            first = False
                        else:
                            self.mmn(fn, reads=rds, keys=[(psn,)])
                self.c("ve", (lambda v, psn=psn: v.tensor_copy(
                    out=t["scr0"][:, 0:512], in_=self.t[psn][:])),
                    reads=[(psn,)], writes=[("scr0",)], tag=f"dne{hcol}{b}")
                self.dma("sy", (lambda sy, b=b, hcol=hcol: sy.dma_start(
                    out=dr["shared_sp"][128 * b:128 * (b + 1), 512 * hcol:512 * (hcol + 1)],
                    in_=t["scr0"][:, 0:512])),
                    reads=[("scr0",)], writes=[("dram:shared_sp", b, hcol)],
                    tag=f"dns{hcol}{b}")
        for b in range(NB):
            self.dma("sy", (lambda sy, b=b, mv32=mv32: sy.dma_start(
                out=mv32[:, 0:2048], in_=dr["oaspill"][128 * b:128 * (b + 1), :])),
                reads=[("dram:oaspill", b)],
                writes=[("mvt",), ("mvt", "sf"), ("mvt", "z"), ("mvt", "dn")],
                tag=f"f0{b}")
            self.dma("sy", (lambda sy, b=b: sy.dma_start(
                out=t["scr0"][:, :], in_=dr["shared_sp"][128 * b:128 * (b + 1), :])),
                reads=[("dram:shared_sp", b, hc_) for hc_ in range(4)],
                writes=[("scr0",)], tag=f"f1{b}")
            self.dma("sy", (lambda sy, b=b: sy.dma_start(
                out=t["scr1"][:, :], in_=dr["rs_out"][128 * b:128 * (b + 1), :])),
                reads=[("dram:rs_out",)], writes=[("scr1",)], tag=f"f2{b}")
            self.c("ve", (lambda v, mv32=mv32: v.tensor_tensor(
                t["scr0"][:, :], t["scr0"][:, :], mv32[:, 0:2048], ALU.add)),
                reads=[("scr0",), ("mvt",)], writes=[("scr0",)], tag=f"fa{b}")
            self.c("ve", lambda v: v.tensor_tensor(
                t["scr0"][:, :], t["scr0"][:, :], t["scr1"][:, :], ALU.add),
                reads=[("scr0",), ("scr1",)], writes=[("scr0",)], tag=f"fb{b}")
            self.dma("sy", (lambda sy, b=b: sy.dma_start(
                out=self.out[128 * b:128 * (b + 1), :], in_=t["scr0"][:, :])),
                reads=[("scr0",)], writes=[("dram:out", b)], tag=f"fout{b}")

    def ph_out_partial(self):
        dr = self.dr
        self.dma("sy", lambda sy: sy.dma_start(out=self.out[:, :], in_=dr["xspill"][:, :]),
                 reads=[("dram:xspill", b_) for b_ in range(NB)],
                 writes=[("dram:out",)], tag="st_out")


def make_layout_inputs(core):
    rank = core % 4
    blks = blocks_of(core)
    attidx = np.zeros((128, 288), np.int32)
    for pr in range(4):
        h = 4 * rank + pr
        for r in range(4):
            for hl in range(2):
                attidx[:, pr * 8 + r * 2 + hl] = (r * 4096 + hl * 2048 + h * 128 +
                                                  np.arange(128))
                for bi in range(8):
                    attidx[:, 32 + pr * 64 + r * 16 + hl * 8 + bi] = (
                        r * 32768 + hl * 16384 + h * 1024 + bi * 128 + np.arange(128))
    avidx = np.zeros((128, 32), np.int32)
    for r in range(4):
        for b in range(NB):
            avidx[:, r * 8 + b] = r * 4096 + blks[b] * 128 + np.arange(128)
    exidx = np.zeros((128, 64), np.int32)
    for el in range(EPC):
        for sup in range(2):
            for sb in range(8):
                exidx[:, el * 16 + sup * 8 + sb] = ((4 * core + el) * CAP + sup * 1024 +
                                                    sb * 128 + np.arange(128))
    esel = np.zeros((128, 64), np.float32)
    for i in range(32):
        f = 32 * core + i
        esel[f % 128, (f // 128) * 32 + i] = 1.0
    hsel = np.zeros((16, 4), np.float32)
    for pr in range(4):
        hsel[4 * rank + pr, pr] = 1.0
    rowbase = np.zeros((128, 1), np.float32)
    for p in range(128):
        s_ = p // 64
        bglob = (p % 64) // 2
        half = p % 2
        oc = 4 * s_ + BLK_OWNER[bglob]
        l = BLK_IDX[bglob] * 128 + half * 64
        rowbase[p, 0] = oc * ROWS + l
    return {"attidx": attidx, "avidx": avidx, "exidx": exidx, "hsel": hsel,
            "esel": esel, "rowbase": rowbase}


def shard_inputs(inputs):
    x = np.asarray(inputs["L0_attn_x"])
    ws = {m: np.asarray(inputs[f"L0_attn_w_{m}"]) for m in "qkvo"}
    gw = np.asarray(inputs["L0_ffn_gate_w"])
    nw = np.asarray(inputs["L0_ffn_norm_w"]).reshape(1, D)
    w13 = np.asarray(inputs["L0_ffn_w13"])
    w2 = np.asarray(inputs["L0_ffn_w2"])
    shg = np.asarray(inputs["L0_ffn_shared_gate_w"])
    shu = np.asarray(inputs["L0_ffn_shared_up_w"])
    shd = np.asarray(inputs["L0_ffn_shared_down_w"])
    maps = []
    for c in range(NCORE):
        seq = c // 4
        rows = np.concatenate([x[seq, 128 * b:128 * (b + 1), :] for b in blocks_of(c)], 0)
        maps.append({
            "x_in": np.ascontiguousarray(rows),
            **{f"w_{m}": np.ascontiguousarray(ws[m][:, 256 * c:256 * (c + 1)]) for m in "qkvo"},
            "gate_w": gw, "norm_w": nw,
            "w13": np.ascontiguousarray(w13[EPC * c:EPC * (c + 1)]),
            "w2": np.ascontiguousarray(w2[EPC * c:EPC * (c + 1)]),
            "sh_g": shg, "sh_u": shu, "sh_d": shd,
            **make_layout_inputs(c),
        })
    return maps


def gather_output(results):
    out = np.zeros((2, S, D), np.float32)
    for c in range(NCORE):
        seq = c // 4
        r = np.asarray(results[c]["out"])
        for i, b in enumerate(blocks_of(c)):
            out[seq, 128 * b:128 * (b + 1), :] = r[128 * i:128 * (i + 1), :]
    return out


def kernel(**inputs):
    import os
    os.environ.setdefault("NEURON_SCRATCHPAD_PAGE_SIZE", "512")
    mk = MK(upto=UPTO)
    nc = mk.emit()
    maps = shard_inputs(inputs)
    res = run_bass_kernel_spmd(nc, maps, core_ids=list(range(NCORE)))
    return gather_output(res.results)
